# version 09
# baseline (speedup 1.0000x reference)
import numpy as np
import ml_dtypes
import concourse.bass as bass
import concourse.mybir as mybir
from concourse.bass_utils import run_bass_kernel_spmd

F32 = mybir.dt.float32
BF16 = mybir.dt.bfloat16
AF = mybir.ActivationFunctionType
ALU = mybir.AluOpType
AX = mybir.AxisListType

TOK = 2048
D = 2048
DT = 16
TB = 512
NTB = 4
H = 16
L = 64
HL = 1024
E = 8192
ET = 64
LN_EPS = 1e-5
Q_EPS = 1e-8
ISQ = float(1.0 / np.sqrt(128.0))
NW = float(E * D)

TEN, VEC, SCA, GPS, SYN = "tensor", "vector", "scalar", "gpsimd", "sync"


class Sem:
    __slots__ = ("h", "n")

    def __init__(self, nc, name):
        self.h = nc.alloc_semaphore(name)
        self.n = 0


def build_program(debug=False):
    nc = bass.Bass("TRN2", target_bir_lowering=False)

    x_in = nc.dram_tensor("x", [TOK, D], F32, kind="ExternalInput").ap()
    wq = nc.dram_tensor("wq", [D, D], F32, kind="ExternalInput").ap()
    bq = nc.dram_tensor("bq", [D], F32, kind="ExternalInput").ap()
    wv = nc.dram_tensor("wv", [D, D], F32, kind="ExternalInput").ap()
    bv = nc.dram_tensor("bv", [D], F32, kind="ExternalInput").ap()
    wo = nc.dram_tensor("wo", [D, D], F32, kind="ExternalInput").ap()
    bo = nc.dram_tensor("bo", [D], F32, kind="ExternalInput").ap()
    lat = nc.dram_tensor("lat", [L, D], F32, kind="ExternalInput").ap()
    wup = nc.dram_tensor("wup", [E, D], F32, kind="ExternalInput").ap()
    wgt = nc.dram_tensor("wgt", [E, D], F32, kind="ExternalInput").ap()
    wdn = nc.dram_tensor("wdn", [D, E], F32, kind="ExternalInput").ap()
    out = nc.dram_tensor("out", [TOK, D], F32, kind="ExternalOutput").ap()

    x_bf = nc.dram_tensor("x_bf", [TOK, D], BF16, kind="Internal").ap()
    wv_bf = nc.dram_tensor("wv_bf", [D, D], BF16, kind="Internal").ap()
    wo_bf = nc.dram_tensor("wo_bf", [D, D], BF16, kind="Internal").ap()
    g_mat = nc.dram_tensor("g_mat", [HL, D], BF16, kind="Internal").ap()
    mm_mat = nc.dram_tensor("mm_mat", [D, HL], BF16, kind="Internal").ap()
    mm_lo = nc.dram_tensor("mm_lo", [D, HL], BF16, kind="Internal").ap()
    up_q = nc.dram_tensor("up_q", [E, D], BF16, kind="Internal").ap()
    gt_q = nc.dram_tensor("gt_q", [E, D], BF16, kind="Internal").ap()
    dn_q = nc.dram_tensor("dn_q", [D, E], BF16, kind="Internal").ap()
    up_qT = nc.dram_tensor("up_qT", [D, E], BF16, kind="Internal").ap()
    gt_qT = nc.dram_tensor("gt_qT", [D, E], BF16, kind="Internal").ap()
    dn_qT = nc.dram_tensor("dn_qT", [E, D], BF16, kind="Internal").ap()
    s_scr = [nc.dram_tensor(f"s_scr{i}", [n], F32, kind="Internal").ap()
             for i, n in ((0, E), (1, E), (2, D))]
    ident_dram = nc.inline_tensor(np.eye(128, dtype=ml_dtypes.bfloat16), name="ident_c")

    def S(name, shape, dtype):
        return nc.alloc_sbuf_tensor(name, shape, dtype)

    xT = S("xT", [128, DT, TOK], BF16)
    h1T = S("h1T", [128, DT, TB], BF16)
    sqr = S("sqr", [128, 2, TB], BF16)
    rowf = S("rowf", [1, 4, TB], F32)
    rowb = S("rowb", [1, 2, TB], BF16)
    rbc = S("rbc", [128, TB], BF16)
    abc = S("abc", [128, TB], BF16)
    p1r = S("p1r", [128, 2, 1032], F32)
    ctile = S("ctile", [128, 1024], BF16)
    qout = S("qout", [128, 2, 1024], BF16)
    tstg = S("tstg", [128, 1024], BF16)
    big = S("big", [128, 3, 1024], BF16)
    ident = S("ident", [128, 128], BF16)
    latT = S("latT", [128, H, L], BF16)
    bqT = S("bqT", [128, 16], BF16)
    bq_sb = S("bq_sb", [16, 128], BF16)
    blog = S("blog", [1, HL], BF16)
    bv_sb = S("bv_sb", [1, D], BF16)
    bo_sb = S("bo_sb", [1, D], BF16)
    ones1 = S("ones1", [1, 128], BF16)
    ones1f = S("ones1f", [1, 128], F32)
    onescf = S("onescf", [128, 1], F32)
    onescb = S("onescb", [128, 1], BF16)
    onesN = S("onesN", [1, TB], BF16)
    rowacc = [S("rowacc_up", [128, 64], F32), S("rowacc_gt", [128, 64], F32),
              S("rowacc_dn", [128, 16], F32)]
    line = S("line", [1, 2, 264], F32)
    thc = S("thc", [128, 8], F32)
    nthc = S("nthc", [128, 8], F32)
    sc1 = S("sc1", [1, 16], F32)
    cpool = S("cpool", [1, 16], F32)
    mx8 = S("mx8", [128, H, 8], F32)
    m1r = S("m1r", [128, H], F32)
    taur = S("taur", [128, H], F32)
    ssum = S("ssum", [128, H], F32)
    rsum = S("rsum", [128, H], F32)

    AR_B = 72 * 1024
    arena = nc.alloc_sbuf_tensor("arena", [128, AR_B], mybir.dt.uint8)
    ar0 = nc.lookup_mloc(arena).addr

    def A(name, shape, dtype, off):
        return nc.alloc_sbuf_tensor_at(name, shape, dtype, offset=ar0 + off)

    hid = A("hid", [128, ET, TB], BF16, 0)
    silu_sb = A("silu_sb", [128, 4, TB], BF16, 64 * 1024)
    logit = A("logit", [128, 2, HL], F32, 0)
    ework = A("ework", [128, HL], F32, 8 * 1024)
    maskb = A("maskb", [128, HL], BF16, 12 * 1024)
    p_sb = A("p_sb", [128, 2, HL], BF16, 14 * 1024)
    pT_sb = A("pT_sb", [128, 8, TB], BF16, 18 * 1024)
    attnT = A("attnT", [128, DT, TB], BF16, 26 * 1024)
    stg = A("stg", [128, 2, D], BF16, 42 * 1024)
    gstg = A("gstg", [64, 2, D], BF16, 50 * 1024)
    mstg = A("mstg", [128, 2, 1024], BF16, 58 * 1024)
    mlo_st = A("mlo_st", [128, 1024], BF16, 62 * 1024)
    pf32 = A("pf32", [128, 1024], F32, 50 * 1024)
    lat_sb = A("lat_sb", [64, D], BF16, 26 * 1024)
    mmr = A("mmr", [128, 2, HL], BF16, 58 * 1024)
    sml = A("sml", [128, 4, TB], BF16, 68 * 1024)
    ostg = A("ostg", [128, D], F32, 50 * 1024)

    taps = {}
    if debug:
        for nm, shape, dtp in (("tap_h1", [128, DT, TB], BF16), ("tap_lg", [128, 2, HL], F32),
                               ("tap_p", [128, 2, HL], BF16), ("tap_at", [128, DT, TB], BF16),
                               ("tap_o", [128, DT, TB], BF16), ("tap_x1", [128, DT, TB], BF16),
                               ("tap_hid", [128, ET, TB], BF16), ("tap_f", [128, DT, TB], BF16)):
            taps[nm] = nc.dram_tensor(nm, shape, dtp, kind="Internal").ap()

    psX = nc.alloc_psum_tensor("psX", [128, 4, 512], F32)
    psY = nc.alloc_psum_tensor("psY", [128, 4, 512], F32)
    Xb = psX.ap()
    Yb = psY.ap()
    Xflat = Xb.rearrange("p a b -> p (a b)")
    Yflat = Yb.rearrange("p a b -> p (a b)")
    X2 = Xflat.rearrange("p (a b) -> p a b", a=2)
    X8 = Xb.rearrange("p a (b c) -> p (a b) c", b=2)
    Y8 = Yb.rearrange("p a (b c) -> p (a b) c", b=2)
    Xbf = Xb.bitcast(BF16)
    Ybf = Yb.bitcast(BF16)

    sched = {k: [] for k in (TEN, VEC, SCA, GPS, SYN)}
    sems = {}

    def sem(name):
        if name not in sems:
            sems[name] = Sem(nc, name)
        return sems[name]

    def em(eng, fn):
        sched[eng].append(fn)

    def wait(eng, s, n):
        if n > 0:
            em(eng, lambda e, h=s.h, v=n: e.wait_ge(h, v))

    def dma(eng, out_ap, in_ap, s, transpose=False):
        s.n += 16
        if transpose:
            em(eng, lambda e, o=out_ap, i=in_ap, h=s.h: e.dma_start(o, i, transpose=True).then_inc(h, 16))
        else:
            em(eng, lambda e, o=out_ap, i=in_ap, h=s.h: e.dma_start(o, i).then_inc(h, 16))
        return s.n

    def op(eng, fn):
        em(eng, fn)

    def opi(eng, s, fn):
        s.n += 1
        em(eng, lambda e, f=fn, h=s.h: f(e).then_inc(h, 1))
        return s.n

    def vd(eng):
        em(eng, lambda e: e.drain())

    def nopi(eng, s):
        s.n += 1
        em(eng, lambda e, h=s.h: e.nop().then_inc(h, 1))
        return s.n

    g_ld = sem("g_ld")
    g_st = sem("g_st")
    s_ld = sem("s_ld")
    q_st = sem("q_st")
    o_st = sem("o_st")
    v_ = sem("v_")
    p_ = sem("p_")
    c_ = sem("c_")
    v_p1 = sem("v_p1")
    v_q = sem("v_q")
    th_dm = sem("th_dm")
    th_ln = [sem("th_ln0"), sem("th_ln1")]
    v_sq = sem("v_sq")
    v_ev = sem("v_ev")
    v_p = sem("v_p")
    fbig = sem("fbig")
    fsml = sem("fsml")
    fmm = sem("fmm")
    fq2 = sem("fq2")
    c_si = sem("c_si")
    v_hid = sem("v_hid")
    g_c = sem("g_c")

    class RingLd:

        def __init__(self, name, nslots):
            self.sems = [sem(f"{name}_s{i}") for i in range(nslots)]
            self.nslots = nslots
            self.k = 0

        def load(self, eng, out_ap, in_ap, transpose=False):
            slot = self.k % self.nslots
            s = self.sems[slot]
            self.k += 1
            v = dma(eng, out_ap, in_ap, s, transpose=transpose)
            return slot, s, v

    rbig = RingLd("rbig", 3)
    rsml = RingLd("rsml", 4)
    rmm = RingLd("rmm", 2)
    big_free = []
    sml_free = []
    mm_free = []
    rp1 = RingLd("rp1", 2)
    rstg = RingLd("rstg", 2)
    stgT = {name: [sem(f"stgT_{name}0"), sem(f"stgT_{name}1")]
            for name in ("x", "wv", "wo")}
    msT = [sem("msT0"), sem("msT1")]
    mloT = sem("mloT")
    gsT = [sem("gsT0"), sem("gsT1")]
    g_sm = sem("g_sm")
    qoS = [sem("qoS0"), sem("qoS1")]
    qtS = [sem("qtS0"), sem("qtS1")]
    qprogT = sem("qprogT")
    c_qt = sem("c_qt")
    qprog = sem("qprog")
    g_tp = sem("g_tp")

    CONSTS = [1.0 / D, LN_EPS, Q_EPS, 0.5, -0.5, 1.0 / NW, ISQ, -1.0, float(2.0 ** -30), -128.0]
    for i, cv in enumerate(CONSTS):
        op(GPS, lambda g, i=i, cv=cv: g.memset(cpool[0:1, i:i + 1], float(cv)))
        nc.const_aps.register(F32, float(cv), cpool[0:1, i:i + 1])
    op(GPS, lambda g: g.memset(ones1[:, :], 1.0))
    op(GPS, lambda g: g.memset(ones1f[:, :], 1.0))
    op(GPS, lambda g: g.memset(onescf[:, :], 1.0))
    op(GPS, lambda g: g.memset(onescb[:, :], 1.0))
    op(GPS, lambda g: g.memset(onesN[:, :], 1.0))
    opi(GPS, g_c, lambda g: g.memset(sc1[0:1, :], 0.0))
    for eng in (TEN, VEC, SCA, SYN):
        wait(eng, g_c, g_c.n)

    dma(GPS, ident[:, :], ident_dram.ap(), g_sm)
    slot_store = [None, None]
    for name, dst, srcd in (("x", x_bf, x_in), ("wv", wv_bf, wv), ("wo", wo_bf, wo)):
        for i in range(16):
            slotp = (rstg.k) % 2
            if slot_store[slotp] is not None:
                wait(GPS, slot_store[slotp][0], slot_store[slotp][1])
            slot, lsem, lv = rstg.load(GPS, stg[:, slotp, :], srcd[i * 128:(i + 1) * 128, :])
            wait(GPS, lsem, lv)
            sv = dma(GPS, dst[i * 128:(i + 1) * 128, :], stg[:, slotp, :], stgT[name][slotp])
            slot_store[slotp] = (stgT[name][slotp], sv)
    xbf_done = (stgT["x"][0].n, stgT["x"][1].n)
    wvbf_done = (stgT["wv"][0].n, stgT["wv"][1].n)
    wobf_done = (stgT["wo"][0].n, stgT["wo"][1].n)

    dma(GPS, lat_sb[:, :], lat, g_sm)
    dma(GPS, bq_sb[:, :], bq.rearrange("(a b) -> a b", a=16), g_sm)
    dma(GPS, bv_sb[:, :], bv.rearrange("(a b) -> a b", a=1), g_sm)
    small_done = dma(GPS, bo_sb[:, :], bo.rearrange("(a b) -> a b", a=1), g_sm)

    wait(SYN, stgT["x"][0], xbf_done[0])
    wait(SYN, stgT["x"][1], xbf_done[1])
    xT_ready = 0
    for dt in range(DT):
        xT_ready = dma(SYN, xT[:, dt, :], x_bf[0:TOK, dt * 128:(dt + 1) * 128], s_ld, transpose=True)

    wait(TEN, g_sm, small_done)
    for grp in range(4):
        if grp > 0:
            wait(TEN, v_ev, v_ev.n)
        for k in range(4):
            h = grp * 4 + k
            fn = (lambda t, h=h, k=k: t.transpose(
                Ybf[:, 0, k * 64:(k + 1) * 64], lat_sb[:, h * 128:(h + 1) * 128],
                ident[0:64, 0:64]))
            if k == 3:
                opi(TEN, p_, fn)
            else:
                op(TEN, fn)
        wait(VEC, p_, p_.n)
        opi(VEC, v_ev, lambda v, grp=grp: v.tensor_copy(
            latT.ap().rearrange("p a b -> p (a b)")[:, grp * 256:(grp + 1) * 256],
            Ybf[:, 0, 0:256]))
    wait(TEN, v_ev, v_ev.n)
    opi(TEN, p_, lambda t: t.transpose(Ybf[:, 0, 0:16], bq_sb[:, :], ident[0:16, 0:16]))
    wait(VEC, p_, p_.n)
    latT_v = opi(VEC, v_ev, lambda v: v.tensor_copy(bqT[:, :], Ybf[:, 0, 0:16]))

    mm_pe = [0] * H
    mm_ev = [0] * H
    mm_store = [0] * H
    mlo_store = 0
    for h in range(H):
        slotp = rstg.k % 2
        if slot_store[slotp] is not None:
            wait(GPS, slot_store[slotp][0], slot_store[slotp][1])
            slot_store[slotp] = None
        if h >= 2:
            wait(GPS, p_, mm_pe[h - 2])
        slot, lsem, wq_ld = rstg.load(GPS, stg[:, slotp, :], wq[h * 128:(h + 1) * 128, :])
        wait(TEN, lsem, wq_ld)
        if h == 0:
            wait(TEN, v_ev, latT_v)
        if h >= 2:
            wait(TEN, v_ev, mm_ev[h - 2])
        for dt in range(DT):
            op(TEN, lambda t, h=h, dt=dt, slot=slot: t.matmul(
                X2[:, h % 2, dt * 64:(dt + 1) * 64], stg[:, slot, dt * 128:(dt + 1) * 128],
                latT[:, h, :], start=True, stop=True))
        bank = 1 if h < 8 else 2
        col = (h % 8) * 64
        mm_pe[h] = opi(TEN, p_, lambda t, h=h, bank=bank, col=col: t.matmul(
            Yb[0:1, bank, col:col + 64], bqT[:, h:h + 1], latT[:, h, :], start=True, stop=True))
        wait(VEC, p_, mm_pe[h])
        if h >= 2:
            wait(VEC, msT[h % 2], mm_store[h - 2])
        if h >= 1:
            wait(VEC, mloT, mlo_store)
        op(VEC, lambda v, h=h: v.tensor_scalar_mul(pf32[:, :], X2[:, h % 2, :], ISQ))
        vd(VEC)
        op(VEC, lambda v, h=h: v.tensor_copy(mstg[:, h % 2, :], pf32[:, :]))
        vd(VEC)
        mm_ev[h] = opi(VEC, v_ev, lambda v, h=h: v.tensor_sub(
            mlo_st[:, :], pf32[:, :], mstg[:, h % 2, :]))
        wait(GPS, v_ev, mm_ev[h])
        mm_store[h] = dma(GPS, mm_mat.rearrange("(a p) q -> p a q", p=128)[:, :, h * 64:(h + 1) * 64],
                          mstg[:, h % 2, :].rearrange("p (a b) -> p a b", b=64), msT[h % 2])
        mlo_store = dma(GPS, mm_lo.rearrange("(a p) q -> p a q", p=128)[:, :, h * 64:(h + 1) * 64],
                        mlo_st[:, :].rearrange("p (a b) -> p a b", b=64), mloT)
    mmst_done = (msT[0].n, msT[1].n, mloT.n)
    wait(VEC, p_, p_.n)
    op(VEC, lambda v: v.tensor_scalar_mul(blog[0:1, 0:512], Yb[0:1, 1, :], ISQ))
    blog_v = opi(VEC, v_ev, lambda v: v.tensor_scalar_mul(blog[0:1, 512:1024], Yb[0:1, 2, :], ISQ))

    g_pe = [0] * H
    g_ev = [0] * H
    g_store = [0] * H
    for h in range(H):
        if h == 0:
            wait(SYN, stgT["wo"][0], wobf_done[0])
            wait(SYN, stgT["wo"][1], wobf_done[1])
            wait(TEN, v_ev, blog_v)
        par = h % 2
        gps_flat = Xflat if par == 0 else Yflat
        if h >= 2:
            wait(TEN, v_ev, g_ev[h - 2])
        for half in range(2):
            ridx = rbig.k
            if ridx >= 3:
                wait(SYN, p_, big_free[ridx - 3])
            _, wsem, wch = rbig.load(SYN, big[:, ridx % 3, :],
                                     wo_bf[half * 1024:(half + 1) * 1024,
                                           h * 128:(h + 1) * 128], transpose=True)
            wait(TEN, wsem, wch)
            for n in range(2):
                fn = (lambda t, h=h, n=n, half=half, gf=gps_flat, ridx=ridx: t.matmul(
                    gf[0:64, (half * 2 + n) * 512:(half * 2 + n + 1) * 512],
                    latT[:, h, :], big[:, ridx % 3, n * 512:(n + 1) * 512],
                    start=True, stop=True))
                if n == 1:
                    big_free.append(opi(TEN, p_, fn))
                else:
                    op(TEN, fn)
        g_pe[h] = big_free[-1]
        wait(VEC, p_, g_pe[h])
        if h >= 2:
            wait(VEC, gsT[h % 2], g_store[h - 2])
        g_ev[h] = opi(VEC, v_ev, lambda v, par=par, gf=gps_flat: v.tensor_copy(
            gstg[:, par, :], gf[0:64, :]))
        wait(SYN, v_ev, g_ev[h])
        g_store[h] = dma(SYN, g_mat[h * 64:(h + 1) * 64, :], gstg[:, par, :], gsT[h % 2])
    gst_done = (gsT[0].n, gsT[1].n)

    bg = []
    bg_pos = [0]

    def drain_bg(k):
        e = min(bg_pos[0] + k, len(bg))
        for i in range(bg_pos[0], e):
            em(VEC, bg[i])
        bg_pos[0] = e

    def tap(name, src_ap, wsem, wval):
        if not debug:
            return None
        ts_ = sem("tps_" + name)
        wait(SYN, wsem, wval)
        dma(SYN, taps[name], src_ap, ts_)
        return ts_

    def tap_gate(eng, tok):
        if debug and tok is not None:
            wait(eng, tok, 16)

    def mat_tile_ap(m, j):
        if m == 0:
            return wup[(j // 2) * 128:(j // 2 + 1) * 128, (j % 2) * 1024:(j % 2 + 1) * 1024]
        if m == 1:
            return wgt[(j // 2) * 128:(j // 2 + 1) * 128, (j % 2) * 1024:(j % 2 + 1) * 1024]
        cc, rr = j // 16, j % 16
        return wdn[rr * 128:(rr + 1) * 128, cc * 1024:(cc + 1) * 1024]

    def mat_tile_ap_p1(m, j):
        if m < 2:
            return mat_tile_ap(m, j)
        rr, cc = j // 8, j % 8
        return wdn[rr * 128:(rr + 1) * 128, cc * 1024:(cc + 1) * 1024]

    v_p1_fold = [0, 0, 0]
    p1_done = []
    for m in range(3):
        ncc = 2 if m < 2 else 8
        for j in range(128):
            idx = m * 128 + j
            if idx >= 2:
                wait(GPS, v_p1, p1_done[idx - 2])
            slot, lsem, ldn = rp1.load(GPS, p1r[:, idx % 2, 1:1025], mat_tile_ap_p1(m, j))
            v_p1.n += 1
            p1_done.append(v_p1.n)
            cc, rb = j % ncc, j // ncc
            nxt = (idx + 1) % 2

            def p1job(v, ldn=ldn, slot=idx % 2, nxt=nxt, cc=cc, rb=rb, m=m, ncc=ncc,
                      h1=v_p1.h, lh=lsem.h):
                v.wait_ge(lh, ldn)
                lo = 1 if cc == 0 else 0
                dst = rowacc[m][:, rb:rb + 1] if cc == ncc - 1 else p1r[:, nxt, 0:1]
                r = v.reduce_sum(dst, p1r[:, slot, lo:1025],
                                 axis=AX.X, apply_absolute_value=True).then_inc(h1, 1)
                v.drain()
                return r
            bg.append(p1job)
        v_p1_fold[m] = v_p1.n

    def emit_th(m):
        nblk = 64 if m < 2 else 16
        racc = rowacc[m]
        scr = s_scr[m]
        wait(SYN, v_p1, v_p1_fold[m])
        for rb in range(nblk):
            dma(SYN, scr[rb * 128:(rb + 1) * 128], racc[:, rb:rb + 1], th_dm)
        wait(SYN, th_dm, th_dm.n)
        nch = nblk * 128 // 256
        red_done = []
        for c in range(nch):
            slot = c % 2
            if c >= 2:
                wait(SYN, v_p1, red_done[c - 2])
            ldv = dma(SYN, line[0:1, slot, 1:257],
                      scr[c * 256:(c + 1) * 256].rearrange("(q n) -> q n", q=1), th_ln[slot])
            wait(VEC, th_ln[slot], ldv)
            lo = 1 if c == 0 else 0
            dst = sc1[0:1, 0:1] if c == nch - 1 else line[0:1, (c + 1) % 2, 0:1]
            red_done.append(opi(VEC, v_p1, lambda v, dst=dst, slot=slot, lo=lo:
                                v.reduce_sum(dst, line[0:1, slot, lo:257], axis=AX.X)))
            vd(VEC)
        op(VEC, lambda v: v.tensor_scalar(sc1[0:1, 4:5], sc1[0:1, 0:1],
                                          float(1.0 / NW), float(Q_EPS),
                                          op0=ALU.mult, op1=ALU.add))
        vd(VEC)
        op(VEC, lambda v: v.tensor_scalar_mul(sc1[0:1, 5:6], sc1[0:1, 4:5], 0.5))
        op(VEC, lambda v: v.tensor_scalar_mul(sc1[0:1, 6:7], sc1[0:1, 4:5], -0.5))
        vd(VEC)
        op(VEC, lambda v: v.tensor_scalar_mul(sc1[0:1, 8:9], sc1[0:1, 5:6], float(2.0 ** -30)))
        vd(VEC)
        op(VEC, lambda v: v.tensor_add(sc1[0:1, 9:10], sc1[0:1, 5:6], sc1[0:1, 8:9]))
        vd(VEC)
        op(VEC, lambda v: v.tensor_sub(sc1[0:1, 10:11], sc1[0:1, 9:10], sc1[0:1, 5:6]))
        vd(VEC)
        op(VEC, lambda v: v.tensor_sub(sc1[0:1, 10:11], sc1[0:1, 10:11], sc1[0:1, 8:9]))
        vd(VEC)
        op(VEC, lambda v: v.tensor_scalar_mul(sc1[0:1, 10:11], sc1[0:1, 10:11], -128.0))
        vd(VEC)
        op(VEC, lambda v: v.tensor_sub(sc1[0:1, 11:12], sc1[0:1, 5:6], sc1[0:1, 10:11]))
        vrow1 = opi(VEC, v_, lambda v: v.tensor_add(sc1[0:1, 12:13], sc1[0:1, 5:6], sc1[0:1, 10:11]))
        vd(VEC)
        op(VEC, lambda v: v.tensor_scalar_mul(sc1[0:1, 13:14], sc1[0:1, 11:12], -1.0))
        vrow = opi(VEC, v_, lambda v: v.tensor_scalar_mul(sc1[0:1, 14:15], sc1[0:1, 12:13], -1.0))
        wait(TEN, v_, vrow)
        op(TEN, lambda t: t.matmul(Yb[:, 3, 0:1], ones1f[0:1, :], sc1[0:1, 11:12],
                                   start=True, stop=True))
        op(TEN, lambda t: t.matmul(Yb[:, 3, 1:2], ones1f[0:1, :], sc1[0:1, 12:13],
                                   start=True, stop=True))
        op(TEN, lambda t: t.matmul(Yb[:, 3, 2:3], ones1f[0:1, :], sc1[0:1, 13:14],
                                   start=True, stop=True))
        if m == 0:
            op(TEN, lambda t: t.matmul(Yb[:, 3, 3:4], ones1f[0:1, :], sc1[0:1, 14:15],
                                       start=True, stop=True))
            pe = opi(TEN, p_, lambda t: t.matmul(Yb[:, 3, 4:5], ones1f[0:1, :], sc1[0:1, 6:7],
                                                 start=True, stop=True))
        else:
            pe = opi(TEN, p_, lambda t: t.matmul(Yb[:, 3, 3:4], ones1f[0:1, :], sc1[0:1, 14:15],
                                                 start=True, stop=True))
        wait(VEC, p_, pe)
        op(VEC, lambda v, m=m: v.tensor_copy(thc[:, m:m + 1], Yb[:, 3, 0:1]))
        op(VEC, lambda v, m=m: v.tensor_copy(thc[:, 3 + m:4 + m], Yb[:, 3, 1:2]))
        op(VEC, lambda v, m=m: v.tensor_copy(nthc[:, m:m + 1], Yb[:, 3, 2:3]))
        ev = opi(VEC, v_ev, lambda v, m=m: v.tensor_copy(nthc[:, 3 + m:4 + m], Yb[:, 3, 3:4]))
        if m == 0:
            ev = opi(VEC, v_ev, lambda v: v.tensor_copy(thc[:, 6:7], Yb[:, 3, 4:5]))
        wait(TEN, v_ev, ev)
        return ev

    q2_done = []
    nat_st = {}
    qt_st = {}

    def qt_dst(g, k):
        m, j = g // 128, g % 128
        if m < 2:
            dstm = up_qT if m == 0 else gt_qT
            return dstm[(j % 2) * 1024 + k * 128:(j % 2) * 1024 + (k + 1) * 128,
                        (j // 2) * 128:(j // 2 + 1) * 128]
        cc, rr = j // 16, j % 16
        return dn_qT[cc * 1024 + k * 128:cc * 1024 + (k + 1) * 128,
                     rr * 128:(rr + 1) * 128]

    def nat_tile(g):
        m, j = g // 128, g % 128
        if m < 2:
            dstm = up_q if m == 0 else gt_q
            return dstm[(j // 2) * 128:(j // 2 + 1) * 128, (j % 2) * 1024:(j % 2 + 1) * 1024]
        cc, rr = j // 16, j % 16
        return dn_q[rr * 128:(rr + 1) * 128, cc * 1024:(cc + 1) * 1024]

    def emit_pass2(m, th_ev):
        for j in range(128):
            idx = 384 + m * 128 + j
            slot = idx % 2
            glob = m * 128 + j
            if glob >= 2:
                wait(GPS, v_q, q2_done[glob - 2])
            else:
                wait(GPS, v_p1, p1_done[382 + glob])
            _, lsem, ldn = rp1.load(GPS, p1r[:, slot, 0:1024], mat_tile_ap(m, j))
            oslot = glob % 2
            guard = nat_st.get(glob - 2, 0)
            v_q.n += 1
            q2_done.append(v_q.n)

            def quant(v, m=m, slot=slot, oslot=oslot, ldn=ldn, th_ev=th_ev, guard=guard,
                      h1=v_q.h, lh=lsem.h):
                v.wait_ge(lh, ldn)
                v.wait_ge(v_ev.h, th_ev)
                if guard:
                    v.wait_ge(qoS[oslot].h, guard)
                v.drain()
                v.tensor_scalar(ctile[:, :], p1r[:, slot, 0:1024], thc[:, m:m + 1], None,
                                op0=ALU.is_gt)
                v.drain()
                v.scalar_tensor_tensor(ctile[:, :], p1r[:, slot, 0:1024],
                                       thc[:, 3 + m:4 + m], ctile[:, :],
                                       op0=ALU.is_gt, op1=ALU.add)
                v.drain()
                v.scalar_tensor_tensor(ctile[:, :], p1r[:, slot, 0:1024],
                                       nthc[:, m:m + 1], ctile[:, :],
                                       op0=ALU.is_lt, op1=ALU.subtract)
                v.drain()
                return v.scalar_tensor_tensor(qout[:, oslot, :], p1r[:, slot, 0:1024],
                                              nthc[:, 3 + m:4 + m], ctile[:, :],
                                              op0=ALU.is_lt, op1=ALU.add
                                              ).then_inc(h1, 1)
            bg.append(quant)
            wait(GPS, v_q, v_q.n)
            nat_st[glob] = dma(GPS, nat_tile(glob), qout[:, oslot, :], qoS[oslot])

    tp_q = [0]

    def emit_tp_store_tile(gq):
        g, q = gq
        wait(SYN, c_qt, q + 1)
        for k in range(8):
            qt_st[q] = dma(SYN, qt_dst(g, k), tstg[:, k * 128:(k + 1) * 128],
                           qtS[q % 2])

    def emit_tpass_oct(m, oct):
        if tp_q[0] == 0:
            wait(TEN, v_ev, v_ev.n)
            wait(SCA, v_ev, v_ev.n)
        wait(TEN, c_si, c_si.n)
        prev = None
        for i in range(8):
            g = m * 128 + oct * 8 + i
            q = tp_q[0]
            tp_q[0] += 1
            ridx = rbig.k
            if ridx >= 3:
                wait(SYN, p_, big_free[ridx - 3])
            wait(SYN, qoS[g % 2], nat_st[g])
            _, lsem, lv = rbig.load(SYN, big[:, ridx % 3, 0:1024], nat_tile(g))
            if prev is not None:
                emit_tp_store_tile(prev)
            wait(TEN, lsem, lv)
            if q >= 2:
                wait(TEN, c_qt, q - 1)
            for k in range(8):
                fn = (lambda t, q=q, k=k, ridx=ridx: t.matmul(
                    X2[:, q % 2, k * 128:(k + 1) * 128],
                    big[:, ridx % 3, k * 128:(k + 1) * 128], ident[:, :],
                    start=True, stop=True))
                if k == 7:
                    big_free.append(opi(TEN, p_, fn))
                else:
                    op(TEN, fn)
            wait(SCA, p_, big_free[-1])
            if q >= 1:
                wait(SCA, qtS[(q - 1) % 2], qt_st[q - 1])
            c_qt.n += 1
            em(SCA, lambda s, q=q, h=c_qt.h: s.copy(tstg[:, :], X2[:, q % 2, :]
                                                    ).then_inc(h, 1))
            prev = (g, q)
        emit_tp_store_tile(prev)
        wait(SYN, qtS[0], qtS[0].n)
        wait(SYN, qtS[1], qtS[1].n)
        qprogT.n += 1
        em(SYN, lambda gg, h=qprogT.h: gg.sem_inc(h, 1))

    q_need_up = [2 * eg + 1 for eg in range(16)]
    q_need_gt = [2 * eg + 2 for eg in range(16)]
    q_need_dn = [32 + 2 * (c + 1) for c in range(8)]

    def emit_ln(src_tile, dst_tile, W, resid=False):
        wait(TEN, v_ev, v_ev.n)
        sq_base = v_sq.n
        sum_pe = [0] * DT
        for dt in range(DT):
            slot = dt % 2
            if dt >= 2:
                wait(VEC, p_, sum_pe[dt - 2])
            v_sq.n += 1
            em(VEC, lambda v, dt=dt, slot=slot, s=src_tile, h=v_sq.h, W=W: v.tensor_mul(
                sqr[:, slot, 0:W], s(dt), s(dt)).then_inc(h, 1))
            op(TEN, lambda t, dt=dt, s=src_tile, W=W: t.matmul(
                Yb[0:1, 0, 0:W], onescb[:, 0:1], s(dt),
                start=(dt == 0), stop=(dt == DT - 1)))
            wait(TEN, v_sq, sq_base + dt + 1)
            sum_pe[dt] = opi(TEN, p_, lambda t, dt=dt, slot=slot, W=W: t.matmul(
                Yb[0:1, 1, 0:W], onescb[:, 0:1], sqr[:, slot, 0:W],
                start=(dt == 0), stop=(dt == DT - 1)))
        wait(VEC, p_, p_.n)
        op(VEC, lambda v, W=W: v.tensor_scalar_mul(rowf[0:1, 0, 0:W], Yb[0:1, 0, 0:W], float(1.0 / D)))
        op(VEC, lambda v, W=W: v.tensor_scalar_mul(rowf[0:1, 1, 0:W], Yb[0:1, 1, 0:W], float(1.0 / D)))
        vd(VEC)
        op(VEC, lambda v, W=W: v.tensor_mul(rowf[0:1, 2, 0:W], rowf[0:1, 0, 0:W], rowf[0:1, 0, 0:W]))
        vd(VEC)
        vv = opi(VEC, v_, lambda v, W=W: v.tensor_sub(rowf[0:1, 2, 0:W], rowf[0:1, 1, 0:W],
                                                      rowf[0:1, 2, 0:W]))
        wait(SCA, v_, vv)
        op(SCA, lambda s, W=W: s.activation(rowf[0:1, 3, 0:W], rowf[0:1, 2, 0:W], AF.Ln,
                                            bias=float(LN_EPS), scale=1.0))
        em(SCA, lambda s: s.drain())
        cc = opi(SCA, c_, lambda s, W=W: s.activation(rowf[0:1, 2, 0:W], rowf[0:1, 3, 0:W], AF.Exp,
                                                      bias=0.0, scale=-0.5))
        wait(VEC, c_, cc)
        vd(VEC)
        op(VEC, lambda v, W=W: v.tensor_copy(rowb[0:1, 0, 0:W], rowf[0:1, 2, 0:W]))
        vv2 = opi(VEC, v_, lambda v, W=W: v.scalar_tensor_tensor(
            rowb[0:1, 1, 0:W], rowf[0:1, 0, 0:W], -1.0, rowf[0:1, 2, 0:W],
            op0=ALU.mult, op1=ALU.mult))
        wait(TEN, v_, vv2)
        op(TEN, lambda t, W=W: t.matmul(Yb[:, 0, 0:W], ones1[0:1, :], rowb[0:1, 0, 0:W],
                                        start=True, stop=True))
        bc_pe = opi(TEN, p_, lambda t, W=W: t.matmul(Yb[:, 1, 0:W], ones1[0:1, :], rowb[0:1, 1, 0:W],
                                                     start=True, stop=True))
        wait(VEC, p_, bc_pe)
        op(VEC, lambda v, W=W: v.tensor_copy(rbc[:, 0:W], Yb[:, 0, 0:W]))
        bc_v = opi(VEC, v_ev, lambda v, W=W: v.tensor_copy(abc[:, 0:W], Yb[:, 1, 0:W]))
        vd(VEC)
        for dt in range(DT):
            if not resid:
                op(VEC, lambda v, dt=dt, s=src_tile, d=dst_tile, W=W: v.tensor_mul(
                    d(dt), s(dt), rbc[:, 0:W]))
            else:
                op(VEC, lambda v, dt=dt, s=src_tile, W=W: v.tensor_mul(
                    s(dt), s(dt), rbc[:, 0:W]))
        vd(VEC)
        for dt in range(DT):
            tgt = (lambda v, dt=dt, s=src_tile, d=dst_tile, W=W: v.tensor_add(
                (s(dt) if resid else d(dt)), (s(dt) if resid else d(dt)), abc[:, 0:W]))
            if dt == DT - 1 and not resid:
                opi(VEC, v_, tgt)
            else:
                op(VEC, tgt)
        if resid:
            vd(VEC)
            for dt in range(DT):
                tgt = (lambda v, dt=dt, s=src_tile, d=dst_tile, W=W: v.tensor_add(
                    d(dt), d(dt), s(dt)))
                if dt == DT - 1:
                    opi(VEC, v_, tgt)
                else:
                    op(VEC, tgt)
        return v_.n, bc_v

    ln3_done = [0] * NTB
    tp_h1 = tp_lg = tp_p = tp_at = tp_o = tp_x1 = tp_hid = tp_f = None

    def src_xT(tb):
        return lambda dt: xT[:, dt, tb * TB:(tb + 1) * TB]

    def attn_tb(tb):
        nonlocal tp_h1, tp_lg, tp_p, tp_at, tp_o, tp_x1
        if tb == 0:
            wait(TEN, s_ld, xT_ready)
            wait(VEC, s_ld, xT_ready)
            wait(TEN, v_ev, v_ev.n)
        h1_v, _ = emit_ln(src_xT(tb), lambda dt: h1T[:, dt, :], TB)
        if tb == 0:
            tp_h1 = tap("tap_h1", h1T.ap(), v_, h1_v)
        lg_ev = [0, 0]
        for pair in range(2):
            wait(TEN, v_, h1_v)
            wait(TEN, v_ev, v_ev.n)
            for dti in range(2 * DT):
                dt, part = dti // 2, dti % 2
                srcm = mm_mat if part == 0 else mm_lo
                k = rmm.k
                if k == 0:
                    wait(SYN, msT[0], mmst_done[0])
                    wait(SYN, msT[1], mmst_done[1])
                    wait(SYN, mloT, mmst_done[2])
                if k >= 2:
                    wait(SYN, p_, mm_free[k - 2])
                _, msem, mch = rmm.load(SYN, mmr[:, dti % 2, :],
                                        srcm[dt * 128:(dt + 1) * 128, :])
                wait(TEN, msem, mch)
                for tp in range(2):
                    t128 = pair * 2 + tp
                    for nn in range(2):
                        fn = (lambda t, dti=dti, dt=dt, tp=tp, t128=t128, nn=nn: t.matmul(
                            X2[:, tp, nn * 512:(nn + 1) * 512],
                            h1T[:, dt, t128 * 128:(t128 + 1) * 128],
                            mmr[:, dti % 2, nn * 512:(nn + 1) * 512],
                            start=(dti == 0), stop=False))
                        if tp == 1 and nn == 1:
                            mm_free.append(opi(TEN, p_, fn))
                        else:
                            op(TEN, fn)
            for tp in range(2):
                for nn in range(2):
                    fn = (lambda t, tp=tp, nn=nn: t.matmul(
                        X2[:, tp, nn * 512:(nn + 1) * 512], ones1[0:1, :],
                        blog[0:1, nn * 512:(nn + 1) * 512], start=False, stop=True))
                    if tp == 1 and nn == 1:
                        pe_lg = opi(TEN, p_, fn)
                    else:
                        op(TEN, fn)
            wait(VEC, p_, pe_lg)
            if tb == 0 and pair == 1:
                tap_gate(VEC, tp_lg)
            for tp in range(2):
                fn = (lambda v, tp=tp: v.tensor_copy(logit[:, tp, :], X2[:, tp, :]))
                if tp == 1:
                    lg_ev[pair] = opi(VEC, v_ev, fn)
                else:
                    op(VEC, fn)
            if tb == 0 and pair == 0:
                tp_lg = tap("tap_lg", logit.ap(), v_ev, lg_ev[0])
            if tb == 0 and pair == 1:
                tap_gate(VEC, tp_lg)
                tap_gate(VEC, tp_p)
            vd(VEC)
            for tp in range(2):
                for h in range(H):
                    op(VEC, lambda v, tp=tp, h=h: v.max(out=mx8[:, h, :],
                                                        in_=logit[:, tp, h * L:(h + 1) * L]))
                vd(VEC)
                op(VEC, lambda v: v.tensor_copy(m1r[:, :], mx8[:, :, 0]))
                op(VEC, lambda v: v.tensor_copy(taur[:, :], mx8[:, :, 3]))
                l3 = logit[:, tp, :].rearrange("p (h l) -> p h l", l=L)
                m3 = m1r.ap().rearrange("p (h o) -> p h o", o=1).to_broadcast([128, H, L])
                t3 = taur.ap().rearrange("p (h o) -> p h o", o=1).to_broadcast([128, H, L])
                e3 = ework.ap().rearrange("p (h l) -> p h l", l=L)
                k3 = maskb.ap().rearrange("p (h l) -> p h l", l=L)
                vd(VEC)
                op(VEC, lambda v, l3=l3, t3=t3, k3=k3: v.tensor_tensor(k3, l3, t3, op=ALU.is_ge))
                vv = opi(VEC, v_, lambda v, l3=l3, m3=m3, e3=e3: v.tensor_tensor(
                    e3, l3, m3, op=ALU.subtract))
                wait(SCA, v_, vv)
                cc = opi(SCA, c_, lambda s: s.activation(ework[:, :], ework[:, :], AF.Exp,
                                                         bias=0.0, scale=1.0))
                wait(VEC, c_, cc)
                vd(VEC)
                op(VEC, lambda v: v.tensor_mul(ework[:, :], ework[:, :], maskb[:, :]))
                vd(VEC)
                op(VEC, lambda v, e3=e3: v.reduce_sum(ssum[:, :], e3, axis=AX.X))
                vd(VEC)
                op(VEC, lambda v: v.reciprocal(rsum[:, :], ssum[:, :]))
                vd(VEC)
                r3 = rsum.ap().rearrange("p (h o) -> p h o", o=1).to_broadcast([128, H, L])
                p3 = p_sb[:, tp, :].rearrange("p (h l) -> p h l", l=L)
                v_p.n += 1
                em(VEC, lambda v, e3=e3, r3=r3, p3=p3, h=v_p.h: v.tensor_tensor(
                    p3, e3, r3, op=ALU.mult).then_inc(h, 1))
            wait(TEN, v_ev, lg_ev[pair])
            for tp in range(2):
                t128 = pair * 2 + tp
                wait(TEN, v_p, 4 * tb + t128 + 1)
                for hl in range(8):
                    fn = (lambda t, tp=tp, hl=hl: t.transpose(
                        Xbf[:, tp, hl * 128:(hl + 1) * 128],
                        p_sb[:, tp, hl * 128:(hl + 1) * 128], ident[:, :]))
                    if hl == 7:
                        opi(TEN, p_, fn)
                    else:
                        op(TEN, fn)
                wait(VEC, p_, p_.n)
                opi(VEC, v_ev, lambda v, tp=tp, t128=t128: v.tensor_copy(
                    pT_sb.ap()[:, :, t128 * 128:(t128 + 1) * 128],
                    Xbf[:, tp, :].rearrange("p (a b) -> p a b", a=8)))
            if tb == 0 and pair == 0:
                tp_p = tap("tap_p", p_sb.ap(), v_p, 2)
        pT_done = v_ev.n
        drain_bg(100)
        for dh in range(2):
            if dh == 0:
                wait(TEN, v_ev, pT_done)
            for ci in range(16):
                ridx = rbig.k
                if ridx >= 3:
                    wait(SYN, p_, big_free[ridx - 3])
                if tb == 0 and dh == 0 and ci == 0:
                    wait(SYN, stgT["wv"][0], wvbf_done[0])
                    wait(SYN, stgT["wv"][1], wvbf_done[1])
                _, lsem, lv = rbig.load(SYN, big[:, ridx % 3, 0:1024],
                                        wv_bf[dh * 1024:(dh + 1) * 1024,
                                              ci * 128:(ci + 1) * 128], transpose=True)
                wait(TEN, lsem, lv)
                rhs = h1T[:, ci, :]
                for k in range(8):
                    ps = Xb[:, k, :] if k < 4 else Yb[:, k - 4, :]
                    fn = (lambda t, ps=ps, ridx=ridx, k=k, rhs=rhs, ci=ci: t.matmul(
                        ps, big[:, ridx % 3, k * 128:(k + 1) * 128], rhs,
                        start=(ci == 0), stop=False))
                    if k == 7:
                        big_free.append(opi(TEN, p_, fn))
                    else:
                        op(TEN, fn)
            for k in range(8):
                ps = Xb[:, k, :] if k < 4 else Yb[:, k - 4, :]
                fn = (lambda t, ps=ps, k=k, dh=dh: t.matmul(
                    ps, bv_sb[0:1, dh * 1024 + k * 128:dh * 1024 + (k + 1) * 128],
                    onesN[0:1, :], start=False, stop=True))
                if k == 7:
                    at_pe = opi(TEN, p_, fn)
                else:
                    op(TEN, fn)
            wait(VEC, p_, at_pe)
            if tb == 1 and dh == 0:
                tap_gate(VEC, tp_at)
            for k in range(8):
                ps = Xb[:, k, :] if k < 4 else Yb[:, k - 4, :]
                fn = (lambda v, ps=ps, k=k, dh=dh: v.tensor_copy(
                    attnT[:, dh * 8 + k, :], ps))
                if k == 7:
                    opi(VEC, v_ev, fn)
                else:
                    op(VEC, fn)
            wait(TEN, v_ev, v_ev.n)
        if tb == 0:
            tp_at = tap("tap_at", attnT.ap(), v_ev, v_ev.n)
        drain_bg(40)
        for dh in range(2):
            for ci in range(24):
                ridx = rbig.k
                if ridx >= 3:
                    wait(SYN, p_, big_free[ridx - 3])
                if ci < 16:
                    _, lsem, lv = rbig.load(SYN, big[:, ridx % 3, 0:1024],
                                            wo_bf[dh * 1024:(dh + 1) * 1024,
                                                  ci * 128:(ci + 1) * 128], transpose=True)
                else:
                    if tb == 0 and dh == 0 and ci == 16:
                        wait(SYN, gsT[0], gst_done[0])
                        wait(SYN, gsT[1], gst_done[1])
                    _, lsem, lv = rbig.load(SYN, big[:, ridx % 3, 0:1024],
                                            g_mat[(ci - 16) * 128:(ci - 15) * 128,
                                                  dh * 1024:(dh + 1) * 1024])
                wait(TEN, lsem, lv)
                rhs = attnT[:, ci, :] if ci < 16 else pT_sb[:, ci - 16, :]
                for k in range(8):
                    ps = Xb[:, k, :] if k < 4 else Yb[:, k - 4, :]
                    fn = (lambda t, ps=ps, ridx=ridx, k=k, ci=ci, rhs=rhs: t.matmul(
                        ps, big[:, ridx % 3, k * 128:(k + 1) * 128],
                        rhs, start=(ci == 0), stop=False))
                    if k == 7:
                        big_free.append(opi(TEN, p_, fn))
                    else:
                        op(TEN, fn)
            for k in range(8):
                ps = Xb[:, k, :] if k < 4 else Yb[:, k - 4, :]
                fn = (lambda t, ps=ps, k=k, dh=dh: t.matmul(
                    ps, bo_sb[0:1, dh * 1024 + k * 128:dh * 1024 + (k + 1) * 128],
                    onesN[0:1, :], start=False, stop=True))
                if k == 7:
                    o_pe = opi(TEN, p_, fn)
                else:
                    op(TEN, fn)
            wait(VEC, p_, o_pe)
            if tb == 0 and dh == 0:
                tap_gate(VEC, tp_h1)
            for k in range(8):
                ps = Xb[:, k, :] if k < 4 else Yb[:, k - 4, :]
                fn = (lambda v, ps=ps, k=k, dh=dh: v.tensor_copy(
                    h1T[:, dh * 8 + k, :], ps))
                if k == 7:
                    opi(VEC, v_ev, fn)
                else:
                    op(VEC, fn)
            wait(TEN, v_ev, v_ev.n)
        if tb == 0:
            tp_o = tap("tap_o", h1T.ap(), v_ev, v_ev.n)
            tap_gate(VEC, tp_o)
            tap_gate(VEC, tp_h1)
        drain_bg(40)
        v_ln3, _ = emit_ln(lambda dt: h1T[:, dt, :], src_xT(tb), TB, resid=True)
        ln3_done[tb] = v_ln3
        if tb == 0:
            tp_x1 = tap("tap_x1", xT.ap()[:, :, 0:TB], v_, v_ln3)
        drain_bg(30)

    last_ln4 = [0]

    def ffn_tb(tb):
        nonlocal tp_hid, tp_f
        wait(TEN, v_, ln3_done[tb])
        h2_v, _ = emit_ln(src_xT(tb), lambda dt: h1T[:, dt, :], TB)
        for eg in range(16):
            if eg == 0:
                wait(TEN, v_, h2_v)
            if tb == 0:
                emit_tpass_oct(0, eg)
                emit_tpass_oct(1, eg)
                wait(TEN, c_qt, 16 * (eg + 1))
            for mat in range(2):
                srcq = up_qT if mat == 0 else gt_qT
                need = q_need_up[eg] if mat == 0 else q_need_gt[eg]
                if mat == 0 and eg > 0:
                    wait(TEN, c_si, c_si.n)
                if mat == 1 and eg > 0:
                    wait(TEN, v_hid, v_hid.n)
                for d_in in range(DT):
                    ridx = rsml.k
                    if ridx == 0:
                        wait(SYN, v_, ln3_done[3])
                    if ridx >= 4:
                        wait(SYN, p_, sml_free[ridx - 4])
                    if d_in == 0 and tb == 0:
                        wait(SYN, qprogT, need)
                    _, lsem, lv = rsml.load(SYN, sml[:, ridx % 4, :],
                                            srcq[d_in * 128:(d_in + 1) * 128,
                                                 eg * 512:(eg + 1) * 512])
                    wait(TEN, lsem, lv)
                    for et in range(4):
                        ps = Xb[:, et, :] if mat == 0 else Yb[:, et, :]
                        fn = (lambda t, ps=ps, ridx=ridx, et=et, d_in=d_in: t.matmul(
                            ps, sml[:, ridx % 4, et * 128:(et + 1) * 128],
                            h1T[:, d_in, :], start=(d_in == 0), stop=(d_in == DT - 1)))
                        if et == 3:
                            sml_free.append(opi(TEN, p_, fn))
                        else:
                            op(TEN, fn)
                if mat == 0:
                    up_pe = sml_free[-1]
                else:
                    gt_pe = sml_free[-1]
            wait(SCA, p_, up_pe)
            if eg > 0:
                wait(SCA, v_hid, v_hid.n)
            if tb == 0 and eg == 0:
                wait(SCA, v_ev, th_ready0[0])
            for et in range(4):
                fn = (lambda s, et=et: s.activation(silu_sb[:, et, :], Xb[:, et, :], AF.Silu,
                                                    bias=0.0, scale=thc[:, 6:7]))
                if et == 3:
                    c_si.n += 1
                    em(SCA, lambda s, f=fn, h=c_si.h: f(s).then_inc(h, 1))
                else:
                    op(SCA, fn)
            wait(VEC, p_, gt_pe)
            wait(VEC, c_si, c_si.n)
            if tb == 1 and eg == 0:
                tap_gate(VEC, tp_hid)
            for et in range(4):
                fn = (lambda v, et=et, eg=eg: v.tensor_mul(
                    hid[:, eg * 4 + et, :], silu_sb[:, et, :], Yb[:, et, :]))
                if et == 3:
                    v_hid.n += 1
                    em(VEC, lambda v, f=fn, h=v_hid.h: f(v).then_inc(h, 1))
                else:
                    op(VEC, fn)
            drain_bg(24)
        if tb == 0:
            tp_hid = tap("tap_hid", hid.ap(), v_hid, v_hid.n)
            drain_bg(len(bg))
        if tb == 0:
            for oc in range(16):
                emit_tpass_oct(2, oc)
        wait(TEN, v_hid, v_hid.n)
        wait(TEN, c_si, c_si.n)
        if tb == 0:
            wait(TEN, c_qt, 384)
        for dh in range(2):
            for et in range(ET):
                ridx = rbig.k
                if ridx >= 3:
                    wait(SYN, p_, big_free[ridx - 3])
                if tb == 0 and dh == 0 and et % 8 == 0:
                    wait(SYN, qprogT, q_need_dn[et // 8])
                _, lsem, lv = rbig.load(SYN, big[:, ridx % 3, 0:1024],
                                        dn_qT[et * 128:(et + 1) * 128,
                                              dh * 1024:(dh + 1) * 1024])
                wait(TEN, lsem, lv)
                for k in range(8):
                    ps = Xb[:, k, :] if k < 4 else Yb[:, k - 4, :]
                    fn = (lambda t, ps=ps, ridx=ridx, k=k, et=et: t.matmul(
                        ps, big[:, ridx % 3, k * 128:(k + 1) * 128],
                        hid[:, et, :], start=(et == 0), stop=(et == ET - 1)))
                    if k == 7:
                        big_free.append(opi(TEN, p_, fn))
                    else:
                        op(TEN, fn)
            d_pe = big_free[-1]
            wait(VEC, p_, d_pe)
            for k in range(8):
                ps = Xb[:, k, :] if k < 4 else Yb[:, k - 4, :]
                fn = (lambda v, ps=ps, k=k, dh=dh: v.tensor_copy(h1T[:, dh * 8 + k, :], ps))
                if k == 7:
                    opi(VEC, v_ev, fn)
                else:
                    op(VEC, fn)
            wait(TEN, v_ev, v_ev.n)
        if tb == 0:
            tp_f = tap("tap_f", h1T.ap(), v_ev, v_ev.n)
            tap_gate(VEC, tp_f)
            tap_gate(VEC, tp_x1)
        lv4, _ = emit_ln(lambda dt: h1T[:, dt, :], src_xT(tb), TB, resid=True)
        last_ln4[0] = lv4
        drain_bg(20)

    th_ready0 = [0]
    attn_tb(0)
    attn_tb(1)
    th0 = emit_th(0)
    th_ready0[0] = th0
    emit_pass2(0, th0)
    attn_tb(2)
    th1 = emit_th(1)
    emit_pass2(1, th1)
    attn_tb(3)
    th2 = emit_th(2)
    emit_pass2(2, th2)
    ffn_tb(0)
    ffn_tb(1)
    ffn_tb(2)
    ffn_tb(3)

    wait(TEN, v_, last_ln4[0])
    for t128 in range(16):
        wait(TEN, v_ev, v_ev.n)
        for dt in range(DT):
            bank = dt // 8
            fn = (lambda t, dt=dt, t128=t128, bank=bank: t.transpose(
                Xbf[:, bank, (dt % 8) * 128:(dt % 8 + 1) * 128],
                xT[:, dt, t128 * 128:(t128 + 1) * 128], ident[:, :]))
            if dt == DT - 1:
                opi(TEN, p_, fn)
            else:
                op(TEN, fn)
        wait(VEC, p_, p_.n)
        if t128 >= 1:
            wait(VEC, o_st, 16 * t128)
        ev = opi(VEC, v_ev, lambda v: v.tensor_copy(
            ostg[:, :], Xbf[:, 0:2, :].rearrange("p a b -> p (a b)")))
        wait(GPS, v_ev, ev)
        dma(GPS, out[t128 * 128:(t128 + 1) * 128, :], ostg[:, :], o_st)
    wait(GPS, o_st, 16 * 16)
    wait(SYN, o_st, 16 * 16)

    with nc.Block() as block:
        @block.tensor
        def _(t):
            for fn in sched[TEN]:
                fn(t)

        @block.vector
        def _(v):
            for fn in sched[VEC]:
                fn(v)

        @block.scalar
        def _(s):
            for fn in sched[SCA]:
                fn(s)

        @block.gpsimd
        def _(g):
            for fn in sched[GPS]:
                fn(g)

        @block.sync
        def _(s):
            for fn in sched[SYN]:
                fn(s)

    return nc


_NC_CACHE = {}


def kernel(L0_attn_hidden_states, L0_attn_Wq, L0_attn_bq, L0_attn_Wk, L0_attn_bk,
           L0_attn_Wv, L0_attn_bv, L0_attn_Wo, L0_attn_bo, L0_attn_latent_keys,
           L0_ffn_W_up, L0_ffn_W_gate, L0_ffn_W_down):
    x = np.ascontiguousarray(np.asarray(L0_attn_hidden_states, dtype=np.float32)).reshape(16384, 2048)
    w = {
        "wq": np.ascontiguousarray(np.asarray(L0_attn_Wq, dtype=np.float32)),
        "bq": np.ascontiguousarray(np.asarray(L0_attn_bq, dtype=np.float32)),
        "wv": np.ascontiguousarray(np.asarray(L0_attn_Wv, dtype=np.float32)),
        "bv": np.ascontiguousarray(np.asarray(L0_attn_bv, dtype=np.float32)),
        "wo": np.ascontiguousarray(np.asarray(L0_attn_Wo, dtype=np.float32)),
        "bo": np.ascontiguousarray(np.asarray(L0_attn_bo, dtype=np.float32)),
        "lat": np.ascontiguousarray(np.asarray(L0_attn_latent_keys, dtype=np.float32)),
        "wup": np.ascontiguousarray(np.asarray(L0_ffn_W_up, dtype=np.float32)),
        "wgt": np.ascontiguousarray(np.asarray(L0_ffn_W_gate, dtype=np.float32)),
        "wdn": np.ascontiguousarray(np.asarray(L0_ffn_W_down, dtype=np.float32)),
    }
    if "nc" not in _NC_CACHE:
        _NC_CACHE["nc"] = build_program()
    nc = _NC_CACHE["nc"]
    in_maps = [dict(w, x=x[c * TOK:(c + 1) * TOK]) for c in range(8)]
    res = run_bass_kernel_spmd(nc, in_maps, core_ids=list(range(8)))
    outs = [np.asarray(r["out"]) for r in res.results]
    return np.concatenate(outs, axis=0).reshape(2, 8192, 2048).astype(np.float32)


# version 11
# speedup vs baseline: 1.0018x; 1.0018x over previous
import numpy as np
import ml_dtypes
import concourse.bass as bass
import concourse.mybir as mybir
from concourse.bass_utils import run_bass_kernel_spmd

F32 = mybir.dt.float32
BF16 = mybir.dt.bfloat16
AF = mybir.ActivationFunctionType
ALU = mybir.AluOpType
AX = mybir.AxisListType

TOK = 2048
D = 2048
DT = 16
TB = 512
NTB = 4
H = 16
L = 64
HL = 1024
E = 8192
ET = 64
LN_EPS = 1e-5
Q_EPS = 1e-8
ISQ = float(1.0 / np.sqrt(128.0))
NW = float(E * D)

TEN, VEC, SCA, GPS, SYN = "tensor", "vector", "scalar", "gpsimd", "sync"


class Sem:
    __slots__ = ("h", "n")

    def __init__(self, nc, name):
        self.h = nc.alloc_semaphore(name)
        self.n = 0


def build_program(debug=False):
    nc = bass.Bass("TRN2", target_bir_lowering=False)

    x_in = nc.dram_tensor("x", [TOK, D], F32, kind="ExternalInput").ap()
    wq = nc.dram_tensor("wq", [D, D], F32, kind="ExternalInput").ap()
    bq = nc.dram_tensor("bq", [D], F32, kind="ExternalInput").ap()
    wv = nc.dram_tensor("wv", [D, D], F32, kind="ExternalInput").ap()
    bv = nc.dram_tensor("bv", [D], F32, kind="ExternalInput").ap()
    wo = nc.dram_tensor("wo", [D, D], F32, kind="ExternalInput").ap()
    bo = nc.dram_tensor("bo", [D], F32, kind="ExternalInput").ap()
    lat = nc.dram_tensor("lat", [L, D], F32, kind="ExternalInput").ap()
    wup = nc.dram_tensor("wup", [E, D], F32, kind="ExternalInput").ap()
    wgt = nc.dram_tensor("wgt", [E, D], F32, kind="ExternalInput").ap()
    wdn = nc.dram_tensor("wdn", [D, E], F32, kind="ExternalInput").ap()
    out = nc.dram_tensor("out", [TOK, D], F32, kind="ExternalOutput").ap()

    x_bf = nc.dram_tensor("x_bf", [TOK, D], BF16, kind="Internal").ap()
    wv_bf = nc.dram_tensor("wv_bf", [D, D], BF16, kind="Internal").ap()
    wo_bf = nc.dram_tensor("wo_bf", [D, D], BF16, kind="Internal").ap()
    g_mat = nc.dram_tensor("g_mat", [HL, D], BF16, kind="Internal").ap()
    mm_mat = nc.dram_tensor("mm_mat", [D, HL], BF16, kind="Internal").ap()
    mm_lo = nc.dram_tensor("mm_lo", [D, HL], BF16, kind="Internal").ap()
    up_q = nc.dram_tensor("up_q", [E, D], BF16, kind="Internal").ap()
    gt_q = nc.dram_tensor("gt_q", [E, D], BF16, kind="Internal").ap()
    dn_q = nc.dram_tensor("dn_q", [D, E], BF16, kind="Internal").ap()
    up_qT = nc.dram_tensor("up_qT", [D, E], BF16, kind="Internal").ap()
    gt_qT = nc.dram_tensor("gt_qT", [D, E], BF16, kind="Internal").ap()
    dn_qT = nc.dram_tensor("dn_qT", [E, D], BF16, kind="Internal").ap()
    s_scr = [nc.dram_tensor(f"s_scr{i}", [n], F32, kind="Internal").ap()
             for i, n in ((0, E), (1, E), (2, D))]
    ident_dram = nc.inline_tensor(np.eye(128, dtype=ml_dtypes.bfloat16), name="ident_c")

    def S(name, shape, dtype):
        return nc.alloc_sbuf_tensor(name, shape, dtype)

    xT = S("xT", [128, DT, TOK], BF16)
    h1T = S("h1T", [128, DT, TB], BF16)
    sqr = S("sqr", [128, 2, TB], BF16)
    rowf = S("rowf", [1, 4, TB], F32)
    rowb = S("rowb", [1, 2, TB], BF16)
    rbc = S("rbc", [128, TB], BF16)
    abc = S("abc", [128, TB], BF16)
    p1r = S("p1r", [128, 2, 1032], F32)
    ctile = S("ctile", [128, 1024], BF16)
    qout = S("qout", [128, 2, 1024], BF16)
    tstg = S("tstg", [128, 1024], BF16)
    tstg2 = S("tstg2", [128, 1024], BF16)
    big = S("big", [128, 3, 1024], BF16)
    ident = S("ident", [128, 128], BF16)
    latT = S("latT", [128, H, L], BF16)
    bqT = S("bqT", [128, 16], BF16)
    bq_sb = S("bq_sb", [16, 128], BF16)
    blog = S("blog", [1, HL], BF16)
    bv_sb = S("bv_sb", [1, D], BF16)
    bo_sb = S("bo_sb", [1, D], BF16)
    ones1 = S("ones1", [1, 128], BF16)
    ones1f = S("ones1f", [1, 128], F32)
    onescb = S("onescb", [128, 1], BF16)
    onesN = S("onesN", [1, TB], BF16)
    rowacc = [S("rowacc_up", [128, 64], F32), S("rowacc_gt", [128, 64], F32),
              S("rowacc_dn", [128, 16], F32)]
    line = S("line", [1, 2, 72], F32)
    thc = S("thc", [128, 8], F32)
    nthc = S("nthc", [128, 8], F32)
    sc1 = S("sc1", [1, 16], F32)
    cpool = S("cpool", [1, 16], F32)
    mx8 = S("mx8", [128, H, 8], F32)
    m1r = S("m1r", [128, H], F32)
    taur = S("taur", [128, H], F32)
    ssum = S("ssum", [128, H], F32)
    rsum = S("rsum", [128, H], F32)

    AR_B = 72 * 1024
    arena = nc.alloc_sbuf_tensor("arena", [128, AR_B], mybir.dt.uint8)
    ar0 = nc.lookup_mloc(arena).addr

    def A(name, shape, dtype, off):
        return nc.alloc_sbuf_tensor_at(name, shape, dtype, offset=ar0 + off)

    hid = A("hid", [128, ET, TB], BF16, 0)
    silu_sb = A("silu_sb", [128, 4, TB], BF16, 64 * 1024)
    logit = A("logit", [128, 2, HL], F32, 0)
    ework = A("ework", [128, HL], F32, 8 * 1024)
    maskb = A("maskb", [128, HL], BF16, 12 * 1024)
    p_sb = A("p_sb", [128, 2, HL], BF16, 14 * 1024)
    pT_sb = A("pT_sb", [128, 8, TB], BF16, 18 * 1024)
    attnT = A("attnT", [128, DT, TB], BF16, 26 * 1024)
    stg = A("stg", [128, 2, D], BF16, 42 * 1024)
    gstg = A("gstg", [64, 2, D], BF16, 50 * 1024)
    mstg = A("mstg", [128, 2, 1024], BF16, 58 * 1024)
    mlo_st = A("mlo_st", [128, 1024], BF16, 62 * 1024)
    pf32 = A("pf32", [128, 1024], F32, 50 * 1024)
    lat_sb = A("lat_sb", [64, D], BF16, 26 * 1024)
    mmr = A("mmr", [128, 2, HL], BF16, 58 * 1024)
    sml = A("sml", [128, 4, TB], BF16, 68 * 1024)
    ostg = A("ostg", [128, D], F32, 50 * 1024)


    taps = {}
    if debug:
        for nm, shape, dtp in (("tap_h1", [128, DT, TB], BF16), ("tap_lg", [128, 2, HL], F32),
                               ("tap_p", [128, 2, HL], BF16), ("tap_at", [128, DT, TB], BF16),
                               ("tap_o", [128, DT, TB], BF16), ("tap_x1", [128, DT, TB], BF16),
                               ("tap_hid", [128, ET, TB], BF16), ("tap_f", [128, DT, TB], BF16)):
            taps[nm] = nc.dram_tensor(nm, shape, dtp, kind="Internal").ap()

    psX = nc.alloc_psum_tensor("psX", [128, 4, 512], F32)
    psY = nc.alloc_psum_tensor("psY", [128, 4, 512], F32)
    Xb = psX.ap()
    Yb = psY.ap()
    Xflat = Xb.rearrange("p a b -> p (a b)")
    Yflat = Yb.rearrange("p a b -> p (a b)")
    X2 = Xflat.rearrange("p (a b) -> p a b", a=2)
    X8 = Xb.rearrange("p a (b c) -> p (a b) c", b=2)
    Y8 = Yb.rearrange("p a (b c) -> p (a b) c", b=2)
    Xbf = Xb.bitcast(BF16)
    Ybf = Yb.bitcast(BF16)

    sched = {k: [] for k in (TEN, VEC, SCA, GPS, SYN)}
    sems = {}

    def sem(name):
        if name not in sems:
            sems[name] = Sem(nc, name)
        return sems[name]

    def em(eng, fn):
        sched[eng].append(fn)

    def wait(eng, s, n):
        if n > 0:
            em(eng, lambda e, h=s.h, v=n: e.wait_ge(h, v))

    def dma(eng, out_ap, in_ap, s, transpose=False):
        s.n += 16
        if transpose:
            em(eng, lambda e, o=out_ap, i=in_ap, h=s.h: e.dma_start(o, i, transpose=True).then_inc(h, 16))
        else:
            em(eng, lambda e, o=out_ap, i=in_ap, h=s.h: e.dma_start(o, i).then_inc(h, 16))
        return s.n

    def op(eng, fn):
        em(eng, fn)

    def opi(eng, s, fn):
        s.n += 1
        em(eng, lambda e, f=fn, h=s.h: f(e).then_inc(h, 1))
        return s.n

    def vd(eng):
        em(eng, lambda e: e.drain())

    def nopi(eng, s):
        s.n += 1
        em(eng, lambda e, h=s.h: e.nop().then_inc(h, 1))
        return s.n

    g_ld = sem("g_ld")
    g_st = sem("g_st")
    s_ld = sem("s_ld")
    q_st = sem("q_st")
    o_st = sem("o_st")
    v_ = sem("v_")
    p_ = sem("p_")
    c_ = sem("c_")
    v_p1 = sem("v_p1")
    v_q = sem("v_q")
    th_dm = sem("th_dm")
    th_ln = [sem("th_ln0"), sem("th_ln1")]
    v_sq = sem("v_sq")
    v_ev = sem("v_ev")
    v_p = sem("v_p")
    fbig = sem("fbig")
    fsml = sem("fsml")
    fmm = sem("fmm")
    fq2 = sem("fq2")
    c_si = sem("c_si")
    v_hid = sem("v_hid")
    g_c = sem("g_c")

    class RingLd:

        def __init__(self, name, nslots):
            self.sems = [sem(f"{name}_s{i}") for i in range(nslots)]
            self.nslots = nslots
            self.k = 0

        def load(self, eng, out_ap, in_ap, transpose=False):
            slot = self.k % self.nslots
            s = self.sems[slot]
            self.k += 1
            v = dma(eng, out_ap, in_ap, s, transpose=transpose)
            return slot, s, v

    rbig = RingLd("rbig", 3)
    rsml = RingLd("rsml", 4)
    rmm = RingLd("rmm", 2)
    big_free = []
    sml_free = []
    mm_free = []
    rp1 = RingLd("rp1", 2)
    rstg = RingLd("rstg", 2)
    stgT = {name: [sem(f"stgT_{name}0"), sem(f"stgT_{name}1")]
            for name in ("x", "wv", "wo")}
    msT = [sem("msT0"), sem("msT1")]
    mloT = sem("mloT")
    gsT = [sem("gsT0"), sem("gsT1")]
    g_sm = sem("g_sm")
    qoS = [sem("qoS0"), sem("qoS1")]
    qtS = [sem("qtS0"), sem("qtS1")]
    qprogT = sem("qprogT")
    c_qt = sem("c_qt")
    qprog = sem("qprog")
    g_tp = sem("g_tp")

    CONSTS = [1.0 / D, LN_EPS, Q_EPS, 0.5, -0.5, 1.0 / NW, ISQ, -1.0, float(2.0 ** -30), -128.0]
    for i, cv in enumerate(CONSTS):
        op(GPS, lambda g, i=i, cv=cv: g.memset(cpool[0:1, i:i + 1], float(cv)))
        nc.const_aps.register(F32, float(cv), cpool[0:1, i:i + 1])
    op(GPS, lambda g: g.memset(ones1[:, :], 1.0))
    op(GPS, lambda g: g.memset(ones1f[:, :], 1.0))
    op(GPS, lambda g: g.memset(onescb[:, :], 1.0))
    op(GPS, lambda g: g.memset(onesN[:, :], 1.0))
    opi(GPS, g_c, lambda g: g.memset(sc1[0:1, :], 0.0))
    for eng in (TEN, VEC, SCA, SYN):
        wait(eng, g_c, g_c.n)

    dma(GPS, ident[:, :], ident_dram.ap(), g_sm)
    slot_store = [None, None]
    for name, dst, srcd in (("x", x_bf, x_in), ("wv", wv_bf, wv), ("wo", wo_bf, wo)):
        for i in range(16):
            slotp = (rstg.k) % 2
            if slot_store[slotp] is not None:
                wait(GPS, slot_store[slotp][0], slot_store[slotp][1])
            slot, lsem, lv = rstg.load(GPS, stg[:, slotp, :], srcd[i * 128:(i + 1) * 128, :])
            wait(GPS, lsem, lv)
            sv = dma(GPS, dst[i * 128:(i + 1) * 128, :], stg[:, slotp, :], stgT[name][slotp])
            slot_store[slotp] = (stgT[name][slotp], sv)
    xbf_done = (stgT["x"][0].n, stgT["x"][1].n)
    wvbf_done = (stgT["wv"][0].n, stgT["wv"][1].n)
    wobf_done = (stgT["wo"][0].n, stgT["wo"][1].n)

    dma(GPS, lat_sb[:, :], lat, g_sm)
    dma(GPS, bq_sb[:, :], bq.rearrange("(a b) -> a b", a=16), g_sm)
    dma(GPS, bv_sb[:, :], bv.rearrange("(a b) -> a b", a=1), g_sm)
    small_done = dma(GPS, bo_sb[:, :], bo.rearrange("(a b) -> a b", a=1), g_sm)

    wait(SYN, stgT["x"][0], xbf_done[0])
    wait(SYN, stgT["x"][1], xbf_done[1])
    xT_ready = 0
    for dt in range(DT):
        xT_ready = dma(SYN, xT[:, dt, :], x_bf[0:TOK, dt * 128:(dt + 1) * 128], s_ld, transpose=True)

    wait(TEN, g_sm, small_done)
    for grp in range(4):
        if grp > 0:
            wait(TEN, v_ev, v_ev.n)
        for k in range(4):
            h = grp * 4 + k
            fn = (lambda t, h=h, k=k: t.transpose(
                Ybf[:, 0, k * 64:(k + 1) * 64], lat_sb[:, h * 128:(h + 1) * 128],
                ident[0:64, 0:64]))
            if k == 3:
                opi(TEN, p_, fn)
            else:
                op(TEN, fn)
        wait(VEC, p_, p_.n)
        opi(VEC, v_ev, lambda v, grp=grp: v.tensor_copy(
            latT.ap().rearrange("p a b -> p (a b)")[:, grp * 256:(grp + 1) * 256],
            Ybf[:, 0, 0:256]))
    wait(TEN, v_ev, v_ev.n)
    opi(TEN, p_, lambda t: t.transpose(Ybf[:, 0, 0:16], bq_sb[:, :], ident[0:16, 0:16]))
    wait(VEC, p_, p_.n)
    latT_v = opi(VEC, v_ev, lambda v: v.tensor_copy(bqT[:, :], Ybf[:, 0, 0:16]))

    mm_pe = [0] * H
    mm_ev = [0] * H
    mm_store = [0] * H
    mlo_store = 0
    for h in range(H):
        slotp = rstg.k % 2
        if slot_store[slotp] is not None:
            wait(GPS, slot_store[slotp][0], slot_store[slotp][1])
            slot_store[slotp] = None
        if h >= 2:
            wait(GPS, p_, mm_pe[h - 2])
        slot, lsem, wq_ld = rstg.load(GPS, stg[:, slotp, :], wq[h * 128:(h + 1) * 128, :])
        wait(TEN, lsem, wq_ld)
        if h == 0:
            wait(TEN, v_ev, latT_v)
        if h >= 2:
            wait(TEN, v_ev, mm_ev[h - 2])
        for dt in range(DT):
            op(TEN, lambda t, h=h, dt=dt, slot=slot: t.matmul(
                X2[:, h % 2, dt * 64:(dt + 1) * 64], stg[:, slot, dt * 128:(dt + 1) * 128],
                latT[:, h, :], start=True, stop=True))
        bank = 1 if h < 8 else 2
        col = (h % 8) * 64
        mm_pe[h] = opi(TEN, p_, lambda t, h=h, bank=bank, col=col: t.matmul(
            Yb[0:1, bank, col:col + 64], bqT[:, h:h + 1], latT[:, h, :], start=True, stop=True))
        wait(VEC, p_, mm_pe[h])
        if h >= 2:
            wait(VEC, msT[h % 2], mm_store[h - 2])
        if h >= 1:
            wait(VEC, mloT, mlo_store)
        op(VEC, lambda v, h=h: v.tensor_scalar_mul(pf32[:, :], X2[:, h % 2, :], ISQ))
        vd(VEC)
        op(VEC, lambda v, h=h: v.tensor_copy(mstg[:, h % 2, :], pf32[:, :]))
        vd(VEC)
        mm_ev[h] = opi(VEC, v_ev, lambda v, h=h: v.tensor_sub(
            mlo_st[:, :], pf32[:, :], mstg[:, h % 2, :]))
        wait(GPS, v_ev, mm_ev[h])
        mm_store[h] = dma(GPS, mm_mat.rearrange("(a p) q -> p a q", p=128)[:, :, h * 64:(h + 1) * 64],
                          mstg[:, h % 2, :].rearrange("p (a b) -> p a b", b=64), msT[h % 2])
        mlo_store = dma(GPS, mm_lo.rearrange("(a p) q -> p a q", p=128)[:, :, h * 64:(h + 1) * 64],
                        mlo_st[:, :].rearrange("p (a b) -> p a b", b=64), mloT)
    mmst_done = (msT[0].n, msT[1].n, mloT.n)
    wait(VEC, p_, p_.n)
    op(VEC, lambda v: v.tensor_scalar_mul(blog[0:1, 0:512], Yb[0:1, 1, :], ISQ))
    blog_v = opi(VEC, v_ev, lambda v: v.tensor_scalar_mul(blog[0:1, 512:1024], Yb[0:1, 2, :], ISQ))

    g_pe = [0] * H
    g_ev = [0] * H
    g_store = [0] * H
    for h in range(H):
        if h == 0:
            wait(SYN, stgT["wo"][0], wobf_done[0])
            wait(SYN, stgT["wo"][1], wobf_done[1])
            wait(TEN, v_ev, blog_v)
        par = h % 2
        gps_flat = Xflat if par == 0 else Yflat
        if h >= 2:
            wait(TEN, v_ev, g_ev[h - 2])
        for half in range(2):
            ridx = rbig.k
            if ridx >= 3:
                wait(SYN, p_, big_free[ridx - 3])
            _, wsem, wch = rbig.load(SYN, big[:, ridx % 3, :],
                                     wo_bf[half * 1024:(half + 1) * 1024,
                                           h * 128:(h + 1) * 128], transpose=True)
            wait(TEN, wsem, wch)
            for n in range(2):
                fn = (lambda t, h=h, n=n, half=half, gf=gps_flat, ridx=ridx: t.matmul(
                    gf[0:64, (half * 2 + n) * 512:(half * 2 + n + 1) * 512],
                    latT[:, h, :], big[:, ridx % 3, n * 512:(n + 1) * 512],
                    start=True, stop=True))
                if n == 1:
                    big_free.append(opi(TEN, p_, fn))
                else:
                    op(TEN, fn)
        g_pe[h] = big_free[-1]
        wait(VEC, p_, g_pe[h])
        if h >= 2:
            wait(VEC, gsT[h % 2], g_store[h - 2])
        g_ev[h] = opi(VEC, v_ev, lambda v, par=par, gf=gps_flat: v.tensor_copy(
            gstg[:, par, :], gf[0:64, :]))
        wait(SYN, v_ev, g_ev[h])
        g_store[h] = dma(SYN, g_mat[h * 64:(h + 1) * 64, :], gstg[:, par, :], gsT[h % 2])
    gst_done = (gsT[0].n, gsT[1].n)

    bg = []
    bg_pos = [0]

    def drain_bg(k):
        e = min(bg_pos[0] + k, len(bg))
        for i in range(bg_pos[0], e):
            em(VEC, bg[i])
        bg_pos[0] = e

    def tap(name, src_ap, wsem, wval):
        if not debug:
            return None
        ts_ = sem("tps_" + name)
        wait(SYN, wsem, wval)
        dma(SYN, taps[name], src_ap, ts_)
        return ts_

    def tap_gate(eng, tok):
        if debug and tok is not None:
            wait(eng, tok, 16)

    def mat_tile_ap(m, j):
        if m == 0:
            return wup[(j // 2) * 128:(j // 2 + 1) * 128, (j % 2) * 1024:(j % 2 + 1) * 1024]
        if m == 1:
            return wgt[(j // 2) * 128:(j // 2 + 1) * 128, (j % 2) * 1024:(j % 2 + 1) * 1024]
        cc, rr = j // 16, j % 16
        return wdn[rr * 128:(rr + 1) * 128, cc * 1024:(cc + 1) * 1024]

    def mat_tile_ap_p1(m, j):
        if m < 2:
            return mat_tile_ap(m, j)
        rr, cc = j // 8, j % 8
        return wdn[rr * 128:(rr + 1) * 128, cc * 1024:(cc + 1) * 1024]

    v_p1_fold = [0, 0, 0]
    p1_done = []
    for m in range(3):
        ncc = 2 if m < 2 else 8
        for j in range(128):
            idx = m * 128 + j
            if idx >= 2:
                wait(GPS, v_p1, p1_done[idx - 2])
            slot, lsem, ldn = rp1.load(GPS, p1r[:, idx % 2, 1:1025], mat_tile_ap_p1(m, j))
            v_p1.n += 1
            p1_done.append(v_p1.n)
            cc, rb = j % ncc, j // ncc
            nxt = (idx + 1) % 2

            def p1job(v, ldn=ldn, slot=idx % 2, nxt=nxt, cc=cc, rb=rb, m=m, ncc=ncc,
                      h1=v_p1.h, lh=lsem.h):
                v.wait_ge(lh, ldn)
                lo = 1 if cc == 0 else 0
                dst = rowacc[m][:, rb:rb + 1] if cc == ncc - 1 else p1r[:, nxt, 0:1]
                r = v.reduce_sum(dst, p1r[:, slot, lo:1025],
                                 axis=AX.X, apply_absolute_value=True).then_inc(h1, 1)
                v.drain()
                return r
            bg.append(p1job)
        v_p1_fold[m] = v_p1.n

    def emit_th(m):
        nblk = 64 if m < 2 else 16
        racc = rowacc[m]
        scr = s_scr[m]
        wait(SYN, v_p1, v_p1_fold[m])
        for rb in range(nblk):
            dma(SYN, scr[rb * 128:(rb + 1) * 128], racc[:, rb:rb + 1], th_dm)
        wait(SYN, th_dm, th_dm.n)
        nch = nblk * 128 // 64
        red_done = []
        for c in range(nch):
            slot = c % 2
            if c >= 2:
                wait(SYN, v_p1, red_done[c - 2])
            ldv = dma(SYN, line[0:1, slot, 1:65],
                      scr[c * 64:(c + 1) * 64].rearrange("(q n) -> q n", q=1), th_ln[slot])
            wait(VEC, th_ln[slot], ldv)
            lo = 1 if c == 0 else 0
            dst = sc1[0:1, 0:1] if c == nch - 1 else line[0:1, (c + 1) % 2, 0:1]
            red_done.append(opi(VEC, v_p1, lambda v, dst=dst, slot=slot, lo=lo:
                                v.reduce_sum(dst, line[0:1, slot, lo:65], axis=AX.X)))
            vd(VEC)
        op(VEC, lambda v: v.tensor_scalar(sc1[0:1, 4:5], sc1[0:1, 0:1],
                                          float(1.0 / NW), float(Q_EPS),
                                          op0=ALU.mult, op1=ALU.add))
        vd(VEC)
        op(VEC, lambda v: v.tensor_scalar_mul(sc1[0:1, 5:6], sc1[0:1, 4:5], 0.5))
        op(VEC, lambda v: v.tensor_scalar_mul(sc1[0:1, 6:7], sc1[0:1, 4:5], -0.5))
        vd(VEC)
        op(VEC, lambda v: v.tensor_scalar_mul(sc1[0:1, 8:9], sc1[0:1, 5:6], float(2.0 ** -30)))
        vd(VEC)
        op(VEC, lambda v: v.tensor_add(sc1[0:1, 9:10], sc1[0:1, 5:6], sc1[0:1, 8:9]))
        vd(VEC)
        op(VEC, lambda v: v.tensor_sub(sc1[0:1, 10:11], sc1[0:1, 9:10], sc1[0:1, 5:6]))
        vd(VEC)
        op(VEC, lambda v: v.tensor_sub(sc1[0:1, 10:11], sc1[0:1, 10:11], sc1[0:1, 8:9]))
        vd(VEC)
        op(VEC, lambda v: v.tensor_scalar_mul(sc1[0:1, 10:11], sc1[0:1, 10:11], -128.0))
        vd(VEC)
        op(VEC, lambda v: v.tensor_sub(sc1[0:1, 11:12], sc1[0:1, 5:6], sc1[0:1, 10:11]))
        vrow1 = opi(VEC, v_, lambda v: v.tensor_add(sc1[0:1, 12:13], sc1[0:1, 5:6], sc1[0:1, 10:11]))
        vd(VEC)
        op(VEC, lambda v: v.tensor_scalar_mul(sc1[0:1, 13:14], sc1[0:1, 11:12], -1.0))
        vrow = opi(VEC, v_, lambda v: v.tensor_scalar_mul(sc1[0:1, 14:15], sc1[0:1, 12:13], -1.0))
        wait(TEN, v_, vrow)
        op(TEN, lambda t: t.matmul(Yb[:, 3, 0:1], ones1f[0:1, :], sc1[0:1, 11:12],
                                   start=True, stop=True))
        op(TEN, lambda t: t.matmul(Yb[:, 3, 1:2], ones1f[0:1, :], sc1[0:1, 12:13],
                                   start=True, stop=True))
        op(TEN, lambda t: t.matmul(Yb[:, 3, 2:3], ones1f[0:1, :], sc1[0:1, 13:14],
                                   start=True, stop=True))
        if m == 0:
            op(TEN, lambda t: t.matmul(Yb[:, 3, 3:4], ones1f[0:1, :], sc1[0:1, 14:15],
                                       start=True, stop=True))
            pe = opi(TEN, p_, lambda t: t.matmul(Yb[:, 3, 4:5], ones1f[0:1, :], sc1[0:1, 6:7],
                                                 start=True, stop=True))
        else:
            pe = opi(TEN, p_, lambda t: t.matmul(Yb[:, 3, 3:4], ones1f[0:1, :], sc1[0:1, 14:15],
                                                 start=True, stop=True))
        wait(VEC, p_, pe)
        op(VEC, lambda v, m=m: v.tensor_copy(thc[:, m:m + 1], Yb[:, 3, 0:1]))
        op(VEC, lambda v, m=m: v.tensor_copy(thc[:, 3 + m:4 + m], Yb[:, 3, 1:2]))
        op(VEC, lambda v, m=m: v.tensor_copy(nthc[:, m:m + 1], Yb[:, 3, 2:3]))
        ev = opi(VEC, v_ev, lambda v, m=m: v.tensor_copy(nthc[:, 3 + m:4 + m], Yb[:, 3, 3:4]))
        if m == 0:
            ev = opi(VEC, v_ev, lambda v: v.tensor_copy(thc[:, 6:7], Yb[:, 3, 4:5]))
        wait(TEN, v_ev, ev)
        return ev

    q2_done = []
    nat_st = {}
    qt_st = {}

    def qt_dst(g, k):
        m, j = g // 128, g % 128
        if m < 2:
            dstm = up_qT if m == 0 else gt_qT
            return dstm[(j % 2) * 1024 + k * 128:(j % 2) * 1024 + (k + 1) * 128,
                        (j // 2) * 128:(j // 2 + 1) * 128]
        cc, rr = j // 16, j % 16
        return dn_qT[cc * 1024 + k * 128:cc * 1024 + (k + 1) * 128,
                     rr * 128:(rr + 1) * 128]

    def nat_tile_mj(m, j):
        if m < 2:
            dstm = up_q if m == 0 else gt_q
            return dstm[(j // 2) * 128:(j // 2 + 1) * 128, (j % 2) * 1024:(j % 2 + 1) * 1024]
        cc, rr = j // 16, j % 16
        return dn_q[rr * 128:(rr + 1) * 128, cc * 1024:(cc + 1) * 1024]

    def emit_pass2(tiles, th_evs):
        for m, j in tiles:
            glob = len(q2_done)
            idx = 384 + glob
            slot = idx % 2
            th_ev = th_evs[m]
            if glob >= 2:
                wait(GPS, v_q, q2_done[glob - 2])
            else:
                wait(GPS, v_p1, p1_done[382 + glob])
            _, lsem, ldn = rp1.load(GPS, p1r[:, slot, 0:1024], mat_tile_ap(m, j))
            oslot = glob % 2
            guard = nat_st.get(glob - 2, 0)
            v_q.n += 1
            q2_done.append(v_q.n)

            def quant(v, m=m, slot=slot, oslot=oslot, ldn=ldn, th_ev=th_ev, guard=guard,
                      h1=v_q.h, lh=lsem.h):
                v.wait_ge(lh, ldn)
                v.wait_ge(v_ev.h, th_ev)
                if guard:
                    v.wait_ge(qoS[oslot].h, guard)
                v.drain()
                v.tensor_scalar(ctile[:, :], p1r[:, slot, 0:1024], thc[:, m:m + 1], None,
                                op0=ALU.is_gt)
                v.drain()
                v.scalar_tensor_tensor(ctile[:, :], p1r[:, slot, 0:1024],
                                       thc[:, 3 + m:4 + m], ctile[:, :],
                                       op0=ALU.is_gt, op1=ALU.add)
                v.drain()
                v.scalar_tensor_tensor(ctile[:, :], p1r[:, slot, 0:1024],
                                       nthc[:, m:m + 1], ctile[:, :],
                                       op0=ALU.is_lt, op1=ALU.subtract)
                v.drain()
                return v.scalar_tensor_tensor(qout[:, oslot, :], p1r[:, slot, 0:1024],
                                              nthc[:, 3 + m:4 + m], ctile[:, :],
                                              op0=ALU.is_lt, op1=ALU.add
                                              ).then_inc(h1, 1)
            bg.append(quant)
            wait(GPS, v_q, v_q.n)
            nat_st[glob] = dma(GPS, nat_tile_mj(m, j), qout[:, oslot, :], qoS[oslot])

    tp_q = [0]

    def emit_tp_store_tile(gq):
        g, q = gq
        ts = tstg if q % 2 == 0 else tstg2
        wait(SYN, c_qt, q + 1)
        for k in range(8):
            qt_st[q] = dma(SYN, qt_dst(g, k), ts[:, k * 128:(k + 1) * 128],
                           qtS[q % 2])

    def emit_tpass_oct(m, oct):
        if tp_q[0] == 0:
            wait(TEN, v_ev, v_ev.n)
            wait(SCA, v_ev, v_ev.n)
        wait(TEN, c_si, c_si.n)
        prev = None
        for i in range(8):
            j = oct * 8 + i
            if m < 2:
                ge = oct * 16 + m * 8 + i
            else:
                ge = 256 + j
            q = tp_q[0]
            tp_q[0] += 1
            ridx = rbig.k
            if ridx >= 3:
                wait(SYN, p_, big_free[ridx - 3])
            wait(SYN, qoS[ge % 2], nat_st[ge])
            _, lsem, lv = rbig.load(SYN, big[:, ridx % 3, 0:1024], nat_tile_mj(m, j))
            if prev is not None:
                emit_tp_store_tile(prev)
            wait(TEN, lsem, lv)
            if q >= 2:
                wait(TEN, c_qt, q - 1)
            for k in range(8):
                fn = (lambda t, q=q, k=k, ridx=ridx: t.matmul(
                    X2[:, q % 2, k * 128:(k + 1) * 128],
                    big[:, ridx % 3, k * 128:(k + 1) * 128], ident[:, :],
                    start=True, stop=True))
                if k == 7:
                    big_free.append(opi(TEN, p_, fn))
                else:
                    op(TEN, fn)
            wait(SCA, p_, big_free[-1])
            if q >= 2:
                wait(SCA, qtS[q % 2], qt_st[q - 2])
            c_qt.n += 1
            ts = tstg if q % 2 == 0 else tstg2
            em(SCA, lambda s, q=q, ts=ts, h=c_qt.h: s.copy(ts[:, :], X2[:, q % 2, :]
                                                           ).then_inc(h, 1))
            prev = (m * 128 + j, q)
        emit_tp_store_tile(prev)
        wait(SYN, qtS[0], qtS[0].n)
        wait(SYN, qtS[1], qtS[1].n)
        qprogT.n += 1
        em(SYN, lambda gg, h=qprogT.h: gg.sem_inc(h, 1))

    q_need_up = [2 * eg + 1 for eg in range(16)]
    q_need_gt = [2 * eg + 2 for eg in range(16)]
    q_need_dn = [32 + 2 * (c + 1) for c in range(8)]

    def emit_ln(src_tile, dst_tile, W, resid=False):
        wait(TEN, v_ev, v_ev.n)
        sq_base = v_sq.n
        sum_pe = [0] * DT
        for dt in range(DT):
            slot = dt % 2
            if dt >= 2:
                wait(VEC, p_, sum_pe[dt - 2])
            v_sq.n += 1
            em(VEC, lambda v, dt=dt, slot=slot, s=src_tile, h=v_sq.h, W=W: v.tensor_mul(
                sqr[:, slot, 0:W], s(dt), s(dt)).then_inc(h, 1))
            op(TEN, lambda t, dt=dt, s=src_tile, W=W: t.matmul(
                Yb[0:1, 0, 0:W], onescb[:, 0:1], s(dt),
                start=(dt == 0), stop=(dt == DT - 1)))
            wait(TEN, v_sq, sq_base + dt + 1)
            sum_pe[dt] = opi(TEN, p_, lambda t, dt=dt, slot=slot, W=W: t.matmul(
                Yb[0:1, 1, 0:W], onescb[:, 0:1], sqr[:, slot, 0:W],
                start=(dt == 0), stop=(dt == DT - 1)))
        wait(VEC, p_, p_.n)
        op(VEC, lambda v, W=W: v.tensor_scalar_mul(rowf[0:1, 0, 0:W], Yb[0:1, 0, 0:W], float(1.0 / D)))
        op(VEC, lambda v, W=W: v.tensor_scalar_mul(rowf[0:1, 1, 0:W], Yb[0:1, 1, 0:W], float(1.0 / D)))
        vd(VEC)
        op(VEC, lambda v, W=W: v.tensor_mul(rowf[0:1, 2, 0:W], rowf[0:1, 0, 0:W], rowf[0:1, 0, 0:W]))
        vd(VEC)
        vv = opi(VEC, v_, lambda v, W=W: v.tensor_sub(rowf[0:1, 2, 0:W], rowf[0:1, 1, 0:W],
                                                      rowf[0:1, 2, 0:W]))
        wait(SCA, v_, vv)
        op(SCA, lambda s, W=W: s.activation(rowf[0:1, 3, 0:W], rowf[0:1, 2, 0:W], AF.Ln,
                                            bias=float(LN_EPS), scale=1.0))
        em(SCA, lambda s: s.drain())
        cc = opi(SCA, c_, lambda s, W=W: s.activation(rowf[0:1, 2, 0:W], rowf[0:1, 3, 0:W], AF.Exp,
                                                      bias=0.0, scale=-0.5))
        wait(VEC, c_, cc)
        vd(VEC)
        op(VEC, lambda v, W=W: v.tensor_copy(rowb[0:1, 0, 0:W], rowf[0:1, 2, 0:W]))
        vv2 = opi(VEC, v_, lambda v, W=W: v.scalar_tensor_tensor(
            rowb[0:1, 1, 0:W], rowf[0:1, 0, 0:W], -1.0, rowf[0:1, 2, 0:W],
            op0=ALU.mult, op1=ALU.mult))
        wait(TEN, v_, vv2)
        op(TEN, lambda t, W=W: t.matmul(Yb[:, 0, 0:W], ones1[0:1, :], rowb[0:1, 0, 0:W],
                                        start=True, stop=True))
        bc_pe = opi(TEN, p_, lambda t, W=W: t.matmul(Yb[:, 1, 0:W], ones1[0:1, :], rowb[0:1, 1, 0:W],
                                                     start=True, stop=True))
        wait(VEC, p_, bc_pe)
        op(VEC, lambda v, W=W: v.tensor_copy(rbc[:, 0:W], Yb[:, 0, 0:W]))
        bc_v = opi(VEC, v_ev, lambda v, W=W: v.tensor_copy(abc[:, 0:W], Yb[:, 1, 0:W]))
        vd(VEC)
        for dt in range(DT):
            if not resid:
                op(VEC, lambda v, dt=dt, s=src_tile, d=dst_tile, W=W: v.tensor_mul(
                    d(dt), s(dt), rbc[:, 0:W]))
            else:
                op(VEC, lambda v, dt=dt, s=src_tile, W=W: v.tensor_mul(
                    s(dt), s(dt), rbc[:, 0:W]))
        vd(VEC)
        for dt in range(DT):
            tgt = (lambda v, dt=dt, s=src_tile, d=dst_tile, W=W: v.tensor_add(
                (s(dt) if resid else d(dt)), (s(dt) if resid else d(dt)), abc[:, 0:W]))
            if dt == DT - 1 and not resid:
                opi(VEC, v_, tgt)
            else:
                op(VEC, tgt)
        if resid:
            vd(VEC)
            for dt in range(DT):
                tgt = (lambda v, dt=dt, s=src_tile, d=dst_tile, W=W: v.tensor_add(
                    d(dt), d(dt), s(dt)))
                if dt == DT - 1:
                    opi(VEC, v_, tgt)
                else:
                    op(VEC, tgt)
        return v_.n, bc_v

    ln3_done = [0] * NTB
    tp_h1 = tp_lg = tp_p = tp_at = tp_o = tp_x1 = tp_hid = tp_f = None

    def src_xT(tb):
        return lambda dt: xT[:, dt, tb * TB:(tb + 1) * TB]

    def attn_tb(tb):
        nonlocal tp_h1, tp_lg, tp_p, tp_at, tp_o, tp_x1
        if tb == 0:
            wait(TEN, s_ld, xT_ready)
            wait(VEC, s_ld, xT_ready)
            wait(TEN, v_ev, v_ev.n)
        h1_v, _ = emit_ln(src_xT(tb), lambda dt: h1T[:, dt, :], TB)
        if tb == 0:
            tp_h1 = tap("tap_h1", h1T.ap(), v_, h1_v)
        lg_ev = [0, 0]
        for pair in range(2):
            wait(TEN, v_, h1_v)
            wait(TEN, v_ev, v_ev.n)
            for dti in range(2 * DT):
                dt, part = dti // 2, dti % 2
                srcm = mm_mat if part == 0 else mm_lo
                k = rmm.k
                if k == 0:
                    wait(SYN, msT[0], mmst_done[0])
                    wait(SYN, msT[1], mmst_done[1])
                    wait(SYN, mloT, mmst_done[2])
                if k >= 2:
                    wait(SYN, p_, mm_free[k - 2])
                _, msem, mch = rmm.load(SYN, mmr[:, dti % 2, :],
                                        srcm[dt * 128:(dt + 1) * 128, :])
                wait(TEN, msem, mch)
                for tp in range(2):
                    t128 = pair * 2 + tp
                    for nn in range(2):
                        fn = (lambda t, dti=dti, dt=dt, tp=tp, t128=t128, nn=nn: t.matmul(
                            X2[:, tp, nn * 512:(nn + 1) * 512],
                            h1T[:, dt, t128 * 128:(t128 + 1) * 128],
                            mmr[:, dti % 2, nn * 512:(nn + 1) * 512],
                            start=(dti == 0), stop=False))
                        if tp == 1 and nn == 1:
                            mm_free.append(opi(TEN, p_, fn))
                        else:
                            op(TEN, fn)
            for tp in range(2):
                for nn in range(2):
                    fn = (lambda t, tp=tp, nn=nn: t.matmul(
                        X2[:, tp, nn * 512:(nn + 1) * 512], ones1[0:1, :],
                        blog[0:1, nn * 512:(nn + 1) * 512], start=False, stop=True))
                    if tp == 1 and nn == 1:
                        pe_lg = opi(TEN, p_, fn)
                    else:
                        op(TEN, fn)
            wait(VEC, p_, pe_lg)
            if tb == 0 and pair == 1:
                tap_gate(VEC, tp_lg)
            for tp in range(2):
                fn = (lambda v, tp=tp: v.tensor_copy(logit[:, tp, :], X2[:, tp, :]))
                if tp == 1:
                    lg_ev[pair] = opi(VEC, v_ev, fn)
                else:
                    op(VEC, fn)
            if tb == 0 and pair == 0:
                tp_lg = tap("tap_lg", logit.ap(), v_ev, lg_ev[0])
            if tb == 0 and pair == 1:
                tap_gate(VEC, tp_lg)
                tap_gate(VEC, tp_p)
            vd(VEC)
            for tp in range(2):
                for h in range(H):
                    op(VEC, lambda v, tp=tp, h=h: v.max(out=mx8[:, h, :],
                                                        in_=logit[:, tp, h * L:(h + 1) * L]))
                vd(VEC)
                op(VEC, lambda v: v.tensor_copy(m1r[:, :], mx8[:, :, 0]))
                op(VEC, lambda v: v.tensor_copy(taur[:, :], mx8[:, :, 3]))
                l3 = logit[:, tp, :].rearrange("p (h l) -> p h l", l=L)
                m3 = m1r.ap().rearrange("p (h o) -> p h o", o=1).to_broadcast([128, H, L])
                t3 = taur.ap().rearrange("p (h o) -> p h o", o=1).to_broadcast([128, H, L])
                e3 = ework.ap().rearrange("p (h l) -> p h l", l=L)
                k3 = maskb.ap().rearrange("p (h l) -> p h l", l=L)
                vd(VEC)
                op(VEC, lambda v, l3=l3, t3=t3, k3=k3: v.tensor_tensor(k3, l3, t3, op=ALU.is_ge))
                vv = opi(VEC, v_, lambda v, l3=l3, m3=m3, e3=e3: v.tensor_tensor(
                    e3, l3, m3, op=ALU.subtract))
                wait(SCA, v_, vv)
                cc = opi(SCA, c_, lambda s: s.activation(ework[:, :], ework[:, :], AF.Exp,
                                                         bias=0.0, scale=1.0))
                wait(VEC, c_, cc)
                vd(VEC)
                op(VEC, lambda v: v.tensor_mul(ework[:, :], ework[:, :], maskb[:, :]))
                vd(VEC)
                op(VEC, lambda v, e3=e3: v.reduce_sum(ssum[:, :], e3, axis=AX.X))
                vd(VEC)
                op(VEC, lambda v: v.reciprocal(rsum[:, :], ssum[:, :]))
                vd(VEC)
                r3 = rsum.ap().rearrange("p (h o) -> p h o", o=1).to_broadcast([128, H, L])
                p3 = p_sb[:, tp, :].rearrange("p (h l) -> p h l", l=L)
                v_p.n += 1
                em(VEC, lambda v, e3=e3, r3=r3, p3=p3, h=v_p.h: v.tensor_tensor(
                    p3, e3, r3, op=ALU.mult).then_inc(h, 1))
            wait(TEN, v_ev, lg_ev[pair])
            for tp in range(2):
                t128 = pair * 2 + tp
                wait(TEN, v_p, 4 * tb + t128 + 1)
                for hl in range(8):
                    fn = (lambda t, tp=tp, hl=hl: t.transpose(
                        Xbf[:, tp, hl * 128:(hl + 1) * 128],
                        p_sb[:, tp, hl * 128:(hl + 1) * 128], ident[:, :]))
                    if hl == 7:
                        opi(TEN, p_, fn)
                    else:
                        op(TEN, fn)
                wait(VEC, p_, p_.n)
                opi(VEC, v_ev, lambda v, tp=tp, t128=t128: v.tensor_copy(
                    pT_sb.ap()[:, :, t128 * 128:(t128 + 1) * 128],
                    Xbf[:, tp, :].rearrange("p (a b) -> p a b", a=8)))
            if tb == 0 and pair == 0:
                tp_p = tap("tap_p", p_sb.ap(), v_p, 2)
        pT_done = v_ev.n
        drain_bg(100)
        for dh in range(2):
            if dh == 0:
                wait(TEN, v_ev, pT_done)
            for ci in range(16):
                ridx = rbig.k
                if ridx >= 3:
                    wait(SYN, p_, big_free[ridx - 3])
                if tb == 0 and dh == 0 and ci == 0:
                    wait(SYN, stgT["wv"][0], wvbf_done[0])
                    wait(SYN, stgT["wv"][1], wvbf_done[1])
                _, lsem, lv = rbig.load(SYN, big[:, ridx % 3, 0:1024],
                                        wv_bf[dh * 1024:(dh + 1) * 1024,
                                              ci * 128:(ci + 1) * 128], transpose=True)
                wait(TEN, lsem, lv)
                rhs = h1T[:, ci, :]
                for k in range(8):
                    ps = Xb[:, k, :] if k < 4 else Yb[:, k - 4, :]
                    fn = (lambda t, ps=ps, ridx=ridx, k=k, rhs=rhs, ci=ci: t.matmul(
                        ps, big[:, ridx % 3, k * 128:(k + 1) * 128], rhs,
                        start=(ci == 0), stop=False))
                    if k == 7:
                        big_free.append(opi(TEN, p_, fn))
                    else:
                        op(TEN, fn)
            for k in range(8):
                ps = Xb[:, k, :] if k < 4 else Yb[:, k - 4, :]
                fn = (lambda t, ps=ps, k=k, dh=dh: t.matmul(
                    ps, bv_sb[0:1, dh * 1024 + k * 128:dh * 1024 + (k + 1) * 128],
                    onesN[0:1, :], start=False, stop=True))
                if k == 7:
                    at_pe = opi(TEN, p_, fn)
                else:
                    op(TEN, fn)
            wait(VEC, p_, at_pe)
            if tb == 1 and dh == 0:
                tap_gate(VEC, tp_at)
            for k in range(8):
                ps = Xb[:, k, :] if k < 4 else Yb[:, k - 4, :]
                fn = (lambda v, ps=ps, k=k, dh=dh: v.tensor_copy(
                    attnT[:, dh * 8 + k, :], ps))
                if k == 7:
                    opi(VEC, v_ev, fn)
                else:
                    op(VEC, fn)
            wait(TEN, v_ev, v_ev.n)
        if tb == 0:
            tp_at = tap("tap_at", attnT.ap(), v_ev, v_ev.n)
        drain_bg(40)
        for dh in range(2):
            for ci in range(24):
                ridx = rbig.k
                if ridx >= 3:
                    wait(SYN, p_, big_free[ridx - 3])
                if ci < 16:
                    _, lsem, lv = rbig.load(SYN, big[:, ridx % 3, 0:1024],
                                            wo_bf[dh * 1024:(dh + 1) * 1024,
                                                  ci * 128:(ci + 1) * 128], transpose=True)
                else:
                    if tb == 0 and dh == 0 and ci == 16:
                        wait(SYN, gsT[0], gst_done[0])
                        wait(SYN, gsT[1], gst_done[1])
                    _, lsem, lv = rbig.load(SYN, big[:, ridx % 3, 0:1024],
                                            g_mat[(ci - 16) * 128:(ci - 15) * 128,
                                                  dh * 1024:(dh + 1) * 1024])
                wait(TEN, lsem, lv)
                rhs = attnT[:, ci, :] if ci < 16 else pT_sb[:, ci - 16, :]
                for k in range(8):
                    ps = Xb[:, k, :] if k < 4 else Yb[:, k - 4, :]
                    fn = (lambda t, ps=ps, ridx=ridx, k=k, ci=ci, rhs=rhs: t.matmul(
                        ps, big[:, ridx % 3, k * 128:(k + 1) * 128],
                        rhs, start=(ci == 0), stop=False))
                    if k == 7:
                        big_free.append(opi(TEN, p_, fn))
                    else:
                        op(TEN, fn)
            for k in range(8):
                ps = Xb[:, k, :] if k < 4 else Yb[:, k - 4, :]
                fn = (lambda t, ps=ps, k=k, dh=dh: t.matmul(
                    ps, bo_sb[0:1, dh * 1024 + k * 128:dh * 1024 + (k + 1) * 128],
                    onesN[0:1, :], start=False, stop=True))
                if k == 7:
                    o_pe = opi(TEN, p_, fn)
                else:
                    op(TEN, fn)
            wait(VEC, p_, o_pe)
            if tb == 0 and dh == 0:
                tap_gate(VEC, tp_h1)
            for k in range(8):
                ps = Xb[:, k, :] if k < 4 else Yb[:, k - 4, :]
                fn = (lambda v, ps=ps, k=k, dh=dh: v.tensor_copy(
                    h1T[:, dh * 8 + k, :], ps))
                if k == 7:
                    opi(VEC, v_ev, fn)
                else:
                    op(VEC, fn)
            wait(TEN, v_ev, v_ev.n)
        if tb == 0:
            tp_o = tap("tap_o", h1T.ap(), v_ev, v_ev.n)
            tap_gate(VEC, tp_o)
            tap_gate(VEC, tp_h1)
        drain_bg(40)
        v_ln3, _ = emit_ln(lambda dt: h1T[:, dt, :], src_xT(tb), TB, resid=True)
        ln3_done[tb] = v_ln3
        if tb == 0:
            tp_x1 = tap("tap_x1", xT.ap()[:, :, 0:TB], v_, v_ln3)
        drain_bg(30)

    last_ln4 = [0]

    def ffn_tb(tb):
        nonlocal tp_hid, tp_f
        wait(TEN, v_, ln3_done[tb])
        h2_v, _ = emit_ln(src_xT(tb), lambda dt: h1T[:, dt, :], TB)
        for eg in range(16):
            if eg == 0:
                wait(TEN, v_, h2_v)
            if tb == 0:
                emit_tpass_oct(0, eg)
                emit_tpass_oct(1, eg)
                wait(TEN, c_qt, 16 * (eg + 1))
            for mat in range(2):
                srcq = up_qT if mat == 0 else gt_qT
                need = q_need_up[eg] if mat == 0 else q_need_gt[eg]
                if mat == 0 and eg > 0:
                    wait(TEN, c_si, c_si.n)
                if mat == 1 and eg > 0:
                    wait(TEN, v_hid, v_hid.n)
                for d_in in range(DT):
                    ridx = rsml.k
                    if ridx == 0:
                        wait(SYN, v_, ln3_done[3])
                    if ridx >= 4:
                        wait(SYN, p_, sml_free[ridx - 4])
                    if d_in == 0 and tb == 0:
                        wait(SYN, qprogT, need)
                    _, lsem, lv = rsml.load(SYN, sml[:, ridx % 4, :],
                                            srcq[d_in * 128:(d_in + 1) * 128,
                                                 eg * 512:(eg + 1) * 512])
                    wait(TEN, lsem, lv)
                    for et in range(4):
                        ps = Xb[:, et, :] if mat == 0 else Yb[:, et, :]
                        fn = (lambda t, ps=ps, ridx=ridx, et=et, d_in=d_in: t.matmul(
                            ps, sml[:, ridx % 4, et * 128:(et + 1) * 128],
                            h1T[:, d_in, :], start=(d_in == 0), stop=(d_in == DT - 1)))
                        if et == 3:
                            sml_free.append(opi(TEN, p_, fn))
                        else:
                            op(TEN, fn)
                if mat == 0:
                    up_pe = sml_free[-1]
                else:
                    gt_pe = sml_free[-1]
            wait(SCA, p_, up_pe)
            if eg > 0:
                wait(SCA, v_hid, v_hid.n)
            if tb == 0 and eg == 0:
                wait(SCA, v_ev, th_ready0[0])
            for et in range(4):
                fn = (lambda s, et=et: s.activation(silu_sb[:, et, :], Xb[:, et, :], AF.Silu,
                                                    bias=0.0, scale=thc[:, 6:7]))
                if et == 3:
                    c_si.n += 1
                    em(SCA, lambda s, f=fn, h=c_si.h: f(s).then_inc(h, 1))
                else:
                    op(SCA, fn)
            wait(VEC, p_, gt_pe)
            wait(VEC, c_si, c_si.n)
            if tb == 1 and eg == 0:
                tap_gate(VEC, tp_hid)
            for et in range(4):
                fn = (lambda v, et=et, eg=eg: v.tensor_mul(
                    hid[:, eg * 4 + et, :], silu_sb[:, et, :], Yb[:, et, :]))
                if et == 3:
                    v_hid.n += 1
                    em(VEC, lambda v, f=fn, h=v_hid.h: f(v).then_inc(h, 1))
                else:
                    op(VEC, fn)
            drain_bg(24)
        if tb == 0:
            tp_hid = tap("tap_hid", hid.ap(), v_hid, v_hid.n)
            drain_bg(len(bg))
        if tb == 0:
            for oc in range(16):
                emit_tpass_oct(2, oc)
        wait(TEN, v_hid, v_hid.n)
        wait(TEN, c_si, c_si.n)
        if tb == 0:
            wait(TEN, c_qt, 384)
        for dh in range(2):
            for et in range(ET):
                ridx = rbig.k
                if ridx >= 3:
                    wait(SYN, p_, big_free[ridx - 3])
                if tb == 0 and dh == 0 and et % 8 == 0:
                    wait(SYN, qprogT, q_need_dn[et // 8])
                _, lsem, lv = rbig.load(SYN, big[:, ridx % 3, 0:1024],
                                        dn_qT[et * 128:(et + 1) * 128,
                                              dh * 1024:(dh + 1) * 1024])
                wait(TEN, lsem, lv)
                for k in range(8):
                    ps = Xb[:, k, :] if k < 4 else Yb[:, k - 4, :]
                    fn = (lambda t, ps=ps, ridx=ridx, k=k, et=et: t.matmul(
                        ps, big[:, ridx % 3, k * 128:(k + 1) * 128],
                        hid[:, et, :], start=(et == 0), stop=(et == ET - 1)))
                    if k == 7:
                        big_free.append(opi(TEN, p_, fn))
                    else:
                        op(TEN, fn)
            d_pe = big_free[-1]
            wait(VEC, p_, d_pe)
            for k in range(8):
                ps = Xb[:, k, :] if k < 4 else Yb[:, k - 4, :]
                fn = (lambda v, ps=ps, k=k, dh=dh: v.tensor_copy(h1T[:, dh * 8 + k, :], ps))
                if k == 7:
                    opi(VEC, v_ev, fn)
                else:
                    op(VEC, fn)
            wait(TEN, v_ev, v_ev.n)
        if tb == 0:
            tp_f = tap("tap_f", h1T.ap(), v_ev, v_ev.n)
            tap_gate(VEC, tp_f)
            tap_gate(VEC, tp_x1)
        lv4, _ = emit_ln(lambda dt: h1T[:, dt, :], src_xT(tb), TB, resid=True)
        last_ln4[0] = lv4
        drain_bg(20)

    th_ready0 = [0]
    attn_tb(0)
    attn_tb(1)
    th0 = emit_th(0)
    th_ready0[0] = th0
    attn_tb(2)
    th1 = emit_th(1)
    tiles01 = []
    for oct in range(16):
        tiles01 += [(0, oct * 8 + i) for i in range(8)]
        tiles01 += [(1, oct * 8 + i) for i in range(8)]
    emit_pass2(tiles01, {0: th0, 1: th1})
    attn_tb(3)
    th2 = emit_th(2)
    emit_pass2([(2, j) for j in range(128)], {2: th2})
    ffn_tb(0)
    ffn_tb(1)
    ffn_tb(2)
    ffn_tb(3)

    wait(TEN, v_, last_ln4[0])
    for t128 in range(16):
        wait(TEN, v_ev, v_ev.n)
        for dt in range(DT):
            bank = dt // 8
            fn = (lambda t, dt=dt, t128=t128, bank=bank: t.transpose(
                Xbf[:, bank, (dt % 8) * 128:(dt % 8 + 1) * 128],
                xT[:, dt, t128 * 128:(t128 + 1) * 128], ident[:, :]))
            if dt == DT - 1:
                opi(TEN, p_, fn)
            else:
                op(TEN, fn)
        wait(VEC, p_, p_.n)
        if t128 >= 1:
            wait(VEC, o_st, 16 * t128)
        ev = opi(VEC, v_ev, lambda v: v.tensor_copy(
            ostg[:, :], Xbf[:, 0:2, :].rearrange("p a b -> p (a b)")))
        wait(GPS, v_ev, ev)
        dma(GPS, out[t128 * 128:(t128 + 1) * 128, :], ostg[:, :], o_st)
    wait(GPS, o_st, 16 * 16)
    wait(SYN, o_st, 16 * 16)

    with nc.Block() as block:
        @block.tensor
        def _(t):
            for fn in sched[TEN]:
                fn(t)

        @block.vector
        def _(v):
            for fn in sched[VEC]:
                fn(v)

        @block.scalar
        def _(s):
            for fn in sched[SCA]:
                fn(s)

        @block.gpsimd
        def _(g):
            for fn in sched[GPS]:
                fn(g)

        @block.sync
        def _(s):
            for fn in sched[SYN]:
                fn(s)

    return nc


_NC_CACHE = {}


def kernel(L0_attn_hidden_states, L0_attn_Wq, L0_attn_bq, L0_attn_Wk, L0_attn_bk,
           L0_attn_Wv, L0_attn_bv, L0_attn_Wo, L0_attn_bo, L0_attn_latent_keys,
           L0_ffn_W_up, L0_ffn_W_gate, L0_ffn_W_down):
    x = np.ascontiguousarray(np.asarray(L0_attn_hidden_states, dtype=np.float32)).reshape(16384, 2048)
    w = {
        "wq": np.ascontiguousarray(np.asarray(L0_attn_Wq, dtype=np.float32)),
        "bq": np.ascontiguousarray(np.asarray(L0_attn_bq, dtype=np.float32)),
        "wv": np.ascontiguousarray(np.asarray(L0_attn_Wv, dtype=np.float32)),
        "bv": np.ascontiguousarray(np.asarray(L0_attn_bv, dtype=np.float32)),
        "wo": np.ascontiguousarray(np.asarray(L0_attn_Wo, dtype=np.float32)),
        "bo": np.ascontiguousarray(np.asarray(L0_attn_bo, dtype=np.float32)),
        "lat": np.ascontiguousarray(np.asarray(L0_attn_latent_keys, dtype=np.float32)),
        "wup": np.ascontiguousarray(np.asarray(L0_ffn_W_up, dtype=np.float32)),
        "wgt": np.ascontiguousarray(np.asarray(L0_ffn_W_gate, dtype=np.float32)),
        "wdn": np.ascontiguousarray(np.asarray(L0_ffn_W_down, dtype=np.float32)),
    }
    if "nc" not in _NC_CACHE:
        _NC_CACHE["nc"] = build_program()
    nc = _NC_CACHE["nc"]
    in_maps = [dict(w, x=x[c * TOK:(c + 1) * TOK]) for c in range(8)]
    res = run_bass_kernel_spmd(nc, in_maps, core_ids=list(range(8)))
    outs = [np.asarray(r["out"]) for r in res.results]
    return np.concatenate(outs, axis=0).reshape(2, 8192, 2048).astype(np.float32)


# version 12
# speedup vs baseline: 1.2019x; 1.1997x over previous
import numpy as np
import ml_dtypes
import concourse.bass as bass
import concourse.mybir as mybir
from concourse.bass_utils import run_bass_kernel_spmd

F32 = mybir.dt.float32
BF16 = mybir.dt.bfloat16
AF = mybir.ActivationFunctionType
ALU = mybir.AluOpType
AX = mybir.AxisListType

TOK = 2048
D = 2048
DT = 16
TB = 512
NTB = 4
H = 16
L = 64
HL = 1024
E = 8192
ET = 64
LN_EPS = 1e-5
Q_EPS = 1e-8
ISQ = float(1.0 / np.sqrt(128.0))
NW = float(E * D)

TEN, VEC, SCA, GPS, SYN = "tensor", "vector", "scalar", "gpsimd", "sync"


class Sem:
    __slots__ = ("h", "n")

    def __init__(self, nc, name):
        self.h = nc.alloc_semaphore(name)
        self.n = 0


def build_program(debug=False):
    nc = bass.Bass("TRN2", target_bir_lowering=False)

    x_in = nc.dram_tensor("x", [TOK, D], F32, kind="ExternalInput").ap()
    wq = nc.dram_tensor("wq", [D, D], F32, kind="ExternalInput").ap()
    bq = nc.dram_tensor("bq", [D], F32, kind="ExternalInput").ap()
    wv = nc.dram_tensor("wv", [D, D], F32, kind="ExternalInput").ap()
    bv = nc.dram_tensor("bv", [D], F32, kind="ExternalInput").ap()
    wo = nc.dram_tensor("wo", [D, D], F32, kind="ExternalInput").ap()
    bo = nc.dram_tensor("bo", [D], F32, kind="ExternalInput").ap()
    lat = nc.dram_tensor("lat", [L, D], F32, kind="ExternalInput").ap()
    wup = nc.dram_tensor("wup", [E, D], F32, kind="ExternalInput").ap()
    wgt = nc.dram_tensor("wgt", [E, D], F32, kind="ExternalInput").ap()
    wdn = nc.dram_tensor("wdn", [D, E], F32, kind="ExternalInput").ap()
    out = nc.dram_tensor("out", [TOK, D], F32, kind="ExternalOutput").ap()

    x_bf = nc.dram_tensor("x_bf", [TOK, D], BF16, kind="Internal").ap()
    wv_bf = nc.dram_tensor("wv_bf", [D, D], BF16, kind="Internal").ap()
    wo_bf = nc.dram_tensor("wo_bf", [D, D], BF16, kind="Internal").ap()
    g_mat = nc.dram_tensor("g_mat", [HL, D], BF16, kind="Internal").ap()
    mm_mat = nc.dram_tensor("mm_mat", [D, HL], BF16, kind="Internal").ap()
    mm_lo = nc.dram_tensor("mm_lo", [D, HL], BF16, kind="Internal").ap()
    up_q = nc.dram_tensor("up_q", [E, D], BF16, kind="Internal").ap()
    gt_q = nc.dram_tensor("gt_q", [E, D], BF16, kind="Internal").ap()
    dn_q = nc.dram_tensor("dn_q", [D, E], BF16, kind="Internal").ap()
    up_qT = nc.dram_tensor("up_qT", [D, E], BF16, kind="Internal").ap()
    gt_qT = nc.dram_tensor("gt_qT", [D, E], BF16, kind="Internal").ap()
    dn_qT = nc.dram_tensor("dn_qT", [E, D], BF16, kind="Internal").ap()
    s_scr = [nc.dram_tensor(f"s_scr{i}", [n], F32, kind="Internal").ap()
             for i, n in ((0, E), (1, E), (2, D))]
    ident_dram = nc.inline_tensor(np.eye(128, dtype=ml_dtypes.bfloat16), name="ident_c")

    def S(name, shape, dtype):
        return nc.alloc_sbuf_tensor(name, shape, dtype)

    xT = S("xT", [128, DT, TOK], BF16)
    h1T = S("h1T", [128, DT, TB], BF16)
    sqr = S("sqr", [128, 2, TB], BF16)
    rowf = S("rowf", [1, 4, TB], F32)
    rowb = S("rowb", [1, 2, TB], BF16)
    rbc = S("rbc", [128, TB], BF16)
    abc = S("abc", [128, TB], BF16)
    p1r = S("p1r", [128, 2, 1032], F32)
    ctile = S("ctile", [128, 1024], BF16)
    qout = S("qout", [128, 2, 1024], BF16)
    tstg = S("tstg", [128, 1024], BF16)
    tstg2 = S("tstg2", [128, 1024], BF16)
    big = S("big", [128, 3, 1024], BF16)
    ident = S("ident", [128, 128], BF16)
    latT = S("latT", [128, H, L], BF16)
    bqT = S("bqT", [128, 16], BF16)
    bq_sb = S("bq_sb", [16, 128], BF16)
    blog = S("blog", [1, HL], BF16)
    bv_sb = S("bv_sb", [1, D], BF16)
    bo_sb = S("bo_sb", [1, D], BF16)
    ones1 = S("ones1", [1, 128], BF16)
    ones1f = S("ones1f", [1, 128], F32)
    onescb = S("onescb", [128, 1], BF16)
    onesN = S("onesN", [1, TB], BF16)
    rowacc = [S("rowacc_up", [128, 64], F32), S("rowacc_gt", [128, 64], F32),
              S("rowacc_dn", [128, 16], F32)]
    line = S("line", [1, 2, 72], F32)
    thc = S("thc", [128, 8], F32)
    nthc = S("nthc", [128, 8], F32)
    sc1 = S("sc1", [1, 16], F32)
    cpool = S("cpool", [1, 16], F32)
    mx8 = S("mx8", [128, H, 8], F32)
    m1r = S("m1r", [128, H], F32)
    taur = S("taur", [128, H], F32)
    ssum = S("ssum", [128, H], F32)
    rsum = S("rsum", [128, H], F32)

    AR_B = 72 * 1024
    arena = nc.alloc_sbuf_tensor("arena", [128, AR_B], mybir.dt.uint8)
    ar0 = nc.lookup_mloc(arena).addr

    def A(name, shape, dtype, off):
        return nc.alloc_sbuf_tensor_at(name, shape, dtype, offset=ar0 + off)

    hid = A("hid", [128, ET, TB], BF16, 0)
    silu_sb = A("silu_sb", [128, 4, TB], BF16, 64 * 1024)
    logit = A("logit", [128, 2, HL], F32, 0)
    ework = A("ework", [128, HL], F32, 8 * 1024)
    maskb = A("maskb", [128, HL], BF16, 12 * 1024)
    p_sb = A("p_sb", [128, 2, HL], BF16, 14 * 1024)
    pT_sb = A("pT_sb", [128, 8, TB], BF16, 18 * 1024)
    attnT = A("attnT", [128, DT, TB], BF16, 26 * 1024)
    stg = A("stg", [128, 2, D], BF16, 42 * 1024)
    gstg = A("gstg", [64, 2, D], BF16, 50 * 1024)
    mstg = A("mstg", [128, 2, 1024], BF16, 58 * 1024)
    mlo_st = A("mlo_st", [128, 1024], BF16, 62 * 1024)
    pf32 = A("pf32", [128, 1024], F32, 50 * 1024)
    lat_sb = A("lat_sb", [64, D], BF16, 26 * 1024)
    mmr = A("mmr", [128, 2, HL], BF16, 58 * 1024)
    sml = A("sml", [128, 4, TB], BF16, 68 * 1024)
    ostg = A("ostg", [128, D], F32, 50 * 1024)


    taps = {}
    if debug:
        for nm, shape, dtp in (("tap_h1", [128, DT, TB], BF16), ("tap_lg", [128, 2, HL], F32),
                               ("tap_p", [128, 2, HL], BF16), ("tap_at", [128, DT, TB], BF16),
                               ("tap_o", [128, DT, TB], BF16), ("tap_x1", [128, DT, TB], BF16),
                               ("tap_hid", [128, ET, TB], BF16), ("tap_f", [128, DT, TB], BF16)):
            taps[nm] = nc.dram_tensor(nm, shape, dtp, kind="Internal").ap()

    psX = nc.alloc_psum_tensor("psX", [128, 4, 512], F32)
    psY = nc.alloc_psum_tensor("psY", [128, 4, 512], F32)
    Xb = psX.ap()
    Yb = psY.ap()
    Xflat = Xb.rearrange("p a b -> p (a b)")
    Yflat = Yb.rearrange("p a b -> p (a b)")
    X2 = Xflat.rearrange("p (a b) -> p a b", a=2)
    X8 = Xb.rearrange("p a (b c) -> p (a b) c", b=2)
    Y8 = Yb.rearrange("p a (b c) -> p (a b) c", b=2)
    Xbf = Xb.bitcast(BF16)
    Ybf = Yb.bitcast(BF16)

    sched = {k: [] for k in (TEN, VEC, SCA, GPS, SYN)}
    sems = {}

    def sem(name):
        if name not in sems:
            sems[name] = Sem(nc, name)
        return sems[name]

    def em(eng, fn):
        sched[eng].append(fn)

    def wait(eng, s, n):
        if n > 0:
            em(eng, lambda e, h=s.h, v=n: e.wait_ge(h, v))

    def dma(eng, out_ap, in_ap, s, transpose=False):
        s.n += 16
        if transpose:
            em(eng, lambda e, o=out_ap, i=in_ap, h=s.h: e.dma_start(o, i, transpose=True).then_inc(h, 16))
        else:
            em(eng, lambda e, o=out_ap, i=in_ap, h=s.h: e.dma_start(o, i).then_inc(h, 16))
        return s.n

    def op(eng, fn):
        em(eng, fn)

    def opi(eng, s, fn):
        s.n += 1
        em(eng, lambda e, f=fn, h=s.h: f(e).then_inc(h, 1))
        return s.n

    def vd(eng):
        em(eng, lambda e: e.drain())

    def nopi(eng, s):
        s.n += 1
        em(eng, lambda e, h=s.h: e.nop().then_inc(h, 1))
        return s.n

    g_ld = sem("g_ld")
    g_st = sem("g_st")
    s_ld = sem("s_ld")
    q_st = sem("q_st")
    o_st = sem("o_st")
    v_ = sem("v_")
    p_ = sem("p_")
    c_ = sem("c_")
    v_p1 = sem("v_p1")
    v_q = sem("v_q")
    th_dm = sem("th_dm")
    th_ln = [sem("th_ln0"), sem("th_ln1")]
    v_sq = sem("v_sq")
    v_ev = sem("v_ev")
    v_p = sem("v_p")
    fbig = sem("fbig")
    fsml = sem("fsml")
    fmm = sem("fmm")
    fq2 = sem("fq2")
    c_si = sem("c_si")
    v_hid = sem("v_hid")
    g_c = sem("g_c")

    class RingLd:

        def __init__(self, name, nslots):
            self.sems = [sem(f"{name}_s{i}") for i in range(nslots)]
            self.nslots = nslots
            self.k = 0

        def load(self, eng, out_ap, in_ap, transpose=False):
            slot = self.k % self.nslots
            s = self.sems[slot]
            self.k += 1
            v = dma(eng, out_ap, in_ap, s, transpose=transpose)
            return slot, s, v

    rbig = RingLd("rbig", 3)
    rsml = RingLd("rsml", 4)
    rmm = RingLd("rmm", 2)
    big_free = []
    sml_free = []
    mm_free = []
    rp1 = RingLd("rp1", 2)
    rstg = RingLd("rstg", 2)
    stgT = {name: [sem(f"stgT_{name}0"), sem(f"stgT_{name}1")]
            for name in ("x", "wv", "wo")}
    msT = [sem("msT0"), sem("msT1")]
    mloT = sem("mloT")
    gsT = [sem("gsT0"), sem("gsT1")]
    g_sm = sem("g_sm")
    qoS = [sem("qoS0"), sem("qoS1")]
    qtS = [sem("qtS0"), sem("qtS1")]
    qprogT = sem("qprogT")
    c_qt = sem("c_qt")
    qprog = sem("qprog")
    g_tp = sem("g_tp")

    CONSTS = [1.0 / D, LN_EPS, Q_EPS, 0.5, -0.5, 1.0 / NW, ISQ, -1.0, float(2.0 ** -30), -128.0]
    for i, cv in enumerate(CONSTS):
        op(GPS, lambda g, i=i, cv=cv: g.memset(cpool[0:1, i:i + 1], float(cv)))
        nc.const_aps.register(F32, float(cv), cpool[0:1, i:i + 1])
    op(GPS, lambda g: g.memset(ones1[:, :], 1.0))
    op(GPS, lambda g: g.memset(ones1f[:, :], 1.0))
    op(GPS, lambda g: g.memset(onescb[:, :], 1.0))
    op(GPS, lambda g: g.memset(onesN[:, :], 1.0))
    opi(GPS, g_c, lambda g: g.memset(sc1[0:1, :], 0.0))
    for eng in (TEN, VEC, SCA, SYN):
        wait(eng, g_c, g_c.n)

    dma(GPS, ident[:, :], ident_dram.ap(), g_sm)
    slot_store = [None, None]
    for name, dst, srcd in (("x", x_bf, x_in), ("wv", wv_bf, wv), ("wo", wo_bf, wo)):
        for i in range(16):
            slotp = (rstg.k) % 2
            if slot_store[slotp] is not None:
                wait(GPS, slot_store[slotp][0], slot_store[slotp][1])
            slot, lsem, lv = rstg.load(GPS, stg[:, slotp, :], srcd[i * 128:(i + 1) * 128, :])
            wait(GPS, lsem, lv)
            sv = dma(GPS, dst[i * 128:(i + 1) * 128, :], stg[:, slotp, :], stgT[name][slotp])
            slot_store[slotp] = (stgT[name][slotp], sv)
    xbf_done = (stgT["x"][0].n, stgT["x"][1].n)
    wvbf_done = (stgT["wv"][0].n, stgT["wv"][1].n)
    wobf_done = (stgT["wo"][0].n, stgT["wo"][1].n)

    dma(GPS, lat_sb[:, :], lat, g_sm)
    dma(GPS, bq_sb[:, :], bq.rearrange("(a b) -> a b", a=16), g_sm)
    dma(GPS, bv_sb[:, :], bv.rearrange("(a b) -> a b", a=1), g_sm)
    small_done = dma(GPS, bo_sb[:, :], bo.rearrange("(a b) -> a b", a=1), g_sm)

    wait(SYN, stgT["x"][0], xbf_done[0])
    wait(SYN, stgT["x"][1], xbf_done[1])
    xT_ready = 0
    for dt in range(DT):
        xT_ready = dma(SYN, xT[:, dt, :], x_bf[0:TOK, dt * 128:(dt + 1) * 128], s_ld, transpose=True)

    wait(TEN, g_sm, small_done)
    for grp in range(4):
        if grp > 0:
            wait(TEN, v_ev, v_ev.n)
        for k in range(4):
            h = grp * 4 + k
            fn = (lambda t, h=h, k=k: t.transpose(
                Ybf[:, 0, k * 64:(k + 1) * 64], lat_sb[:, h * 128:(h + 1) * 128],
                ident[0:64, 0:64]))
            if k == 3:
                opi(TEN, p_, fn)
            else:
                op(TEN, fn)
        wait(VEC, p_, p_.n)
        opi(VEC, v_ev, lambda v, grp=grp: v.tensor_copy(
            latT.ap().rearrange("p a b -> p (a b)")[:, grp * 256:(grp + 1) * 256],
            Ybf[:, 0, 0:256]))
    wait(TEN, v_ev, v_ev.n)
    opi(TEN, p_, lambda t: t.transpose(Ybf[:, 0, 0:16], bq_sb[:, :], ident[0:16, 0:16]))
    wait(VEC, p_, p_.n)
    latT_v = opi(VEC, v_ev, lambda v: v.tensor_copy(bqT[:, :], Ybf[:, 0, 0:16]))

    mm_pe = [0] * H
    mm_ev = [0] * H
    mm_store = [0] * H
    mlo_store = 0
    for h in range(H):
        slotp = rstg.k % 2
        if slot_store[slotp] is not None:
            wait(GPS, slot_store[slotp][0], slot_store[slotp][1])
            slot_store[slotp] = None
        if h >= 2:
            wait(GPS, p_, mm_pe[h - 2])
        slot, lsem, wq_ld = rstg.load(GPS, stg[:, slotp, :], wq[h * 128:(h + 1) * 128, :])
        wait(TEN, lsem, wq_ld)
        if h == 0:
            wait(TEN, v_ev, latT_v)
        if h >= 2:
            wait(TEN, v_ev, mm_ev[h - 2])
        for dt in range(DT):
            op(TEN, lambda t, h=h, dt=dt, slot=slot: t.matmul(
                X2[:, h % 2, dt * 64:(dt + 1) * 64], stg[:, slot, dt * 128:(dt + 1) * 128],
                latT[:, h, :], start=True, stop=True))
        bank = 1 if h < 8 else 2
        col = (h % 8) * 64
        mm_pe[h] = opi(TEN, p_, lambda t, h=h, bank=bank, col=col: t.matmul(
            Yb[0:1, bank, col:col + 64], bqT[:, h:h + 1], latT[:, h, :], start=True, stop=True))
        wait(VEC, p_, mm_pe[h])
        if h >= 2:
            wait(VEC, msT[h % 2], mm_store[h - 2])
        if h >= 1:
            wait(VEC, mloT, mlo_store)
        op(VEC, lambda v, h=h: v.tensor_scalar_mul(pf32[:, :], X2[:, h % 2, :], ISQ))
        vd(VEC)
        op(VEC, lambda v, h=h: v.tensor_copy(mstg[:, h % 2, :], pf32[:, :]))
        vd(VEC)
        mm_ev[h] = opi(VEC, v_ev, lambda v, h=h: v.tensor_sub(
            mlo_st[:, :], pf32[:, :], mstg[:, h % 2, :]))
        wait(GPS, v_ev, mm_ev[h])
        mm_store[h] = dma(GPS, mm_mat.rearrange("(a p) q -> p a q", p=128)[:, :, h * 64:(h + 1) * 64],
                          mstg[:, h % 2, :].rearrange("p (a b) -> p a b", b=64), msT[h % 2])
        mlo_store = dma(GPS, mm_lo.rearrange("(a p) q -> p a q", p=128)[:, :, h * 64:(h + 1) * 64],
                        mlo_st[:, :].rearrange("p (a b) -> p a b", b=64), mloT)
    mmst_done = (msT[0].n, msT[1].n, mloT.n)
    wait(VEC, p_, p_.n)
    op(VEC, lambda v: v.tensor_scalar_mul(blog[0:1, 0:512], Yb[0:1, 1, :], ISQ))
    blog_v = opi(VEC, v_ev, lambda v: v.tensor_scalar_mul(blog[0:1, 512:1024], Yb[0:1, 2, :], ISQ))

    g_pe = [0] * H
    g_ev = [0] * H
    g_store = [0] * H
    for h in range(H):
        if h == 0:
            wait(SYN, stgT["wo"][0], wobf_done[0])
            wait(SYN, stgT["wo"][1], wobf_done[1])
            wait(TEN, v_ev, blog_v)
        par = h % 2
        gps_flat = Xflat if par == 0 else Yflat
        if h >= 2:
            wait(TEN, v_ev, g_ev[h - 2])
        for half in range(2):
            ridx = rbig.k
            if ridx >= 3:
                wait(SYN, p_, big_free[ridx - 3])
            _, wsem, wch = rbig.load(SYN, big[:, ridx % 3, :],
                                     wo_bf[half * 1024:(half + 1) * 1024,
                                           h * 128:(h + 1) * 128], transpose=True)
            wait(TEN, wsem, wch)
            for n in range(2):
                fn = (lambda t, h=h, n=n, half=half, gf=gps_flat, ridx=ridx: t.matmul(
                    gf[0:64, (half * 2 + n) * 512:(half * 2 + n + 1) * 512],
                    latT[:, h, :], big[:, ridx % 3, n * 512:(n + 1) * 512],
                    start=True, stop=True))
                if n == 1:
                    big_free.append(opi(TEN, p_, fn))
                else:
                    op(TEN, fn)
        g_pe[h] = big_free[-1]
        wait(VEC, p_, g_pe[h])
        if h >= 2:
            wait(VEC, gsT[h % 2], g_store[h - 2])
        g_ev[h] = opi(VEC, v_ev, lambda v, par=par, gf=gps_flat: v.tensor_copy(
            gstg[:, par, :], gf[0:64, :]))
        wait(SYN, v_ev, g_ev[h])
        g_store[h] = dma(SYN, g_mat[h * 64:(h + 1) * 64, :], gstg[:, par, :], gsT[h % 2])
    gst_done = (gsT[0].n, gsT[1].n)

    bg = []
    bg_pos = [0]

    def drain_bg(k):
        e = min(bg_pos[0] + k, len(bg))
        for i in range(bg_pos[0], e):
            em(VEC, bg[i])
        bg_pos[0] = e

    def tap(name, src_ap, wsem, wval):
        if not debug:
            return None
        ts_ = sem("tps_" + name)
        wait(SYN, wsem, wval)
        dma(SYN, taps[name], src_ap, ts_)
        return ts_

    def tap_gate(eng, tok):
        if debug and tok is not None:
            wait(eng, tok, 16)

    def mat_tile_ap(m, j):
        if m == 0:
            return wup[(j // 2) * 128:(j // 2 + 1) * 128, (j % 2) * 1024:(j % 2 + 1) * 1024]
        if m == 1:
            return wgt[(j // 2) * 128:(j // 2 + 1) * 128, (j % 2) * 1024:(j % 2 + 1) * 1024]
        cc, rr = j // 16, j % 16
        return wdn[rr * 128:(rr + 1) * 128, cc * 1024:(cc + 1) * 1024]

    def mat_tile_ap_p1(m, j):
        if m < 2:
            return mat_tile_ap(m, j)
        rr, cc = j // 8, j % 8
        return wdn[rr * 128:(rr + 1) * 128, cc * 1024:(cc + 1) * 1024]

    v_p1_fold = [0, 0, 0]
    p1_done = []
    for m in range(3):
        ncc = 2 if m < 2 else 8
        for j in range(128):
            idx = m * 128 + j
            if idx >= 2:
                wait(GPS, v_p1, p1_done[idx - 2])
            slot, lsem, ldn = rp1.load(GPS, p1r[:, idx % 2, 1:1025], mat_tile_ap_p1(m, j))
            v_p1.n += 1
            p1_done.append(v_p1.n)
            cc, rb = j % ncc, j // ncc
            nxt = (idx + 1) % 2

            def p1job(v, ldn=ldn, slot=idx % 2, nxt=nxt, cc=cc, rb=rb, m=m, ncc=ncc,
                      h1=v_p1.h, lh=lsem.h):
                v.wait_ge(lh, ldn)
                lo = 1 if cc == 0 else 0
                dst = rowacc[m][:, rb:rb + 1] if cc == ncc - 1 else p1r[:, nxt, 0:1]
                r = v.reduce_sum(dst, p1r[:, slot, lo:1025],
                                 axis=AX.X, apply_absolute_value=True).then_inc(h1, 1)
                v.drain()
                return r
            bg.append(p1job)
        v_p1_fold[m] = v_p1.n

    def emit_th(m):
        nblk = 64 if m < 2 else 16
        racc = rowacc[m]
        scr = s_scr[m]
        wait(SYN, v_p1, v_p1_fold[m])
        for rb in range(nblk):
            dma(SYN, scr[rb * 128:(rb + 1) * 128], racc[:, rb:rb + 1], th_dm)
        wait(SYN, th_dm, th_dm.n)
        nch = nblk * 128 // 64
        red_done = []
        for c in range(nch):
            slot = c % 2
            if c >= 2:
                wait(SYN, v_p1, red_done[c - 2])
            ldv = dma(SYN, line[0:1, slot, 1:65],
                      scr[c * 64:(c + 1) * 64].rearrange("(q n) -> q n", q=1), th_ln[slot])
            wait(VEC, th_ln[slot], ldv)
            lo = 1 if c == 0 else 0
            dst = sc1[0:1, 0:1] if c == nch - 1 else line[0:1, (c + 1) % 2, 0:1]
            red_done.append(opi(VEC, v_p1, lambda v, dst=dst, slot=slot, lo=lo:
                                v.reduce_sum(dst, line[0:1, slot, lo:65], axis=AX.X)))
            vd(VEC)
        op(VEC, lambda v: v.tensor_scalar(sc1[0:1, 4:5], sc1[0:1, 0:1],
                                          float(1.0 / NW), float(Q_EPS),
                                          op0=ALU.mult, op1=ALU.add))
        vd(VEC)
        op(VEC, lambda v: v.tensor_scalar_mul(sc1[0:1, 5:6], sc1[0:1, 4:5], 0.5))
        op(VEC, lambda v: v.tensor_scalar_mul(sc1[0:1, 6:7], sc1[0:1, 4:5], -0.5))
        vd(VEC)
        op(VEC, lambda v: v.tensor_scalar_mul(sc1[0:1, 8:9], sc1[0:1, 5:6], float(2.0 ** -30)))
        vd(VEC)
        op(VEC, lambda v: v.tensor_add(sc1[0:1, 9:10], sc1[0:1, 5:6], sc1[0:1, 8:9]))
        vd(VEC)
        op(VEC, lambda v: v.tensor_sub(sc1[0:1, 10:11], sc1[0:1, 9:10], sc1[0:1, 5:6]))
        vd(VEC)
        op(VEC, lambda v: v.tensor_sub(sc1[0:1, 10:11], sc1[0:1, 10:11], sc1[0:1, 8:9]))
        vd(VEC)
        op(VEC, lambda v: v.tensor_scalar_mul(sc1[0:1, 10:11], sc1[0:1, 10:11], -128.0))
        vd(VEC)
        op(VEC, lambda v: v.tensor_sub(sc1[0:1, 11:12], sc1[0:1, 5:6], sc1[0:1, 10:11]))
        vrow1 = opi(VEC, v_, lambda v: v.tensor_add(sc1[0:1, 12:13], sc1[0:1, 5:6], sc1[0:1, 10:11]))
        vd(VEC)
        op(VEC, lambda v: v.tensor_scalar_mul(sc1[0:1, 13:14], sc1[0:1, 11:12], -1.0))
        vrow = opi(VEC, v_, lambda v: v.tensor_scalar_mul(sc1[0:1, 14:15], sc1[0:1, 12:13], -1.0))
        wait(TEN, v_, vrow)
        op(TEN, lambda t: t.matmul(Yb[:, 3, 0:1], ones1f[0:1, :], sc1[0:1, 11:12],
                                   start=True, stop=True))
        op(TEN, lambda t: t.matmul(Yb[:, 3, 1:2], ones1f[0:1, :], sc1[0:1, 12:13],
                                   start=True, stop=True))
        op(TEN, lambda t: t.matmul(Yb[:, 3, 2:3], ones1f[0:1, :], sc1[0:1, 13:14],
                                   start=True, stop=True))
        if m == 0:
            op(TEN, lambda t: t.matmul(Yb[:, 3, 3:4], ones1f[0:1, :], sc1[0:1, 14:15],
                                       start=True, stop=True))
            pe = opi(TEN, p_, lambda t: t.matmul(Yb[:, 3, 4:5], ones1f[0:1, :], sc1[0:1, 6:7],
                                                 start=True, stop=True))
        else:
            pe = opi(TEN, p_, lambda t: t.matmul(Yb[:, 3, 3:4], ones1f[0:1, :], sc1[0:1, 14:15],
                                                 start=True, stop=True))
        wait(VEC, p_, pe)
        op(VEC, lambda v, m=m: v.tensor_copy(thc[:, m:m + 1], Yb[:, 3, 0:1]))
        op(VEC, lambda v, m=m: v.tensor_copy(thc[:, 3 + m:4 + m], Yb[:, 3, 1:2]))
        op(VEC, lambda v, m=m: v.tensor_copy(nthc[:, m:m + 1], Yb[:, 3, 2:3]))
        ev = opi(VEC, v_ev, lambda v, m=m: v.tensor_copy(nthc[:, 3 + m:4 + m], Yb[:, 3, 3:4]))
        if m == 0:
            ev = opi(VEC, v_ev, lambda v: v.tensor_copy(thc[:, 6:7], Yb[:, 3, 4:5]))
        wait(TEN, v_ev, ev)
        return ev

    q2_done = []
    nat_st = {}
    pend_st = []

    def flush_nat_st():
        while pend_st:
            pg, pm, pj, po, pq = pend_st.pop(0)
            wait(GPS, v_q, pq)
            nat_st[pg] = dma(GPS, nat_tile_mj(pm, pj), qout[:, po, :], qoS[po])
    qt_st = {}

    def qt_dst(g, k):
        m, j = g // 128, g % 128
        if m < 2:
            dstm = up_qT if m == 0 else gt_qT
            return dstm[(j % 2) * 1024 + k * 128:(j % 2) * 1024 + (k + 1) * 128,
                        (j // 2) * 128:(j // 2 + 1) * 128]
        cc, rr = j // 16, j % 16
        return dn_qT[cc * 1024 + k * 128:cc * 1024 + (k + 1) * 128,
                     rr * 128:(rr + 1) * 128]

    def nat_tile_mj(m, j):
        if m < 2:
            dstm = up_q if m == 0 else gt_q
            return dstm[(j // 2) * 128:(j // 2 + 1) * 128, (j % 2) * 1024:(j % 2 + 1) * 1024]
        cc, rr = j // 16, j % 16
        return dn_q[rr * 128:(rr + 1) * 128, cc * 1024:(cc + 1) * 1024]

    def emit_pass2(tiles, th_evs):
        for m, j in tiles:
            glob = len(q2_done)
            idx = 384 + glob
            slot = idx % 2
            th_ev = th_evs[m]
            if glob >= 2:
                wait(GPS, v_q, q2_done[glob - 2])
            else:
                wait(GPS, v_p1, p1_done[382 + glob])
            _, lsem, ldn = rp1.load(GPS, p1r[:, slot, 0:1024], mat_tile_ap(m, j))
            oslot = glob % 2
            guard = nat_st.get(glob - 2, 0)
            v_q.n += 1
            q2_done.append(v_q.n)

            def quant(v, m=m, slot=slot, oslot=oslot, ldn=ldn, th_ev=th_ev, guard=guard,
                      h1=v_q.h, lh=lsem.h):
                v.wait_ge(lh, ldn)
                v.wait_ge(v_ev.h, th_ev)
                if guard:
                    v.wait_ge(qoS[oslot].h, guard)
                v.drain()
                v.tensor_scalar(ctile[:, :], p1r[:, slot, 0:1024], thc[:, m:m + 1], None,
                                op0=ALU.is_gt)
                v.drain()
                v.scalar_tensor_tensor(ctile[:, :], p1r[:, slot, 0:1024],
                                       thc[:, 3 + m:4 + m], ctile[:, :],
                                       op0=ALU.is_gt, op1=ALU.add)
                v.drain()
                v.scalar_tensor_tensor(ctile[:, :], p1r[:, slot, 0:1024],
                                       nthc[:, m:m + 1], ctile[:, :],
                                       op0=ALU.is_lt, op1=ALU.subtract)
                v.drain()
                return v.scalar_tensor_tensor(qout[:, oslot, :], p1r[:, slot, 0:1024],
                                              nthc[:, 3 + m:4 + m], ctile[:, :],
                                              op0=ALU.is_lt, op1=ALU.add
                                              ).then_inc(h1, 1)
            bg.append(quant)
            pend_st.append((glob, m, j, oslot, v_q.n))
            if len(pend_st) > 1:
                pg, pm, pj, po, pq = pend_st.pop(0)
                wait(GPS, v_q, pq)
                nat_st[pg] = dma(GPS, nat_tile_mj(pm, pj), qout[:, po, :], qoS[po])

    tp_q = [0]

    def emit_tp_store_tile(gq):
        g, q = gq
        ts = tstg if q % 2 == 0 else tstg2
        eng = SYN if q % 2 == 0 else SCA
        wait(eng, c_qt, q + 1)
        for k in range(8):
            qt_st[q] = dma(eng, qt_dst(g, k), ts[:, k * 128:(k + 1) * 128],
                           qtS[q % 2])

    def emit_tpass_oct(m, oct):
        if tp_q[0] == 0:
            wait(TEN, v_ev, v_ev.n)
            wait(SCA, v_ev, v_ev.n)
        wait(TEN, c_si, c_si.n)
        prev = None
        for i in range(8):
            j = oct * 8 + i
            if m < 2:
                ge = oct * 16 + m * 8 + i
            else:
                ge = 256 + j
            q = tp_q[0]
            tp_q[0] += 1
            ridx = rbig.k
            if ridx >= 3:
                wait(SYN, p_, big_free[ridx - 3])
            wait(SYN, qoS[ge % 2], nat_st[ge])
            _, lsem, lv = rbig.load(SYN, big[:, ridx % 3, 0:1024], nat_tile_mj(m, j))
            if prev is not None:
                emit_tp_store_tile(prev)
            wait(TEN, lsem, lv)
            if q >= 2:
                wait(TEN, c_qt, q - 1)
            for k in range(8):
                fn = (lambda t, q=q, k=k, ridx=ridx: t.matmul(
                    X2[:, q % 2, k * 128:(k + 1) * 128],
                    big[:, ridx % 3, k * 128:(k + 1) * 128], ident[:, :],
                    start=True, stop=True))
                if k == 7:
                    big_free.append(opi(TEN, p_, fn))
                else:
                    op(TEN, fn)
            wait(SCA, p_, big_free[-1])
            if q >= 2:
                wait(SCA, qtS[q % 2], qt_st[q - 2])
            c_qt.n += 1
            ts = tstg if q % 2 == 0 else tstg2
            em(SCA, lambda s, q=q, ts=ts, h=c_qt.h: s.copy(ts[:, :], X2[:, q % 2, :]
                                                           ).then_inc(h, 1))
            prev = (m * 128 + j, q)
        emit_tp_store_tile(prev)
        wait(SYN, qtS[0], qtS[0].n)
        wait(SYN, qtS[1], qtS[1].n)
        qprogT.n += 1
        em(SYN, lambda gg, h=qprogT.h: gg.sem_inc(h, 1))

    q_need_up = [2 * eg + 1 for eg in range(16)]
    q_need_gt = [2 * eg + 2 for eg in range(16)]
    q_need_dn = [32 + 2 * (c + 1) for c in range(8)]

    def emit_ln(src_tile, dst_tile, W, resid=False):
        wait(TEN, v_ev, v_ev.n)
        sq_base = v_sq.n
        sum_pe = [0] * DT
        for dt in range(DT):
            slot = dt % 2
            if dt >= 2:
                wait(VEC, p_, sum_pe[dt - 2])
            v_sq.n += 1
            em(VEC, lambda v, dt=dt, slot=slot, s=src_tile, h=v_sq.h, W=W: v.tensor_mul(
                sqr[:, slot, 0:W], s(dt), s(dt)).then_inc(h, 1))
            op(TEN, lambda t, dt=dt, s=src_tile, W=W: t.matmul(
                Yb[0:1, 0, 0:W], onescb[:, 0:1], s(dt),
                start=(dt == 0), stop=(dt == DT - 1)))
            wait(TEN, v_sq, sq_base + dt + 1)
            sum_pe[dt] = opi(TEN, p_, lambda t, dt=dt, slot=slot, W=W: t.matmul(
                Yb[0:1, 1, 0:W], onescb[:, 0:1], sqr[:, slot, 0:W],
                start=(dt == 0), stop=(dt == DT - 1)))
        wait(VEC, p_, p_.n)
        op(VEC, lambda v, W=W: v.tensor_scalar_mul(rowf[0:1, 0, 0:W], Yb[0:1, 0, 0:W], float(1.0 / D)))
        op(VEC, lambda v, W=W: v.tensor_scalar_mul(rowf[0:1, 1, 0:W], Yb[0:1, 1, 0:W], float(1.0 / D)))
        vd(VEC)
        op(VEC, lambda v, W=W: v.tensor_mul(rowf[0:1, 2, 0:W], rowf[0:1, 0, 0:W], rowf[0:1, 0, 0:W]))
        vd(VEC)
        vv = opi(VEC, v_, lambda v, W=W: v.tensor_sub(rowf[0:1, 2, 0:W], rowf[0:1, 1, 0:W],
                                                      rowf[0:1, 2, 0:W]))
        wait(SCA, v_, vv)
        op(SCA, lambda s, W=W: s.activation(rowf[0:1, 3, 0:W], rowf[0:1, 2, 0:W], AF.Ln,
                                            bias=float(LN_EPS), scale=1.0))
        em(SCA, lambda s: s.drain())
        cc = opi(SCA, c_, lambda s, W=W: s.activation(rowf[0:1, 2, 0:W], rowf[0:1, 3, 0:W], AF.Exp,
                                                      bias=0.0, scale=-0.5))
        wait(VEC, c_, cc)
        vd(VEC)
        op(VEC, lambda v, W=W: v.tensor_copy(rowb[0:1, 0, 0:W], rowf[0:1, 2, 0:W]))
        vv2 = opi(VEC, v_, lambda v, W=W: v.scalar_tensor_tensor(
            rowb[0:1, 1, 0:W], rowf[0:1, 0, 0:W], -1.0, rowf[0:1, 2, 0:W],
            op0=ALU.mult, op1=ALU.mult))
        wait(TEN, v_, vv2)
        op(TEN, lambda t, W=W: t.matmul(Yb[:, 0, 0:W], ones1[0:1, :], rowb[0:1, 0, 0:W],
                                        start=True, stop=True))
        bc_pe = opi(TEN, p_, lambda t, W=W: t.matmul(Yb[:, 1, 0:W], ones1[0:1, :], rowb[0:1, 1, 0:W],
                                                     start=True, stop=True))
        wait(VEC, p_, bc_pe)
        op(VEC, lambda v, W=W: v.tensor_copy(rbc[:, 0:W], Yb[:, 0, 0:W]))
        bc_v = opi(VEC, v_ev, lambda v, W=W: v.tensor_copy(abc[:, 0:W], Yb[:, 1, 0:W]))
        vd(VEC)
        for dt in range(DT):
            if not resid:
                op(VEC, lambda v, dt=dt, s=src_tile, d=dst_tile, W=W: v.tensor_mul(
                    d(dt), s(dt), rbc[:, 0:W]))
            else:
                op(VEC, lambda v, dt=dt, s=src_tile, W=W: v.tensor_mul(
                    s(dt), s(dt), rbc[:, 0:W]))
        vd(VEC)
        for dt in range(DT):
            tgt = (lambda v, dt=dt, s=src_tile, d=dst_tile, W=W: v.tensor_add(
                (s(dt) if resid else d(dt)), (s(dt) if resid else d(dt)), abc[:, 0:W]))
            if dt == DT - 1 and not resid:
                opi(VEC, v_, tgt)
            else:
                op(VEC, tgt)
        if resid:
            vd(VEC)
            for dt in range(DT):
                tgt = (lambda v, dt=dt, s=src_tile, d=dst_tile, W=W: v.tensor_add(
                    d(dt), d(dt), s(dt)))
                if dt == DT - 1:
                    opi(VEC, v_, tgt)
                else:
                    op(VEC, tgt)
        return v_.n, bc_v

    ln3_done = [0] * NTB
    tp_h1 = tp_lg = tp_p = tp_at = tp_o = tp_x1 = tp_hid = tp_f = None

    def src_xT(tb):
        return lambda dt: xT[:, dt, tb * TB:(tb + 1) * TB]

    def attn_tb(tb):
        nonlocal tp_h1, tp_lg, tp_p, tp_at, tp_o, tp_x1
        if tb == 0:
            wait(TEN, s_ld, xT_ready)
            wait(VEC, s_ld, xT_ready)
            wait(TEN, v_ev, v_ev.n)
        h1_v, _ = emit_ln(src_xT(tb), lambda dt: h1T[:, dt, :], TB)
        if tb == 0:
            tp_h1 = tap("tap_h1", h1T.ap(), v_, h1_v)
        lg_ev = [0, 0]
        for pair in range(2):
            wait(TEN, v_, h1_v)
            wait(TEN, v_ev, v_ev.n)
            for dti in range(2 * DT):
                dt, part = dti // 2, dti % 2
                srcm = mm_mat if part == 0 else mm_lo
                k = rmm.k
                if k == 0:
                    wait(SYN, msT[0], mmst_done[0])
                    wait(SYN, msT[1], mmst_done[1])
                    wait(SYN, mloT, mmst_done[2])
                if k >= 2:
                    wait(SYN, p_, mm_free[k - 2])
                _, msem, mch = rmm.load(SYN, mmr[:, dti % 2, :],
                                        srcm[dt * 128:(dt + 1) * 128, :])
                wait(TEN, msem, mch)
                for tp in range(2):
                    t128 = pair * 2 + tp
                    for nn in range(2):
                        fn = (lambda t, dti=dti, dt=dt, tp=tp, t128=t128, nn=nn: t.matmul(
                            X2[:, tp, nn * 512:(nn + 1) * 512],
                            h1T[:, dt, t128 * 128:(t128 + 1) * 128],
                            mmr[:, dti % 2, nn * 512:(nn + 1) * 512],
                            start=(dti == 0), stop=False))
                        if tp == 1 and nn == 1:
                            mm_free.append(opi(TEN, p_, fn))
                        else:
                            op(TEN, fn)
            for tp in range(2):
                for nn in range(2):
                    fn = (lambda t, tp=tp, nn=nn: t.matmul(
                        X2[:, tp, nn * 512:(nn + 1) * 512], ones1[0:1, :],
                        blog[0:1, nn * 512:(nn + 1) * 512], start=False, stop=True))
                    if tp == 1 and nn == 1:
                        pe_lg = opi(TEN, p_, fn)
                    else:
                        op(TEN, fn)
            wait(VEC, p_, pe_lg)
            if tb == 0 and pair == 1:
                tap_gate(VEC, tp_lg)
            for tp in range(2):
                fn = (lambda v, tp=tp: v.tensor_copy(logit[:, tp, :], X2[:, tp, :]))
                if tp == 1:
                    lg_ev[pair] = opi(VEC, v_ev, fn)
                else:
                    op(VEC, fn)
            if tb == 0 and pair == 0:
                tp_lg = tap("tap_lg", logit.ap(), v_ev, lg_ev[0])
            if tb == 0 and pair == 1:
                tap_gate(VEC, tp_lg)
                tap_gate(VEC, tp_p)
            vd(VEC)
            for tp in range(2):
                for h in range(H):
                    op(VEC, lambda v, tp=tp, h=h: v.max(out=mx8[:, h, :],
                                                        in_=logit[:, tp, h * L:(h + 1) * L]))
                vd(VEC)
                op(VEC, lambda v: v.tensor_copy(m1r[:, :], mx8[:, :, 0]))
                op(VEC, lambda v: v.tensor_copy(taur[:, :], mx8[:, :, 3]))
                l3 = logit[:, tp, :].rearrange("p (h l) -> p h l", l=L)
                m3 = m1r.ap().rearrange("p (h o) -> p h o", o=1).to_broadcast([128, H, L])
                t3 = taur.ap().rearrange("p (h o) -> p h o", o=1).to_broadcast([128, H, L])
                e3 = ework.ap().rearrange("p (h l) -> p h l", l=L)
                k3 = maskb.ap().rearrange("p (h l) -> p h l", l=L)
                vd(VEC)
                op(VEC, lambda v, l3=l3, t3=t3, k3=k3: v.tensor_tensor(k3, l3, t3, op=ALU.is_ge))
                vv = opi(VEC, v_, lambda v, l3=l3, m3=m3, e3=e3: v.tensor_tensor(
                    e3, l3, m3, op=ALU.subtract))
                wait(SCA, v_, vv)
                cc = opi(SCA, c_, lambda s: s.activation(ework[:, :], ework[:, :], AF.Exp,
                                                         bias=0.0, scale=1.0))
                wait(VEC, c_, cc)
                vd(VEC)
                op(VEC, lambda v: v.tensor_mul(ework[:, :], ework[:, :], maskb[:, :]))
                vd(VEC)
                op(VEC, lambda v, e3=e3: v.reduce_sum(ssum[:, :], e3, axis=AX.X))
                vd(VEC)
                op(VEC, lambda v: v.reciprocal(rsum[:, :], ssum[:, :]))
                vd(VEC)
                r3 = rsum.ap().rearrange("p (h o) -> p h o", o=1).to_broadcast([128, H, L])
                p3 = p_sb[:, tp, :].rearrange("p (h l) -> p h l", l=L)
                v_p.n += 1
                em(VEC, lambda v, e3=e3, r3=r3, p3=p3, h=v_p.h: v.tensor_tensor(
                    p3, e3, r3, op=ALU.mult).then_inc(h, 1))
            wait(TEN, v_ev, lg_ev[pair])
            for tp in range(2):
                t128 = pair * 2 + tp
                wait(TEN, v_p, 4 * tb + t128 + 1)
                for hl in range(8):
                    fn = (lambda t, tp=tp, hl=hl: t.transpose(
                        Xbf[:, tp, hl * 128:(hl + 1) * 128],
                        p_sb[:, tp, hl * 128:(hl + 1) * 128], ident[:, :]))
                    if hl == 7:
                        opi(TEN, p_, fn)
                    else:
                        op(TEN, fn)
                wait(VEC, p_, p_.n)
                opi(VEC, v_ev, lambda v, tp=tp, t128=t128: v.tensor_copy(
                    pT_sb.ap()[:, :, t128 * 128:(t128 + 1) * 128],
                    Xbf[:, tp, :].rearrange("p (a b) -> p a b", a=8)))
            if tb == 0 and pair == 0:
                tp_p = tap("tap_p", p_sb.ap(), v_p, 2)
        pT_done = v_ev.n
        drain_bg(100)
        for dh in range(2):
            if dh == 0:
                wait(TEN, v_ev, pT_done)
            for ci in range(16):
                ridx = rbig.k
                if ridx >= 3:
                    wait(SYN, p_, big_free[ridx - 3])
                if tb == 0 and dh == 0 and ci == 0:
                    wait(SYN, stgT["wv"][0], wvbf_done[0])
                    wait(SYN, stgT["wv"][1], wvbf_done[1])
                _, lsem, lv = rbig.load(SYN, big[:, ridx % 3, 0:1024],
                                        wv_bf[dh * 1024:(dh + 1) * 1024,
                                              ci * 128:(ci + 1) * 128], transpose=True)
                wait(TEN, lsem, lv)
                rhs = h1T[:, ci, :]
                for k in range(8):
                    ps = Xb[:, k, :] if k < 4 else Yb[:, k - 4, :]
                    fn = (lambda t, ps=ps, ridx=ridx, k=k, rhs=rhs, ci=ci: t.matmul(
                        ps, big[:, ridx % 3, k * 128:(k + 1) * 128], rhs,
                        start=(ci == 0), stop=False))
                    if k == 7:
                        big_free.append(opi(TEN, p_, fn))
                    else:
                        op(TEN, fn)
            for k in range(8):
                ps = Xb[:, k, :] if k < 4 else Yb[:, k - 4, :]
                fn = (lambda t, ps=ps, k=k, dh=dh: t.matmul(
                    ps, bv_sb[0:1, dh * 1024 + k * 128:dh * 1024 + (k + 1) * 128],
                    onesN[0:1, :], start=False, stop=True))
                if k == 7:
                    at_pe = opi(TEN, p_, fn)
                else:
                    op(TEN, fn)
            wait(VEC, p_, at_pe)
            if tb == 1 and dh == 0:
                tap_gate(VEC, tp_at)
            for k in range(8):
                ps = Xb[:, k, :] if k < 4 else Yb[:, k - 4, :]
                fn = (lambda v, ps=ps, k=k, dh=dh: v.tensor_copy(
                    attnT[:, dh * 8 + k, :], ps))
                if k == 7:
                    opi(VEC, v_ev, fn)
                else:
                    op(VEC, fn)
            wait(TEN, v_ev, v_ev.n)
        if tb == 0:
            tp_at = tap("tap_at", attnT.ap(), v_ev, v_ev.n)
        drain_bg(40)
        for dh in range(2):
            for ci in range(24):
                ridx = rbig.k
                if ridx >= 3:
                    wait(SYN, p_, big_free[ridx - 3])
                if ci < 16:
                    _, lsem, lv = rbig.load(SYN, big[:, ridx % 3, 0:1024],
                                            wo_bf[dh * 1024:(dh + 1) * 1024,
                                                  ci * 128:(ci + 1) * 128], transpose=True)
                else:
                    if tb == 0 and dh == 0 and ci == 16:
                        wait(SYN, gsT[0], gst_done[0])
                        wait(SYN, gsT[1], gst_done[1])
                    _, lsem, lv = rbig.load(SYN, big[:, ridx % 3, 0:1024],
                                            g_mat[(ci - 16) * 128:(ci - 15) * 128,
                                                  dh * 1024:(dh + 1) * 1024])
                wait(TEN, lsem, lv)
                rhs = attnT[:, ci, :] if ci < 16 else pT_sb[:, ci - 16, :]
                for k in range(8):
                    ps = Xb[:, k, :] if k < 4 else Yb[:, k - 4, :]
                    fn = (lambda t, ps=ps, ridx=ridx, k=k, ci=ci, rhs=rhs: t.matmul(
                        ps, big[:, ridx % 3, k * 128:(k + 1) * 128],
                        rhs, start=(ci == 0), stop=False))
                    if k == 7:
                        big_free.append(opi(TEN, p_, fn))
                    else:
                        op(TEN, fn)
            for k in range(8):
                ps = Xb[:, k, :] if k < 4 else Yb[:, k - 4, :]
                fn = (lambda t, ps=ps, k=k, dh=dh: t.matmul(
                    ps, bo_sb[0:1, dh * 1024 + k * 128:dh * 1024 + (k + 1) * 128],
                    onesN[0:1, :], start=False, stop=True))
                if k == 7:
                    o_pe = opi(TEN, p_, fn)
                else:
                    op(TEN, fn)
            wait(VEC, p_, o_pe)
            if tb == 0 and dh == 0:
                tap_gate(VEC, tp_h1)
            for k in range(8):
                ps = Xb[:, k, :] if k < 4 else Yb[:, k - 4, :]
                fn = (lambda v, ps=ps, k=k, dh=dh: v.tensor_copy(
                    h1T[:, dh * 8 + k, :], ps))
                if k == 7:
                    opi(VEC, v_ev, fn)
                else:
                    op(VEC, fn)
            wait(TEN, v_ev, v_ev.n)
        if tb == 0:
            tp_o = tap("tap_o", h1T.ap(), v_ev, v_ev.n)
            tap_gate(VEC, tp_o)
            tap_gate(VEC, tp_h1)
        drain_bg(40)
        v_ln3, _ = emit_ln(lambda dt: h1T[:, dt, :], src_xT(tb), TB, resid=True)
        ln3_done[tb] = v_ln3
        if tb == 0:
            tp_x1 = tap("tap_x1", xT.ap()[:, :, 0:TB], v_, v_ln3)
        drain_bg(30)

    last_ln4 = [0]

    def ffn_tb(tb):
        nonlocal tp_hid, tp_f
        wait(TEN, v_, ln3_done[tb])
        h2_v, _ = emit_ln(src_xT(tb), lambda dt: h1T[:, dt, :], TB)
        for eg in range(16):
            if eg == 0:
                wait(TEN, v_, h2_v)
            if tb == 0:
                emit_tpass_oct(0, eg)
                emit_tpass_oct(1, eg)
                wait(TEN, c_qt, 16 * (eg + 1))
            for mat in range(2):
                srcq = up_qT if mat == 0 else gt_qT
                need = q_need_up[eg] if mat == 0 else q_need_gt[eg]
                if mat == 0 and eg > 0:
                    wait(TEN, c_si, c_si.n)
                if mat == 1 and eg > 0:
                    wait(TEN, v_hid, v_hid.n)
                for d_in in range(DT):
                    ridx = rsml.k
                    if ridx == 0:
                        wait(SYN, v_, ln3_done[3])
                    if ridx >= 4:
                        wait(SYN, p_, sml_free[ridx - 4])
                    if d_in == 0 and tb == 0:
                        wait(SYN, qprogT, need)
                    _, lsem, lv = rsml.load(SYN, sml[:, ridx % 4, :],
                                            srcq[d_in * 128:(d_in + 1) * 128,
                                                 eg * 512:(eg + 1) * 512])
                    wait(TEN, lsem, lv)
                    for et in range(4):
                        ps = Xb[:, et, :] if mat == 0 else Yb[:, et, :]
                        fn = (lambda t, ps=ps, ridx=ridx, et=et, d_in=d_in: t.matmul(
                            ps, sml[:, ridx % 4, et * 128:(et + 1) * 128],
                            h1T[:, d_in, :], start=(d_in == 0), stop=(d_in == DT - 1)))
                        if et == 3:
                            sml_free.append(opi(TEN, p_, fn))
                        else:
                            op(TEN, fn)
                if mat == 0:
                    up_pe = sml_free[-1]
                else:
                    gt_pe = sml_free[-1]
            wait(SCA, p_, up_pe)
            if eg > 0:
                wait(SCA, v_hid, v_hid.n)
            if tb == 0 and eg == 0:
                wait(SCA, v_ev, th_ready0[0])
            for et in range(4):
                fn = (lambda s, et=et: s.activation(silu_sb[:, et, :], Xb[:, et, :], AF.Silu,
                                                    bias=0.0, scale=thc[:, 6:7]))
                if et == 3:
                    c_si.n += 1
                    em(SCA, lambda s, f=fn, h=c_si.h: f(s).then_inc(h, 1))
                else:
                    op(SCA, fn)
            wait(VEC, p_, gt_pe)
            wait(VEC, c_si, c_si.n)
            if tb == 1 and eg == 0:
                tap_gate(VEC, tp_hid)
            for et in range(4):
                fn = (lambda v, et=et, eg=eg: v.tensor_mul(
                    hid[:, eg * 4 + et, :], silu_sb[:, et, :], Yb[:, et, :]))
                if et == 3:
                    v_hid.n += 1
                    em(VEC, lambda v, f=fn, h=v_hid.h: f(v).then_inc(h, 1))
                else:
                    op(VEC, fn)
            drain_bg(24)
        if tb == 0:
            tp_hid = tap("tap_hid", hid.ap(), v_hid, v_hid.n)
            drain_bg(len(bg))
        if tb == 0:
            for oc in range(16):
                emit_tpass_oct(2, oc)
        wait(TEN, v_hid, v_hid.n)
        wait(TEN, c_si, c_si.n)
        if tb == 0:
            wait(TEN, c_qt, 384)
        for dh in range(2):
            for et in range(ET):
                ridx = rbig.k
                if ridx >= 3:
                    wait(SYN, p_, big_free[ridx - 3])
                if tb == 0 and dh == 0 and et % 8 == 0:
                    wait(SYN, qprogT, q_need_dn[et // 8])
                _, lsem, lv = rbig.load(SYN, big[:, ridx % 3, 0:1024],
                                        dn_qT[et * 128:(et + 1) * 128,
                                              dh * 1024:(dh + 1) * 1024])
                wait(TEN, lsem, lv)
                for k in range(8):
                    ps = Xb[:, k, :] if k < 4 else Yb[:, k - 4, :]
                    fn = (lambda t, ps=ps, ridx=ridx, k=k, et=et: t.matmul(
                        ps, big[:, ridx % 3, k * 128:(k + 1) * 128],
                        hid[:, et, :], start=(et == 0), stop=(et == ET - 1)))
                    if k == 7:
                        big_free.append(opi(TEN, p_, fn))
                    else:
                        op(TEN, fn)
            d_pe = big_free[-1]
            wait(VEC, p_, d_pe)
            for k in range(8):
                ps = Xb[:, k, :] if k < 4 else Yb[:, k - 4, :]
                fn = (lambda v, ps=ps, k=k, dh=dh: v.tensor_copy(h1T[:, dh * 8 + k, :], ps))
                if k == 7:
                    opi(VEC, v_ev, fn)
                else:
                    op(VEC, fn)
            wait(TEN, v_ev, v_ev.n)
        if tb == 0:
            tp_f = tap("tap_f", h1T.ap(), v_ev, v_ev.n)
            tap_gate(VEC, tp_f)
            tap_gate(VEC, tp_x1)
        lv4, _ = emit_ln(lambda dt: h1T[:, dt, :], src_xT(tb), TB, resid=True)
        last_ln4[0] = lv4
        drain_bg(20)

    th_ready0 = [0]
    attn_tb(0)
    attn_tb(1)
    th0 = emit_th(0)
    th_ready0[0] = th0
    attn_tb(2)
    th1 = emit_th(1)
    tiles01 = []
    for oct in range(16):
        tiles01 += [(0, oct * 8 + i) for i in range(8)]
        tiles01 += [(1, oct * 8 + i) for i in range(8)]
    emit_pass2(tiles01, {0: th0, 1: th1})
    flush_nat_st()
    attn_tb(3)
    th2 = emit_th(2)
    emit_pass2([(2, j) for j in range(128)], {2: th2})
    flush_nat_st()
    ffn_tb(0)
    ffn_tb(1)
    ffn_tb(2)
    ffn_tb(3)

    wait(TEN, v_, last_ln4[0])
    for t128 in range(16):
        wait(TEN, v_ev, v_ev.n)
        for dt in range(DT):
            bank = dt // 8
            fn = (lambda t, dt=dt, t128=t128, bank=bank: t.transpose(
                Xbf[:, bank, (dt % 8) * 128:(dt % 8 + 1) * 128],
                xT[:, dt, t128 * 128:(t128 + 1) * 128], ident[:, :]))
            if dt == DT - 1:
                opi(TEN, p_, fn)
            else:
                op(TEN, fn)
        wait(VEC, p_, p_.n)
        if t128 >= 1:
            wait(VEC, o_st, 16 * t128)
        ev = opi(VEC, v_ev, lambda v: v.tensor_copy(
            ostg[:, :], Xbf[:, 0:2, :].rearrange("p a b -> p (a b)")))
        wait(GPS, v_ev, ev)
        dma(GPS, out[t128 * 128:(t128 + 1) * 128, :], ostg[:, :], o_st)
    wait(GPS, o_st, 16 * 16)
    wait(SYN, o_st, 16 * 16)

    with nc.Block() as block:
        @block.tensor
        def _(t):
            for fn in sched[TEN]:
                fn(t)

        @block.vector
        def _(v):
            for fn in sched[VEC]:
                fn(v)

        @block.scalar
        def _(s):
            for fn in sched[SCA]:
                fn(s)

        @block.gpsimd
        def _(g):
            for fn in sched[GPS]:
                fn(g)

        @block.sync
        def _(s):
            for fn in sched[SYN]:
                fn(s)

    return nc


_NC_CACHE = {}


def kernel(L0_attn_hidden_states, L0_attn_Wq, L0_attn_bq, L0_attn_Wk, L0_attn_bk,
           L0_attn_Wv, L0_attn_bv, L0_attn_Wo, L0_attn_bo, L0_attn_latent_keys,
           L0_ffn_W_up, L0_ffn_W_gate, L0_ffn_W_down):
    x = np.ascontiguousarray(np.asarray(L0_attn_hidden_states, dtype=np.float32)).reshape(16384, 2048)
    w = {
        "wq": np.ascontiguousarray(np.asarray(L0_attn_Wq, dtype=np.float32)),
        "bq": np.ascontiguousarray(np.asarray(L0_attn_bq, dtype=np.float32)),
        "wv": np.ascontiguousarray(np.asarray(L0_attn_Wv, dtype=np.float32)),
        "bv": np.ascontiguousarray(np.asarray(L0_attn_bv, dtype=np.float32)),
        "wo": np.ascontiguousarray(np.asarray(L0_attn_Wo, dtype=np.float32)),
        "bo": np.ascontiguousarray(np.asarray(L0_attn_bo, dtype=np.float32)),
        "lat": np.ascontiguousarray(np.asarray(L0_attn_latent_keys, dtype=np.float32)),
        "wup": np.ascontiguousarray(np.asarray(L0_ffn_W_up, dtype=np.float32)),
        "wgt": np.ascontiguousarray(np.asarray(L0_ffn_W_gate, dtype=np.float32)),
        "wdn": np.ascontiguousarray(np.asarray(L0_ffn_W_down, dtype=np.float32)),
    }
    if "nc" not in _NC_CACHE:
        _NC_CACHE["nc"] = build_program()
    nc = _NC_CACHE["nc"]
    in_maps = [dict(w, x=x[c * TOK:(c + 1) * TOK]) for c in range(8)]
    res = run_bass_kernel_spmd(nc, in_maps, core_ids=list(range(8)))
    outs = [np.asarray(r["out"]) for r in res.results]
    return np.concatenate(outs, axis=0).reshape(2, 8192, 2048).astype(np.float32)


# version 13
# speedup vs baseline: 1.2793x; 1.0644x over previous
import numpy as np
import ml_dtypes
import concourse.bass as bass
import concourse.mybir as mybir
from concourse.bass_utils import run_bass_kernel_spmd

F32 = mybir.dt.float32
BF16 = mybir.dt.bfloat16
AF = mybir.ActivationFunctionType
ALU = mybir.AluOpType
AX = mybir.AxisListType

TOK = 2048
D = 2048
DT = 16
TB = 512
NTB = 4
H = 16
L = 64
HL = 1024
E = 8192
ET = 64
LN_EPS = 1e-5
Q_EPS = 1e-8
ISQ = float(1.0 / np.sqrt(128.0))
NW = float(E * D)

TEN, VEC, SCA, GPS, SYN = "tensor", "vector", "scalar", "gpsimd", "sync"


class Sem:
    __slots__ = ("h", "n")

    def __init__(self, nc, name):
        self.h = nc.alloc_semaphore(name)
        self.n = 0


def build_program(debug=False):
    nc = bass.Bass("TRN2", target_bir_lowering=False)

    x_in = nc.dram_tensor("x", [TOK, D], F32, kind="ExternalInput").ap()
    wq = nc.dram_tensor("wq", [D, D], F32, kind="ExternalInput").ap()
    bq = nc.dram_tensor("bq", [D], F32, kind="ExternalInput").ap()
    wv = nc.dram_tensor("wv", [D, D], F32, kind="ExternalInput").ap()
    bv = nc.dram_tensor("bv", [D], F32, kind="ExternalInput").ap()
    wo = nc.dram_tensor("wo", [D, D], F32, kind="ExternalInput").ap()
    bo = nc.dram_tensor("bo", [D], F32, kind="ExternalInput").ap()
    lat = nc.dram_tensor("lat", [L, D], F32, kind="ExternalInput").ap()
    wup = nc.dram_tensor("wup", [E, D], F32, kind="ExternalInput").ap()
    wgt = nc.dram_tensor("wgt", [E, D], F32, kind="ExternalInput").ap()
    wdn = nc.dram_tensor("wdn", [D, E], F32, kind="ExternalInput").ap()
    out = nc.dram_tensor("out", [TOK, D], F32, kind="ExternalOutput").ap()

    x_bf = nc.dram_tensor("x_bf", [TOK, D], BF16, kind="Internal").ap()
    wv_bf = nc.dram_tensor("wv_bf", [D, D], BF16, kind="Internal").ap()
    wo_bf = nc.dram_tensor("wo_bf", [D, D], BF16, kind="Internal").ap()
    g_mat = nc.dram_tensor("g_mat", [HL, D], BF16, kind="Internal").ap()
    mm_mat = nc.dram_tensor("mm_mat", [D, HL], BF16, kind="Internal").ap()
    mm_lo = nc.dram_tensor("mm_lo", [D, HL], BF16, kind="Internal").ap()
    up_q = nc.dram_tensor("up_q", [E, D], BF16, kind="Internal").ap()
    gt_q = nc.dram_tensor("gt_q", [E, D], BF16, kind="Internal").ap()
    dn_q = nc.dram_tensor("dn_q", [D, E], BF16, kind="Internal").ap()
    up_qT = nc.dram_tensor("up_qT", [D, E], BF16, kind="Internal").ap()
    gt_qT = nc.dram_tensor("gt_qT", [D, E], BF16, kind="Internal").ap()
    dn_qT = nc.dram_tensor("dn_qT", [E, D], BF16, kind="Internal").ap()
    s_scr = [nc.dram_tensor(f"s_scr{i}", [n], F32, kind="Internal").ap()
             for i, n in ((0, E), (1, E), (2, D))]
    ident_dram = nc.inline_tensor(np.eye(128, dtype=ml_dtypes.bfloat16), name="ident_c")

    def S(name, shape, dtype):
        return nc.alloc_sbuf_tensor(name, shape, dtype)

    xT = S("xT", [128, DT, TOK], BF16)
    h1T = S("h1T", [128, DT, TB], BF16)
    sqr = S("sqr", [128, 2, TB], BF16)
    rowf = S("rowf", [1, 4, TB], F32)
    rowb = S("rowb", [1, 2, TB], BF16)
    rbc = S("rbc", [128, TB], BF16)
    abc = S("abc", [128, TB], BF16)
    p1r = S("p1r", [128, 2, 1032], F32)
    ctile = S("ctile", [128, 1024], BF16)
    qout = S("qout", [128, 2, 1024], BF16)
    tstg = S("tstg", [128, 1024], BF16)
    tstg2 = S("tstg2", [128, 1024], BF16)
    big = S("big", [128, 3, 1024], BF16)
    ident = S("ident", [128, 128], BF16)
    latT = S("latT", [128, H, L], BF16)
    bqT = S("bqT", [128, 16], BF16)
    bq_sb = S("bq_sb", [16, 128], BF16)
    blog = S("blog", [1, HL], BF16)
    bv_sb = S("bv_sb", [1, D], BF16)
    bo_sb = S("bo_sb", [1, D], BF16)
    ones1 = S("ones1", [1, 128], BF16)
    ones1f = S("ones1f", [1, 128], F32)
    onescb = S("onescb", [128, 1], BF16)
    onesN = S("onesN", [1, TB], BF16)
    rowacc = [S("rowacc_up", [128, 64], F32), S("rowacc_gt", [128, 64], F32),
              S("rowacc_dn", [128, 16], F32)]
    line = S("line", [1, 2, 72], F32)
    thc = S("thc", [128, 8], F32)
    nthc = S("nthc", [128, 8], F32)
    sc1 = S("sc1", [1, 16], F32)
    cpool = S("cpool", [1, 16], F32)
    mx8 = S("mx8", [128, H, 8], F32)
    m1r = S("m1r", [128, H], F32)
    taur = S("taur", [128, H], F32)
    ssum = S("ssum", [128, H], F32)
    rsum = S("rsum", [128, H], F32)

    AR_B = 72 * 1024
    arena = nc.alloc_sbuf_tensor("arena", [128, AR_B], mybir.dt.uint8)
    ar0 = nc.lookup_mloc(arena).addr

    def A(name, shape, dtype, off):
        return nc.alloc_sbuf_tensor_at(name, shape, dtype, offset=ar0 + off)

    hid = A("hid", [128, ET, TB], BF16, 0)
    silu_sb = A("silu_sb", [128, 4, TB], BF16, 64 * 1024)
    logit = A("logit", [128, 2, HL], F32, 0)
    ework = A("ework", [128, HL], F32, 8 * 1024)
    maskb = A("maskb", [128, HL], BF16, 12 * 1024)
    p_sb = A("p_sb", [128, 2, HL], BF16, 14 * 1024)
    pT_sb = A("pT_sb", [128, 8, TB], BF16, 18 * 1024)
    attnT = A("attnT", [128, DT, TB], BF16, 26 * 1024)
    stg = A("stg", [128, 2, D], BF16, 42 * 1024)
    gstg = A("gstg", [64, 2, D], BF16, 50 * 1024)
    mstg = A("mstg", [128, 2, 1024], BF16, 58 * 1024)
    mlo_st = A("mlo_st", [128, 1024], BF16, 62 * 1024)
    pf32 = A("pf32", [128, 1024], F32, 50 * 1024)
    lat_sb = A("lat_sb", [64, D], BF16, 26 * 1024)
    mmr = A("mmr", [128, 2, HL], BF16, 58 * 1024)
    sml = A("sml", [128, 4, TB], BF16, 68 * 1024)
    ostg = A("ostg", [128, D], F32, 50 * 1024)


    taps = {}
    if debug:
        for nm, shape, dtp in (("tap_h1", [128, DT, TB], BF16), ("tap_lg", [128, 2, HL], F32),
                               ("tap_p", [128, 2, HL], BF16), ("tap_at", [128, DT, TB], BF16),
                               ("tap_o", [128, DT, TB], BF16), ("tap_x1", [128, DT, TB], BF16),
                               ("tap_hid", [128, ET, TB], BF16), ("tap_f", [128, DT, TB], BF16)):
            taps[nm] = nc.dram_tensor(nm, shape, dtp, kind="Internal").ap()

    psX = nc.alloc_psum_tensor("psX", [128, 4, 512], F32)
    psY = nc.alloc_psum_tensor("psY", [128, 4, 512], F32)
    Xb = psX.ap()
    Yb = psY.ap()
    Xflat = Xb.rearrange("p a b -> p (a b)")
    Yflat = Yb.rearrange("p a b -> p (a b)")
    X2 = Xflat.rearrange("p (a b) -> p a b", a=2)
    X8 = Xb.rearrange("p a (b c) -> p (a b) c", b=2)
    Y8 = Yb.rearrange("p a (b c) -> p (a b) c", b=2)
    Xbf = Xb.bitcast(BF16)
    Ybf = Yb.bitcast(BF16)

    sched = {k: [] for k in (TEN, VEC, SCA, GPS, SYN)}
    sems = {}

    def sem(name):
        if name not in sems:
            sems[name] = Sem(nc, name)
        return sems[name]

    def em(eng, fn):
        sched[eng].append(fn)

    def wait(eng, s, n):
        if n > 0:
            em(eng, lambda e, h=s.h, v=n: e.wait_ge(h, v))

    def dma(eng, out_ap, in_ap, s, transpose=False):
        s.n += 16
        if transpose:
            em(eng, lambda e, o=out_ap, i=in_ap, h=s.h: e.dma_start(o, i, transpose=True).then_inc(h, 16))
        else:
            em(eng, lambda e, o=out_ap, i=in_ap, h=s.h: e.dma_start(o, i).then_inc(h, 16))
        return s.n

    def op(eng, fn):
        em(eng, fn)

    def opi(eng, s, fn):
        s.n += 1
        em(eng, lambda e, f=fn, h=s.h: f(e).then_inc(h, 1))
        return s.n

    def vd(eng):
        em(eng, lambda e: e.drain())

    def nopi(eng, s):
        s.n += 1
        em(eng, lambda e, h=s.h: e.nop().then_inc(h, 1))
        return s.n

    g_ld = sem("g_ld")
    g_st = sem("g_st")
    s_ld = sem("s_ld")
    q_st = sem("q_st")
    o_st = sem("o_st")
    v_ = sem("v_")
    p_ = sem("p_")
    c_ = sem("c_")
    v_p1 = sem("v_p1")
    v_q = sem("v_q")
    th_dm = sem("th_dm")
    th_ln = [sem("th_ln0"), sem("th_ln1")]
    v_sq = sem("v_sq")
    v_ev = sem("v_ev")
    v_p = sem("v_p")
    fbig = sem("fbig")
    fsml = sem("fsml")
    fmm = sem("fmm")
    fq2 = sem("fq2")
    c_si = sem("c_si")
    v_hid = sem("v_hid")
    g_c = sem("g_c")

    class RingLd:

        def __init__(self, name, nslots):
            self.sems = [sem(f"{name}_s{i}") for i in range(nslots)]
            self.nslots = nslots
            self.k = 0

        def load(self, eng, out_ap, in_ap, transpose=False):
            slot = self.k % self.nslots
            s = self.sems[slot]
            self.k += 1
            v = dma(eng, out_ap, in_ap, s, transpose=transpose)
            return slot, s, v

    rbig = RingLd("rbig", 3)
    rsml = RingLd("rsml", 4)
    rmm = RingLd("rmm", 2)
    big_free = []
    sml_free = []
    mm_free = []
    rp1 = RingLd("rp1", 2)
    rstg = RingLd("rstg", 2)
    stgT = {name: [sem(f"stgT_{name}0"), sem(f"stgT_{name}1")]
            for name in ("x", "wv", "wo")}
    msT = [sem("msT0"), sem("msT1")]
    mloT = sem("mloT")
    gsT = [sem("gsT0"), sem("gsT1")]
    g_sm = sem("g_sm")
    qoS = [sem("qoS0"), sem("qoS1")]
    qtS = [sem("qtS0"), sem("qtS1")]
    qprogT = sem("qprogT")
    c_qt = sem("c_qt")
    qprog = sem("qprog")
    g_tp = sem("g_tp")

    CONSTS = [1.0 / D, LN_EPS, Q_EPS, 0.5, -0.5, 1.0 / NW, ISQ, -1.0, float(2.0 ** -30), -128.0]
    for i, cv in enumerate(CONSTS):
        op(GPS, lambda g, i=i, cv=cv: g.memset(cpool[0:1, i:i + 1], float(cv)))
        nc.const_aps.register(F32, float(cv), cpool[0:1, i:i + 1])
    op(GPS, lambda g: g.memset(ones1[:, :], 1.0))
    op(GPS, lambda g: g.memset(ones1f[:, :], 1.0))
    op(GPS, lambda g: g.memset(onescb[:, :], 1.0))
    op(GPS, lambda g: g.memset(onesN[:, :], 1.0))
    opi(GPS, g_c, lambda g: g.memset(sc1[0:1, :], 0.0))
    for eng in (TEN, VEC, SCA, SYN):
        wait(eng, g_c, g_c.n)

    dma(GPS, ident[:, :], ident_dram.ap(), g_sm)
    slot_store = [None, None]
    for name, dst, srcd in (("x", x_bf, x_in), ("wv", wv_bf, wv), ("wo", wo_bf, wo)):
        for i in range(16):
            slotp = (rstg.k) % 2
            if slot_store[slotp] is not None:
                wait(GPS, slot_store[slotp][0], slot_store[slotp][1])
            slot, lsem, lv = rstg.load(GPS, stg[:, slotp, :], srcd[i * 128:(i + 1) * 128, :])
            wait(GPS, lsem, lv)
            sv = dma(GPS, dst[i * 128:(i + 1) * 128, :], stg[:, slotp, :], stgT[name][slotp])
            slot_store[slotp] = (stgT[name][slotp], sv)
    xbf_done = (stgT["x"][0].n, stgT["x"][1].n)
    wvbf_done = (stgT["wv"][0].n, stgT["wv"][1].n)
    wobf_done = (stgT["wo"][0].n, stgT["wo"][1].n)

    dma(GPS, lat_sb[:, :], lat, g_sm)
    dma(GPS, bq_sb[:, :], bq.rearrange("(a b) -> a b", a=16), g_sm)
    dma(GPS, bv_sb[:, :], bv.rearrange("(a b) -> a b", a=1), g_sm)
    small_done = dma(GPS, bo_sb[:, :], bo.rearrange("(a b) -> a b", a=1), g_sm)

    wait(SYN, stgT["x"][0], xbf_done[0])
    wait(SYN, stgT["x"][1], xbf_done[1])
    xT_ready = 0
    for dt in range(DT):
        xT_ready = dma(SYN, xT[:, dt, :], x_bf[0:TOK, dt * 128:(dt + 1) * 128], s_ld, transpose=True)

    wait(TEN, g_sm, small_done)
    for grp in range(4):
        if grp > 0:
            wait(TEN, v_ev, v_ev.n)
        for k in range(4):
            h = grp * 4 + k
            fn = (lambda t, h=h, k=k: t.transpose(
                Ybf[:, 0, k * 64:(k + 1) * 64], lat_sb[:, h * 128:(h + 1) * 128],
                ident[0:64, 0:64]))
            if k == 3:
                opi(TEN, p_, fn)
            else:
                op(TEN, fn)
        wait(VEC, p_, p_.n)
        opi(VEC, v_ev, lambda v, grp=grp: v.tensor_copy(
            latT.ap().rearrange("p a b -> p (a b)")[:, grp * 256:(grp + 1) * 256],
            Ybf[:, 0, 0:256]))
    wait(TEN, v_ev, v_ev.n)
    opi(TEN, p_, lambda t: t.transpose(Ybf[:, 0, 0:16], bq_sb[:, :], ident[0:16, 0:16]))
    wait(VEC, p_, p_.n)
    latT_v = opi(VEC, v_ev, lambda v: v.tensor_copy(bqT[:, :], Ybf[:, 0, 0:16]))

    mm_pe = [0] * H
    mm_ev = [0] * H
    mm_store = [0] * H
    mlo_store = 0
    for h in range(H):
        slotp = rstg.k % 2
        if slot_store[slotp] is not None:
            wait(GPS, slot_store[slotp][0], slot_store[slotp][1])
            slot_store[slotp] = None
        if h >= 2:
            wait(GPS, p_, mm_pe[h - 2])
        slot, lsem, wq_ld = rstg.load(GPS, stg[:, slotp, :], wq[h * 128:(h + 1) * 128, :])
        wait(TEN, lsem, wq_ld)
        if h == 0:
            wait(TEN, v_ev, latT_v)
        if h >= 2:
            wait(TEN, v_ev, mm_ev[h - 2])
        for dt in range(DT):
            op(TEN, lambda t, h=h, dt=dt, slot=slot: t.matmul(
                X2[:, h % 2, dt * 64:(dt + 1) * 64], stg[:, slot, dt * 128:(dt + 1) * 128],
                latT[:, h, :], start=True, stop=True))
        bank = 1 if h < 8 else 2
        col = (h % 8) * 64
        mm_pe[h] = opi(TEN, p_, lambda t, h=h, bank=bank, col=col: t.matmul(
            Yb[0:1, bank, col:col + 64], bqT[:, h:h + 1], latT[:, h, :], start=True, stop=True))
        wait(VEC, p_, mm_pe[h])
        if h >= 2:
            wait(VEC, msT[h % 2], mm_store[h - 2])
        if h >= 1:
            wait(VEC, mloT, mlo_store)
        op(VEC, lambda v, h=h: v.tensor_scalar_mul(pf32[:, :], X2[:, h % 2, :], ISQ))
        vd(VEC)
        op(VEC, lambda v, h=h: v.tensor_copy(mstg[:, h % 2, :], pf32[:, :]))
        vd(VEC)
        mm_ev[h] = opi(VEC, v_ev, lambda v, h=h: v.tensor_sub(
            mlo_st[:, :], pf32[:, :], mstg[:, h % 2, :]))
        wait(GPS, v_ev, mm_ev[h])
        mm_store[h] = dma(GPS, mm_mat.rearrange("(a p) q -> p a q", p=128)[:, :, h * 64:(h + 1) * 64],
                          mstg[:, h % 2, :].rearrange("p (a b) -> p a b", b=64), msT[h % 2])
        mlo_store = dma(GPS, mm_lo.rearrange("(a p) q -> p a q", p=128)[:, :, h * 64:(h + 1) * 64],
                        mlo_st[:, :].rearrange("p (a b) -> p a b", b=64), mloT)
    mmst_done = (msT[0].n, msT[1].n, mloT.n)
    wait(VEC, p_, p_.n)
    op(VEC, lambda v: v.tensor_scalar_mul(blog[0:1, 0:512], Yb[0:1, 1, :], ISQ))
    blog_v = opi(VEC, v_ev, lambda v: v.tensor_scalar_mul(blog[0:1, 512:1024], Yb[0:1, 2, :], ISQ))

    g_pe = [0] * H
    g_ev = [0] * H
    g_store = [0] * H
    for h in range(H):
        if h == 0:
            wait(SYN, stgT["wo"][0], wobf_done[0])
            wait(SYN, stgT["wo"][1], wobf_done[1])
            wait(TEN, v_ev, blog_v)
        par = h % 2
        gps_flat = Xflat if par == 0 else Yflat
        if h >= 2:
            wait(TEN, v_ev, g_ev[h - 2])
        for half in range(2):
            ridx = rbig.k
            if ridx >= 3:
                wait(SYN, p_, big_free[ridx - 3])
            _, wsem, wch = rbig.load(SYN, big[:, ridx % 3, :],
                                     wo_bf[half * 1024:(half + 1) * 1024,
                                           h * 128:(h + 1) * 128], transpose=True)
            wait(TEN, wsem, wch)
            for n in range(2):
                fn = (lambda t, h=h, n=n, half=half, gf=gps_flat, ridx=ridx: t.matmul(
                    gf[0:64, (half * 2 + n) * 512:(half * 2 + n + 1) * 512],
                    latT[:, h, :], big[:, ridx % 3, n * 512:(n + 1) * 512],
                    start=True, stop=True))
                if n == 1:
                    big_free.append(opi(TEN, p_, fn))
                else:
                    op(TEN, fn)
        g_pe[h] = big_free[-1]
        wait(VEC, p_, g_pe[h])
        if h >= 2:
            wait(VEC, gsT[h % 2], g_store[h - 2])
        g_ev[h] = opi(VEC, v_ev, lambda v, par=par, gf=gps_flat: v.tensor_copy(
            gstg[:, par, :], gf[0:64, :]))
        wait(SYN, v_ev, g_ev[h])
        g_store[h] = dma(SYN, g_mat[h * 64:(h + 1) * 64, :], gstg[:, par, :], gsT[h % 2])
    gst_done = (gsT[0].n, gsT[1].n)

    bg = []
    bg_pos = [0]

    def drain_bg(k):
        e = min(bg_pos[0] + k, len(bg))
        for i in range(bg_pos[0], e):
            em(VEC, bg[i])
        bg_pos[0] = e

    def tap(name, src_ap, wsem, wval):
        if not debug:
            return None
        ts_ = sem("tps_" + name)
        wait(SYN, wsem, wval)
        dma(SYN, taps[name], src_ap, ts_)
        return ts_

    def tap_gate(eng, tok):
        if debug and tok is not None:
            wait(eng, tok, 16)

    def mat_tile_ap(m, j):
        if m == 0:
            return wup[(j // 2) * 128:(j // 2 + 1) * 128, (j % 2) * 1024:(j % 2 + 1) * 1024]
        if m == 1:
            return wgt[(j // 2) * 128:(j // 2 + 1) * 128, (j % 2) * 1024:(j % 2 + 1) * 1024]
        cc, rr = j // 16, j % 16
        return wdn[rr * 128:(rr + 1) * 128, cc * 1024:(cc + 1) * 1024]

    def mat_tile_ap_p1(m, j):
        if m < 2:
            return mat_tile_ap(m, j)
        rr, cc = j // 8, j % 8
        return wdn[rr * 128:(rr + 1) * 128, cc * 1024:(cc + 1) * 1024]

    v_p1_fold = [0, 0, 0]
    p1_done = []
    for m in range(3):
        ncc = 2 if m < 2 else 8
        for j in range(128):
            idx = m * 128 + j
            if idx >= 2:
                wait(GPS, v_p1, p1_done[idx - 2])
            slot, lsem, ldn = rp1.load(GPS, p1r[:, idx % 2, 1:1025], mat_tile_ap_p1(m, j))
            v_p1.n += 1
            p1_done.append(v_p1.n)
            cc, rb = j % ncc, j // ncc
            nxt = (idx + 1) % 2

            def p1job(v, ldn=ldn, slot=idx % 2, nxt=nxt, cc=cc, rb=rb, m=m, ncc=ncc,
                      h1=v_p1.h, lh=lsem.h):
                v.wait_ge(lh, ldn)
                lo = 1 if cc == 0 else 0
                dst = rowacc[m][:, rb:rb + 1] if cc == ncc - 1 else p1r[:, nxt, 0:1]
                r = v.reduce_sum(dst, p1r[:, slot, lo:1025],
                                 axis=AX.X, apply_absolute_value=True).then_inc(h1, 1)
                v.drain()
                return r
            bg.append(p1job)
        v_p1_fold[m] = v_p1.n

    def emit_th(m):
        nblk = 64 if m < 2 else 16
        racc = rowacc[m]
        scr = s_scr[m]
        wait(SYN, v_p1, v_p1_fold[m])
        for rb in range(nblk):
            dma(SYN, scr[rb * 128:(rb + 1) * 128], racc[:, rb:rb + 1], th_dm)
        wait(SYN, th_dm, th_dm.n)
        nch = nblk * 128 // 64
        red_done = []
        for c in range(nch):
            slot = c % 2
            if c >= 2:
                wait(SYN, v_p1, red_done[c - 2])
            ldv = dma(SYN, line[0:1, slot, 1:65],
                      scr[c * 64:(c + 1) * 64].rearrange("(q n) -> q n", q=1), th_ln[slot])
            wait(VEC, th_ln[slot], ldv)
            lo = 1 if c == 0 else 0
            dst = sc1[0:1, 0:1] if c == nch - 1 else line[0:1, (c + 1) % 2, 0:1]
            red_done.append(opi(VEC, v_p1, lambda v, dst=dst, slot=slot, lo=lo:
                                v.reduce_sum(dst, line[0:1, slot, lo:65], axis=AX.X)))
            vd(VEC)
        op(VEC, lambda v: v.tensor_scalar(sc1[0:1, 4:5], sc1[0:1, 0:1],
                                          float(1.0 / NW), float(Q_EPS),
                                          op0=ALU.mult, op1=ALU.add))
        vd(VEC)
        op(VEC, lambda v: v.tensor_scalar_mul(sc1[0:1, 5:6], sc1[0:1, 4:5], 0.5))
        op(VEC, lambda v: v.tensor_scalar_mul(sc1[0:1, 6:7], sc1[0:1, 4:5], -0.5))
        vd(VEC)
        op(VEC, lambda v: v.tensor_scalar_mul(sc1[0:1, 8:9], sc1[0:1, 5:6], float(2.0 ** -30)))
        vd(VEC)
        op(VEC, lambda v: v.tensor_add(sc1[0:1, 9:10], sc1[0:1, 5:6], sc1[0:1, 8:9]))
        vd(VEC)
        op(VEC, lambda v: v.tensor_sub(sc1[0:1, 10:11], sc1[0:1, 9:10], sc1[0:1, 5:6]))
        vd(VEC)
        op(VEC, lambda v: v.tensor_sub(sc1[0:1, 10:11], sc1[0:1, 10:11], sc1[0:1, 8:9]))
        vd(VEC)
        op(VEC, lambda v: v.tensor_scalar_mul(sc1[0:1, 10:11], sc1[0:1, 10:11], -128.0))
        vd(VEC)
        op(VEC, lambda v: v.tensor_sub(sc1[0:1, 11:12], sc1[0:1, 5:6], sc1[0:1, 10:11]))
        vrow1 = opi(VEC, v_, lambda v: v.tensor_add(sc1[0:1, 12:13], sc1[0:1, 5:6], sc1[0:1, 10:11]))
        vd(VEC)
        op(VEC, lambda v: v.tensor_scalar_mul(sc1[0:1, 13:14], sc1[0:1, 11:12], -1.0))
        vrow = opi(VEC, v_, lambda v: v.tensor_scalar_mul(sc1[0:1, 14:15], sc1[0:1, 12:13], -1.0))
        wait(TEN, v_, vrow)
        op(TEN, lambda t: t.matmul(Yb[:, 3, 0:1], ones1f[0:1, :], sc1[0:1, 11:12],
                                   start=True, stop=True))
        op(TEN, lambda t: t.matmul(Yb[:, 3, 1:2], ones1f[0:1, :], sc1[0:1, 12:13],
                                   start=True, stop=True))
        op(TEN, lambda t: t.matmul(Yb[:, 3, 2:3], ones1f[0:1, :], sc1[0:1, 13:14],
                                   start=True, stop=True))
        if m == 0:
            op(TEN, lambda t: t.matmul(Yb[:, 3, 3:4], ones1f[0:1, :], sc1[0:1, 14:15],
                                       start=True, stop=True))
            pe = opi(TEN, p_, lambda t: t.matmul(Yb[:, 3, 4:5], ones1f[0:1, :], sc1[0:1, 6:7],
                                                 start=True, stop=True))
        else:
            pe = opi(TEN, p_, lambda t: t.matmul(Yb[:, 3, 3:4], ones1f[0:1, :], sc1[0:1, 14:15],
                                                 start=True, stop=True))
        wait(VEC, p_, pe)
        op(VEC, lambda v, m=m: v.tensor_copy(thc[:, m:m + 1], Yb[:, 3, 0:1]))
        op(VEC, lambda v, m=m: v.tensor_copy(thc[:, 3 + m:4 + m], Yb[:, 3, 1:2]))
        op(VEC, lambda v, m=m: v.tensor_copy(nthc[:, m:m + 1], Yb[:, 3, 2:3]))
        ev = opi(VEC, v_ev, lambda v, m=m: v.tensor_copy(nthc[:, 3 + m:4 + m], Yb[:, 3, 3:4]))
        if m == 0:
            ev = opi(VEC, v_ev, lambda v: v.tensor_copy(thc[:, 6:7], Yb[:, 3, 4:5]))
        wait(TEN, v_ev, ev)
        return ev

    q2_done = []
    nat_st = {}
    pend_st = []

    def flush_nat_st():
        while pend_st:
            pg, pm, pj, po, pq = pend_st.pop(0)
            wait(GPS, v_q, pq)
            nat_st[pg] = dma(GPS, nat_tile_mj(pm, pj), qout[:, po, :], qoS[po])
    qt_st = {}

    def qt_dst_full(g):
        m, j = g // 128, g % 128
        if m < 2:
            dstm = up_qT if m == 0 else gt_qT
            slab = dstm[(j % 2) * 1024:(j % 2) * 1024 + 1024,
                        (j // 2) * 128:(j // 2 + 1) * 128]
        else:
            cc, rr = j // 16, j % 16
            slab = dn_qT[cc * 1024:cc * 1024 + 1024, rr * 128:(rr + 1) * 128]
        return slab.rearrange("(k c) r -> c k r", k=8)

    def nat_tile_mj(m, j):
        if m < 2:
            dstm = up_q if m == 0 else gt_q
            return dstm[(j // 2) * 128:(j // 2 + 1) * 128, (j % 2) * 1024:(j % 2 + 1) * 1024]
        cc, rr = j // 16, j % 16
        return dn_q[rr * 128:(rr + 1) * 128, cc * 1024:(cc + 1) * 1024]

    def emit_pass2(tiles, th_evs):
        for m, j in tiles:
            glob = len(q2_done)
            idx = 384 + glob
            slot = idx % 2
            th_ev = th_evs[m]
            if glob >= 2:
                wait(GPS, v_q, q2_done[glob - 2])
            else:
                wait(GPS, v_p1, p1_done[382 + glob])
            _, lsem, ldn = rp1.load(GPS, p1r[:, slot, 0:1024], mat_tile_ap(m, j))
            oslot = glob % 2
            guard = nat_st.get(glob - 2, 0)
            v_q.n += 1
            q2_done.append(v_q.n)

            def quant(v, m=m, slot=slot, oslot=oslot, ldn=ldn, th_ev=th_ev, guard=guard,
                      h1=v_q.h, lh=lsem.h):
                v.wait_ge(lh, ldn)
                v.wait_ge(v_ev.h, th_ev)
                if guard:
                    v.wait_ge(qoS[oslot].h, guard)
                v.drain()
                v.tensor_scalar(ctile[:, :], p1r[:, slot, 0:1024], thc[:, m:m + 1], None,
                                op0=ALU.is_gt)
                v.drain()
                v.scalar_tensor_tensor(ctile[:, :], p1r[:, slot, 0:1024],
                                       thc[:, 3 + m:4 + m], ctile[:, :],
                                       op0=ALU.is_gt, op1=ALU.add)
                v.drain()
                v.scalar_tensor_tensor(ctile[:, :], p1r[:, slot, 0:1024],
                                       nthc[:, m:m + 1], ctile[:, :],
                                       op0=ALU.is_lt, op1=ALU.subtract)
                v.drain()
                return v.scalar_tensor_tensor(qout[:, oslot, :], p1r[:, slot, 0:1024],
                                              nthc[:, 3 + m:4 + m], ctile[:, :],
                                              op0=ALU.is_lt, op1=ALU.add
                                              ).then_inc(h1, 1)
            bg.append(quant)
            pend_st.append((glob, m, j, oslot, v_q.n))
            if len(pend_st) > 1:
                pg, pm, pj, po, pq = pend_st.pop(0)
                wait(GPS, v_q, pq)
                nat_st[pg] = dma(GPS, nat_tile_mj(pm, pj), qout[:, po, :], qoS[po])

    tp_q = [0]

    def emit_tp_store_tile(gq):
        g, q = gq
        ts = tstg if q % 2 == 0 else tstg2
        eng = SYN if q % 2 == 0 else SCA
        wait(eng, c_qt, q + 1)
        qt_st[q] = dma(eng, qt_dst_full(g),
                       ts[:, :].rearrange("p (k r) -> p k r", k=8), qtS[q % 2])

    def emit_tpass_oct(m, oct):
        if tp_q[0] == 0:
            wait(TEN, v_ev, v_ev.n)
            wait(SCA, v_ev, v_ev.n)
        wait(TEN, c_si, c_si.n)
        prev = None
        for i in range(8):
            j = oct * 8 + i
            if m < 2:
                ge = oct * 16 + m * 8 + i
            else:
                ge = 256 + j
            q = tp_q[0]
            tp_q[0] += 1
            ridx = rbig.k
            if ridx >= 3:
                wait(SYN, p_, big_free[ridx - 3])
            wait(SYN, qoS[ge % 2], nat_st[ge])
            _, lsem, lv = rbig.load(SYN, big[:, ridx % 3, 0:1024], nat_tile_mj(m, j))
            if prev is not None:
                emit_tp_store_tile(prev)
            wait(TEN, lsem, lv)
            if q >= 2:
                wait(TEN, c_qt, q - 1)
            for k in range(8):
                fn = (lambda t, q=q, k=k, ridx=ridx: t.matmul(
                    X2[:, q % 2, k * 128:(k + 1) * 128],
                    big[:, ridx % 3, k * 128:(k + 1) * 128], ident[:, :],
                    start=True, stop=True))
                if k == 7:
                    big_free.append(opi(TEN, p_, fn))
                else:
                    op(TEN, fn)
            wait(SCA, p_, big_free[-1])
            if q >= 2:
                wait(SCA, qtS[q % 2], qt_st[q - 2])
            c_qt.n += 1
            ts = tstg if q % 2 == 0 else tstg2
            em(SCA, lambda s, q=q, ts=ts, h=c_qt.h: s.copy(ts[:, :], X2[:, q % 2, :]
                                                           ).then_inc(h, 1))
            prev = (m * 128 + j, q)
        emit_tp_store_tile(prev)
        wait(SYN, qtS[0], qtS[0].n)
        wait(SYN, qtS[1], qtS[1].n)
        qprogT.n += 1
        em(SYN, lambda gg, h=qprogT.h: gg.sem_inc(h, 1))

    q_need_up = [2 * eg + 1 for eg in range(16)]
    q_need_gt = [2 * eg + 2 for eg in range(16)]
    q_need_dn = [32 + 2 * (c + 1) for c in range(8)]

    def emit_ln(src_tile, dst_tile, W, resid=False):
        wait(TEN, v_ev, v_ev.n)
        sq_base = v_sq.n
        sum_pe = [0] * DT
        for dt in range(DT):
            slot = dt % 2
            if dt >= 2:
                wait(VEC, p_, sum_pe[dt - 2])
            v_sq.n += 1
            em(VEC, lambda v, dt=dt, slot=slot, s=src_tile, h=v_sq.h, W=W: v.tensor_mul(
                sqr[:, slot, 0:W], s(dt), s(dt)).then_inc(h, 1))
            op(TEN, lambda t, dt=dt, s=src_tile, W=W: t.matmul(
                Yb[0:1, 0, 0:W], onescb[:, 0:1], s(dt),
                start=(dt == 0), stop=(dt == DT - 1)))
            wait(TEN, v_sq, sq_base + dt + 1)
            sum_pe[dt] = opi(TEN, p_, lambda t, dt=dt, slot=slot, W=W: t.matmul(
                Yb[0:1, 1, 0:W], onescb[:, 0:1], sqr[:, slot, 0:W],
                start=(dt == 0), stop=(dt == DT - 1)))
        wait(VEC, p_, p_.n)
        op(VEC, lambda v, W=W: v.tensor_scalar_mul(rowf[0:1, 0, 0:W], Yb[0:1, 0, 0:W], float(1.0 / D)))
        op(VEC, lambda v, W=W: v.tensor_scalar_mul(rowf[0:1, 1, 0:W], Yb[0:1, 1, 0:W], float(1.0 / D)))
        vd(VEC)
        op(VEC, lambda v, W=W: v.tensor_mul(rowf[0:1, 2, 0:W], rowf[0:1, 0, 0:W], rowf[0:1, 0, 0:W]))
        vd(VEC)
        vv = opi(VEC, v_, lambda v, W=W: v.tensor_sub(rowf[0:1, 2, 0:W], rowf[0:1, 1, 0:W],
                                                      rowf[0:1, 2, 0:W]))
        wait(SCA, v_, vv)
        op(SCA, lambda s, W=W: s.activation(rowf[0:1, 3, 0:W], rowf[0:1, 2, 0:W], AF.Ln,
                                            bias=float(LN_EPS), scale=1.0))
        em(SCA, lambda s: s.drain())
        cc = opi(SCA, c_, lambda s, W=W: s.activation(rowf[0:1, 2, 0:W], rowf[0:1, 3, 0:W], AF.Exp,
                                                      bias=0.0, scale=-0.5))
        wait(VEC, c_, cc)
        vd(VEC)
        op(VEC, lambda v, W=W: v.tensor_copy(rowb[0:1, 0, 0:W], rowf[0:1, 2, 0:W]))
        vv2 = opi(VEC, v_, lambda v, W=W: v.scalar_tensor_tensor(
            rowb[0:1, 1, 0:W], rowf[0:1, 0, 0:W], -1.0, rowf[0:1, 2, 0:W],
            op0=ALU.mult, op1=ALU.mult))
        wait(TEN, v_, vv2)
        op(TEN, lambda t, W=W: t.matmul(Yb[:, 0, 0:W], ones1[0:1, :], rowb[0:1, 0, 0:W],
                                        start=True, stop=True))
        bc_pe = opi(TEN, p_, lambda t, W=W: t.matmul(Yb[:, 1, 0:W], ones1[0:1, :], rowb[0:1, 1, 0:W],
                                                     start=True, stop=True))
        wait(VEC, p_, bc_pe)
        op(VEC, lambda v, W=W: v.tensor_copy(rbc[:, 0:W], Yb[:, 0, 0:W]))
        bc_v = opi(VEC, v_ev, lambda v, W=W: v.tensor_copy(abc[:, 0:W], Yb[:, 1, 0:W]))
        vd(VEC)
        for dt in range(DT):
            if not resid:
                op(VEC, lambda v, dt=dt, s=src_tile, d=dst_tile, W=W: v.tensor_mul(
                    d(dt), s(dt), rbc[:, 0:W]))
            else:
                op(VEC, lambda v, dt=dt, s=src_tile, W=W: v.tensor_mul(
                    s(dt), s(dt), rbc[:, 0:W]))
        vd(VEC)
        for dt in range(DT):
            tgt = (lambda v, dt=dt, s=src_tile, d=dst_tile, W=W: v.tensor_add(
                (s(dt) if resid else d(dt)), (s(dt) if resid else d(dt)), abc[:, 0:W]))
            if dt == DT - 1 and not resid:
                opi(VEC, v_, tgt)
            else:
                op(VEC, tgt)
        if resid:
            vd(VEC)
            for dt in range(DT):
                tgt = (lambda v, dt=dt, s=src_tile, d=dst_tile, W=W: v.tensor_add(
                    d(dt), d(dt), s(dt)))
                if dt == DT - 1:
                    opi(VEC, v_, tgt)
                else:
                    op(VEC, tgt)
        return v_.n, bc_v

    ln3_done = [0] * NTB
    tp_h1 = tp_lg = tp_p = tp_at = tp_o = tp_x1 = tp_hid = tp_f = None

    def src_xT(tb):
        return lambda dt: xT[:, dt, tb * TB:(tb + 1) * TB]

    def attn_tb(tb):
        nonlocal tp_h1, tp_lg, tp_p, tp_at, tp_o, tp_x1
        if tb == 0:
            wait(TEN, s_ld, xT_ready)
            wait(VEC, s_ld, xT_ready)
            wait(TEN, v_ev, v_ev.n)
        h1_v, _ = emit_ln(src_xT(tb), lambda dt: h1T[:, dt, :], TB)
        if tb == 0:
            tp_h1 = tap("tap_h1", h1T.ap(), v_, h1_v)
        lg_ev = [0, 0]
        for pair in range(2):
            wait(TEN, v_, h1_v)
            wait(TEN, v_ev, v_ev.n)
            for dti in range(2 * DT):
                dt, part = dti // 2, dti % 2
                srcm = mm_mat if part == 0 else mm_lo
                k = rmm.k
                if k == 0:
                    wait(SYN, msT[0], mmst_done[0])
                    wait(SYN, msT[1], mmst_done[1])
                    wait(SYN, mloT, mmst_done[2])
                if k >= 2:
                    wait(SYN, p_, mm_free[k - 2])
                _, msem, mch = rmm.load(SYN, mmr[:, dti % 2, :],
                                        srcm[dt * 128:(dt + 1) * 128, :])
                wait(TEN, msem, mch)
                for tp in range(2):
                    t128 = pair * 2 + tp
                    for nn in range(2):
                        fn = (lambda t, dti=dti, dt=dt, tp=tp, t128=t128, nn=nn: t.matmul(
                            X2[:, tp, nn * 512:(nn + 1) * 512],
                            h1T[:, dt, t128 * 128:(t128 + 1) * 128],
                            mmr[:, dti % 2, nn * 512:(nn + 1) * 512],
                            start=(dti == 0), stop=False))
                        if tp == 1 and nn == 1:
                            mm_free.append(opi(TEN, p_, fn))
                        else:
                            op(TEN, fn)
            for tp in range(2):
                for nn in range(2):
                    fn = (lambda t, tp=tp, nn=nn: t.matmul(
                        X2[:, tp, nn * 512:(nn + 1) * 512], ones1[0:1, :],
                        blog[0:1, nn * 512:(nn + 1) * 512], start=False, stop=True))
                    if tp == 1 and nn == 1:
                        pe_lg = opi(TEN, p_, fn)
                    else:
                        op(TEN, fn)
            wait(VEC, p_, pe_lg)
            if tb == 0 and pair == 1:
                tap_gate(VEC, tp_lg)
            for tp in range(2):
                fn = (lambda v, tp=tp: v.tensor_copy(logit[:, tp, :], X2[:, tp, :]))
                if tp == 1:
                    lg_ev[pair] = opi(VEC, v_ev, fn)
                else:
                    op(VEC, fn)
            if tb == 0 and pair == 0:
                tp_lg = tap("tap_lg", logit.ap(), v_ev, lg_ev[0])
            if tb == 0 and pair == 1:
                tap_gate(VEC, tp_lg)
                tap_gate(VEC, tp_p)
            vd(VEC)
            for tp in range(2):
                for h in range(H):
                    op(VEC, lambda v, tp=tp, h=h: v.max(out=mx8[:, h, :],
                                                        in_=logit[:, tp, h * L:(h + 1) * L]))
                vd(VEC)
                op(VEC, lambda v: v.tensor_copy(m1r[:, :], mx8[:, :, 0]))
                op(VEC, lambda v: v.tensor_copy(taur[:, :], mx8[:, :, 3]))
                l3 = logit[:, tp, :].rearrange("p (h l) -> p h l", l=L)
                m3 = m1r.ap().rearrange("p (h o) -> p h o", o=1).to_broadcast([128, H, L])
                t3 = taur.ap().rearrange("p (h o) -> p h o", o=1).to_broadcast([128, H, L])
                e3 = ework.ap().rearrange("p (h l) -> p h l", l=L)
                k3 = maskb.ap().rearrange("p (h l) -> p h l", l=L)
                vd(VEC)
                op(VEC, lambda v, l3=l3, t3=t3, k3=k3: v.tensor_tensor(k3, l3, t3, op=ALU.is_ge))
                vv = opi(VEC, v_, lambda v, l3=l3, m3=m3, e3=e3: v.tensor_tensor(
                    e3, l3, m3, op=ALU.subtract))
                wait(SCA, v_, vv)
                cc = opi(SCA, c_, lambda s: s.activation(ework[:, :], ework[:, :], AF.Exp,
                                                         bias=0.0, scale=1.0))
                wait(VEC, c_, cc)
                vd(VEC)
                op(VEC, lambda v: v.tensor_mul(ework[:, :], ework[:, :], maskb[:, :]))
                vd(VEC)
                op(VEC, lambda v, e3=e3: v.reduce_sum(ssum[:, :], e3, axis=AX.X))
                vd(VEC)
                op(VEC, lambda v: v.reciprocal(rsum[:, :], ssum[:, :]))
                vd(VEC)
                r3 = rsum.ap().rearrange("p (h o) -> p h o", o=1).to_broadcast([128, H, L])
                p3 = p_sb[:, tp, :].rearrange("p (h l) -> p h l", l=L)
                v_p.n += 1
                em(VEC, lambda v, e3=e3, r3=r3, p3=p3, h=v_p.h: v.tensor_tensor(
                    p3, e3, r3, op=ALU.mult).then_inc(h, 1))
            wait(TEN, v_ev, lg_ev[pair])
            for tp in range(2):
                t128 = pair * 2 + tp
                wait(TEN, v_p, 4 * tb + t128 + 1)
                for hl in range(8):
                    fn = (lambda t, tp=tp, hl=hl: t.transpose(
                        Xbf[:, tp, hl * 128:(hl + 1) * 128],
                        p_sb[:, tp, hl * 128:(hl + 1) * 128], ident[:, :]))
                    if hl == 7:
                        opi(TEN, p_, fn)
                    else:
                        op(TEN, fn)
                wait(VEC, p_, p_.n)
                opi(VEC, v_ev, lambda v, tp=tp, t128=t128: v.tensor_copy(
                    pT_sb.ap()[:, :, t128 * 128:(t128 + 1) * 128],
                    Xbf[:, tp, :].rearrange("p (a b) -> p a b", a=8)))
            if tb == 0 and pair == 0:
                tp_p = tap("tap_p", p_sb.ap(), v_p, 2)
        pT_done = v_ev.n
        drain_bg(100)
        for dh in range(2):
            if dh == 0:
                wait(TEN, v_ev, pT_done)
            for ci in range(16):
                ridx = rbig.k
                if ridx >= 3:
                    wait(SYN, p_, big_free[ridx - 3])
                if tb == 0 and dh == 0 and ci == 0:
                    wait(SYN, stgT["wv"][0], wvbf_done[0])
                    wait(SYN, stgT["wv"][1], wvbf_done[1])
                _, lsem, lv = rbig.load(SYN, big[:, ridx % 3, 0:1024],
                                        wv_bf[dh * 1024:(dh + 1) * 1024,
                                              ci * 128:(ci + 1) * 128], transpose=True)
                wait(TEN, lsem, lv)
                rhs = h1T[:, ci, :]
                for k in range(8):
                    ps = Xb[:, k, :] if k < 4 else Yb[:, k - 4, :]
                    fn = (lambda t, ps=ps, ridx=ridx, k=k, rhs=rhs, ci=ci: t.matmul(
                        ps, big[:, ridx % 3, k * 128:(k + 1) * 128], rhs,
                        start=(ci == 0), stop=False))
                    if k == 7:
                        big_free.append(opi(TEN, p_, fn))
                    else:
                        op(TEN, fn)
            for k in range(8):
                ps = Xb[:, k, :] if k < 4 else Yb[:, k - 4, :]
                fn = (lambda t, ps=ps, k=k, dh=dh: t.matmul(
                    ps, bv_sb[0:1, dh * 1024 + k * 128:dh * 1024 + (k + 1) * 128],
                    onesN[0:1, :], start=False, stop=True))
                if k == 7:
                    at_pe = opi(TEN, p_, fn)
                else:
                    op(TEN, fn)
            wait(VEC, p_, at_pe)
            if tb == 1 and dh == 0:
                tap_gate(VEC, tp_at)
            for k in range(8):
                ps = Xb[:, k, :] if k < 4 else Yb[:, k - 4, :]
                fn = (lambda v, ps=ps, k=k, dh=dh: v.tensor_copy(
                    attnT[:, dh * 8 + k, :], ps))
                if k == 7:
                    opi(VEC, v_ev, fn)
                else:
                    op(VEC, fn)
            wait(TEN, v_ev, v_ev.n)
        if tb == 0:
            tp_at = tap("tap_at", attnT.ap(), v_ev, v_ev.n)
        drain_bg(40)
        for dh in range(2):
            for ci in range(24):
                ridx = rbig.k
                if ridx >= 3:
                    wait(SYN, p_, big_free[ridx - 3])
                if ci < 16:
                    _, lsem, lv = rbig.load(SYN, big[:, ridx % 3, 0:1024],
                                            wo_bf[dh * 1024:(dh + 1) * 1024,
                                                  ci * 128:(ci + 1) * 128], transpose=True)
                else:
                    if tb == 0 and dh == 0 and ci == 16:
                        wait(SYN, gsT[0], gst_done[0])
                        wait(SYN, gsT[1], gst_done[1])
                    _, lsem, lv = rbig.load(SYN, big[:, ridx % 3, 0:1024],
                                            g_mat[(ci - 16) * 128:(ci - 15) * 128,
                                                  dh * 1024:(dh + 1) * 1024])
                wait(TEN, lsem, lv)
                rhs = attnT[:, ci, :] if ci < 16 else pT_sb[:, ci - 16, :]
                for k in range(8):
                    ps = Xb[:, k, :] if k < 4 else Yb[:, k - 4, :]
                    fn = (lambda t, ps=ps, ridx=ridx, k=k, ci=ci, rhs=rhs: t.matmul(
                        ps, big[:, ridx % 3, k * 128:(k + 1) * 128],
                        rhs, start=(ci == 0), stop=False))
                    if k == 7:
                        big_free.append(opi(TEN, p_, fn))
                    else:
                        op(TEN, fn)
            for k in range(8):
                ps = Xb[:, k, :] if k < 4 else Yb[:, k - 4, :]
                fn = (lambda t, ps=ps, k=k, dh=dh: t.matmul(
                    ps, bo_sb[0:1, dh * 1024 + k * 128:dh * 1024 + (k + 1) * 128],
                    onesN[0:1, :], start=False, stop=True))
                if k == 7:
                    o_pe = opi(TEN, p_, fn)
                else:
                    op(TEN, fn)
            wait(VEC, p_, o_pe)
            if tb == 0 and dh == 0:
                tap_gate(VEC, tp_h1)
            for k in range(8):
                ps = Xb[:, k, :] if k < 4 else Yb[:, k - 4, :]
                fn = (lambda v, ps=ps, k=k, dh=dh: v.tensor_copy(
                    h1T[:, dh * 8 + k, :], ps))
                if k == 7:
                    opi(VEC, v_ev, fn)
                else:
                    op(VEC, fn)
            wait(TEN, v_ev, v_ev.n)
        if tb == 0:
            tp_o = tap("tap_o", h1T.ap(), v_ev, v_ev.n)
            tap_gate(VEC, tp_o)
            tap_gate(VEC, tp_h1)
        drain_bg(40)
        v_ln3, _ = emit_ln(lambda dt: h1T[:, dt, :], src_xT(tb), TB, resid=True)
        ln3_done[tb] = v_ln3
        if tb == 0:
            tp_x1 = tap("tap_x1", xT.ap()[:, :, 0:TB], v_, v_ln3)
        drain_bg(30)

    last_ln4 = [0]

    def ffn_tb(tb):
        nonlocal tp_hid, tp_f
        wait(TEN, v_, ln3_done[tb])
        h2_v, _ = emit_ln(src_xT(tb), lambda dt: h1T[:, dt, :], TB)
        for eg in range(16):
            if eg == 0:
                wait(TEN, v_, h2_v)
            if tb == 0:
                emit_tpass_oct(0, eg)
                emit_tpass_oct(1, eg)
                wait(TEN, c_qt, 16 * (eg + 1))
            for mat in range(2):
                srcq = up_qT if mat == 0 else gt_qT
                need = q_need_up[eg] if mat == 0 else q_need_gt[eg]
                if mat == 0 and eg > 0:
                    wait(TEN, c_si, c_si.n)
                if mat == 1 and eg > 0:
                    wait(TEN, v_hid, v_hid.n)
                for d_in in range(DT):
                    ridx = rsml.k
                    if ridx == 0:
                        wait(SYN, v_, ln3_done[3])
                    if ridx >= 4:
                        wait(SYN, p_, sml_free[ridx - 4])
                    if d_in == 0 and tb == 0:
                        wait(SYN, qprogT, need)
                    _, lsem, lv = rsml.load(SYN, sml[:, ridx % 4, :],
                                            srcq[d_in * 128:(d_in + 1) * 128,
                                                 eg * 512:(eg + 1) * 512])
                    wait(TEN, lsem, lv)
                    for et in range(4):
                        ps = Xb[:, et, :] if mat == 0 else Yb[:, et, :]
                        fn = (lambda t, ps=ps, ridx=ridx, et=et, d_in=d_in: t.matmul(
                            ps, sml[:, ridx % 4, et * 128:(et + 1) * 128],
                            h1T[:, d_in, :], start=(d_in == 0), stop=(d_in == DT - 1)))
                        if et == 3:
                            sml_free.append(opi(TEN, p_, fn))
                        else:
                            op(TEN, fn)
                if mat == 0:
                    up_pe = sml_free[-1]
                else:
                    gt_pe = sml_free[-1]
            wait(SCA, p_, up_pe)
            if eg > 0:
                wait(SCA, v_hid, v_hid.n)
            if tb == 0 and eg == 0:
                wait(SCA, v_ev, th_ready0[0])
            for et in range(4):
                fn = (lambda s, et=et: s.activation(silu_sb[:, et, :], Xb[:, et, :], AF.Silu,
                                                    bias=0.0, scale=thc[:, 6:7]))
                if et == 3:
                    c_si.n += 1
                    em(SCA, lambda s, f=fn, h=c_si.h: f(s).then_inc(h, 1))
                else:
                    op(SCA, fn)
            wait(VEC, p_, gt_pe)
            wait(VEC, c_si, c_si.n)
            if tb == 1 and eg == 0:
                tap_gate(VEC, tp_hid)
            for et in range(4):
                fn = (lambda v, et=et, eg=eg: v.tensor_mul(
                    hid[:, eg * 4 + et, :], silu_sb[:, et, :], Yb[:, et, :]))
                if et == 3:
                    v_hid.n += 1
                    em(VEC, lambda v, f=fn, h=v_hid.h: f(v).then_inc(h, 1))
                else:
                    op(VEC, fn)
            drain_bg(24)
        if tb == 0:
            tp_hid = tap("tap_hid", hid.ap(), v_hid, v_hid.n)
            drain_bg(len(bg))
        if tb == 0:
            for oc in range(16):
                emit_tpass_oct(2, oc)
        wait(TEN, v_hid, v_hid.n)
        wait(TEN, c_si, c_si.n)
        if tb == 0:
            wait(TEN, c_qt, 384)
        for dh in range(2):
            for et in range(ET):
                ridx = rbig.k
                if ridx >= 3:
                    wait(SYN, p_, big_free[ridx - 3])
                if tb == 0 and dh == 0 and et % 8 == 0:
                    wait(SYN, qprogT, q_need_dn[et // 8])
                _, lsem, lv = rbig.load(SYN, big[:, ridx % 3, 0:1024],
                                        dn_qT[et * 128:(et + 1) * 128,
                                              dh * 1024:(dh + 1) * 1024])
                wait(TEN, lsem, lv)
                for k in range(8):
                    ps = Xb[:, k, :] if k < 4 else Yb[:, k - 4, :]
                    fn = (lambda t, ps=ps, ridx=ridx, k=k, et=et: t.matmul(
                        ps, big[:, ridx % 3, k * 128:(k + 1) * 128],
                        hid[:, et, :], start=(et == 0), stop=(et == ET - 1)))
                    if k == 7:
                        big_free.append(opi(TEN, p_, fn))
                    else:
                        op(TEN, fn)
            d_pe = big_free[-1]
            wait(VEC, p_, d_pe)
            for k in range(8):
                ps = Xb[:, k, :] if k < 4 else Yb[:, k - 4, :]
                fn = (lambda v, ps=ps, k=k, dh=dh: v.tensor_copy(h1T[:, dh * 8 + k, :], ps))
                if k == 7:
                    opi(VEC, v_ev, fn)
                else:
                    op(VEC, fn)
            wait(TEN, v_ev, v_ev.n)
        if tb == 0:
            tp_f = tap("tap_f", h1T.ap(), v_ev, v_ev.n)
            tap_gate(VEC, tp_f)
            tap_gate(VEC, tp_x1)
        lv4, _ = emit_ln(lambda dt: h1T[:, dt, :], src_xT(tb), TB, resid=True)
        last_ln4[0] = lv4
        drain_bg(20)

    th_ready0 = [0]
    attn_tb(0)
    attn_tb(1)
    th0 = emit_th(0)
    th_ready0[0] = th0
    attn_tb(2)
    th1 = emit_th(1)
    tiles01 = []
    for oct in range(16):
        tiles01 += [(0, oct * 8 + i) for i in range(8)]
        tiles01 += [(1, oct * 8 + i) for i in range(8)]
    emit_pass2(tiles01, {0: th0, 1: th1})
    flush_nat_st()
    attn_tb(3)
    th2 = emit_th(2)
    emit_pass2([(2, j) for j in range(128)], {2: th2})
    flush_nat_st()
    ffn_tb(0)
    ffn_tb(1)
    ffn_tb(2)
    ffn_tb(3)

    wait(TEN, v_, last_ln4[0])
    for t128 in range(16):
        wait(TEN, v_ev, v_ev.n)
        for dt in range(DT):
            bank = dt // 8
            fn = (lambda t, dt=dt, t128=t128, bank=bank: t.transpose(
                Xbf[:, bank, (dt % 8) * 128:(dt % 8 + 1) * 128],
                xT[:, dt, t128 * 128:(t128 + 1) * 128], ident[:, :]))
            if dt == DT - 1:
                opi(TEN, p_, fn)
            else:
                op(TEN, fn)
        wait(VEC, p_, p_.n)
        if t128 >= 1:
            wait(VEC, o_st, 16 * t128)
        ev = opi(VEC, v_ev, lambda v: v.tensor_copy(
            ostg[:, :], Xbf[:, 0:2, :].rearrange("p a b -> p (a b)")))
        wait(GPS, v_ev, ev)
        dma(GPS, out[t128 * 128:(t128 + 1) * 128, :], ostg[:, :], o_st)
    wait(GPS, o_st, 16 * 16)
    wait(SYN, o_st, 16 * 16)

    with nc.Block() as block:
        @block.tensor
        def _(t):
            for fn in sched[TEN]:
                fn(t)

        @block.vector
        def _(v):
            for fn in sched[VEC]:
                fn(v)

        @block.scalar
        def _(s):
            for fn in sched[SCA]:
                fn(s)

        @block.gpsimd
        def _(g):
            for fn in sched[GPS]:
                fn(g)

        @block.sync
        def _(s):
            for fn in sched[SYN]:
                fn(s)

    return nc


_NC_CACHE = {}


def kernel(L0_attn_hidden_states, L0_attn_Wq, L0_attn_bq, L0_attn_Wk, L0_attn_bk,
           L0_attn_Wv, L0_attn_bv, L0_attn_Wo, L0_attn_bo, L0_attn_latent_keys,
           L0_ffn_W_up, L0_ffn_W_gate, L0_ffn_W_down):
    x = np.ascontiguousarray(np.asarray(L0_attn_hidden_states, dtype=np.float32)).reshape(16384, 2048)
    w = {
        "wq": np.ascontiguousarray(np.asarray(L0_attn_Wq, dtype=np.float32)),
        "bq": np.ascontiguousarray(np.asarray(L0_attn_bq, dtype=np.float32)),
        "wv": np.ascontiguousarray(np.asarray(L0_attn_Wv, dtype=np.float32)),
        "bv": np.ascontiguousarray(np.asarray(L0_attn_bv, dtype=np.float32)),
        "wo": np.ascontiguousarray(np.asarray(L0_attn_Wo, dtype=np.float32)),
        "bo": np.ascontiguousarray(np.asarray(L0_attn_bo, dtype=np.float32)),
        "lat": np.ascontiguousarray(np.asarray(L0_attn_latent_keys, dtype=np.float32)),
        "wup": np.ascontiguousarray(np.asarray(L0_ffn_W_up, dtype=np.float32)),
        "wgt": np.ascontiguousarray(np.asarray(L0_ffn_W_gate, dtype=np.float32)),
        "wdn": np.ascontiguousarray(np.asarray(L0_ffn_W_down, dtype=np.float32)),
    }
    if "nc" not in _NC_CACHE:
        _NC_CACHE["nc"] = build_program()
    nc = _NC_CACHE["nc"]
    in_maps = [dict(w, x=x[c * TOK:(c + 1) * TOK]) for c in range(8)]
    res = run_bass_kernel_spmd(nc, in_maps, core_ids=list(range(8)))
    outs = [np.asarray(r["out"]) for r in res.results]
    return np.concatenate(outs, axis=0).reshape(2, 8192, 2048).astype(np.float32)
